# version 01
# baseline (speedup 1.0000x reference)
import math
from contextlib import ExitStack

import numpy as np
import ml_dtypes

import concourse.bass as bass
import concourse.mybir as mybir
from concourse.bass_utils import run_bass_kernel_spmd

F32 = mybir.dt.float32
BF16 = mybir.dt.bfloat16
AF = mybir.ActivationFunctionType
ALU = mybir.AluOpType
AX = mybir.AxisListType

NCORES = 8
D = 1024
T = 4096
KT = D // 128
TC = 512

ENGINES = ("sync", "scalar", "vector", "tensor", "gpsimd")
DMA_QUEUES = {"sync": "dS", "scalar": "dA", "gpsimd": "dG"}


class Prog:

    def __init__(self, name="prog"):
        self.nc = bass.Bass("TRN2", target_bir_lowering=False)
        self.stack = ExitStack()
        self.stack.__enter__()
        self.scopes = [self.stack]
        self.gate = ()
        self.ops = {e: [] for e in ENGINES}
        self._sems = {}
        self.tot = {}
        self._names = set()
        for s in ("pe", "v", "s", "g", "sp", "dS", "dA", "dG"):
            self._get_sem(s)

    from contextlib import contextmanager as _cm

    @_cm
    def scope(self):
        es = ExitStack()
        es.__enter__()
        self.scopes.append(es)
        try:
            yield
        finally:
            self.scopes.pop()
            es.__exit__(None, None, None)

    def barrier(self):
        self.gate = tuple((s, t) for s, t in self.tot.items() if t > 0)

    def _get_sem(self, name):
        if name not in self._sems:
            self._sems[name] = self.stack.enter_context(self.nc.semaphore(name))
            self.tot[name] = 0
        return self._sems[name]

    def _uniq(self, name):
        i, n = 0, name
        while n in self._names:
            i += 1
            n = f"{name}_{i}"
        self._names.add(n)
        return n

    def sbuf(self, name, shape, dtype, root=False):
        scope = self.scopes[0] if root else self.scopes[-1]
        return scope.enter_context(
            self.nc.sbuf_tensor(self._uniq(name), list(shape), dtype))

    def psum(self, name, shape, dtype=F32):
        return self.scopes[-1].enter_context(
            self.nc.psum_tensor(self._uniq(name), list(shape), dtype))

    def dram(self, name, shape, dtype, kind=None):
        if kind is None:
            t = self.nc.dram_tensor(self._uniq(name), list(shape), dtype)
        else:
            t = self.nc.dram_tensor(name, list(shape), dtype, kind=kind)
        return t.ap()

    ENG_SEM = {"tensor": "pe", "vector": "v", "scalar": "s", "gpsimd": "g", "sync": "sp"}

    def op(self, eng, fn, waits=(), inc=False):
        semname = self.ENG_SEM[eng]
        tok = None
        if inc:
            self.tot[semname] += 1
            tok = (semname, self.tot[semname])
        self.ops[eng].append((fn, tuple(w for w in waits if w) + self.gate,
                              (semname, 1) if inc else None))
        return tok

    def dma(self, eng, out, in_, waits=(), sem=None, **kw):
        q = sem if sem is not None else DMA_QUEUES[eng]
        q = {"sync": "", "scalar": "A", "gpsimd": "G"}[eng] + q
        self._get_sem(q)
        waits = tuple(w for w in waits if w)
        if not hasattr(self, "_burst"):
            self._burst = {}
            self._last_q = {}
        if self._last_q.get(eng) != q:
            prev_end = self._burst.get(q, 0)
            if prev_end > 0:
                waits = waits + ((q, prev_end),)
        self._last_q[eng] = q
        self.tot[q] += 16
        self._burst[q] = self.tot[q]
        tok = (q, self.tot[q])

        def fn(e, out=out, in_=in_, kw=kw):
            return e.dma_start(out=out, in_=in_, **kw)

        self.ops[eng].append((fn, waits + self.gate, (q, 16)))
        return tok

    N_DSEM_POOL = 16

    def dsems(self, n):
        if not hasattr(self, "_dcur"):
            self._dcur = 0
        out = []
        for _ in range(n):
            name = f"dp{self._dcur % self.N_DSEM_POOL}"
            self._dcur += 1
            self._get_sem(name)
            out.append(name)
        return out

    def finalize(self):
        blk = self.stack.enter_context(self.nc.Block())
        for eng in ENGINES:
            if not self.ops[eng]:
                continue
            oplist = self.ops[eng]

            def body(e, oplist=oplist, eng=eng):
                satisfied = {}
                for fn, waits, inc in oplist:
                    merged = {}
                    for (sname, val) in waits:
                        merged[sname] = max(merged.get(sname, 0), val)
                    for sname, val in merged.items():
                        if satisfied.get(sname, 0) >= val:
                            continue
                        e.wait_ge(self._sems[sname], val)
                        satisfied[sname] = val
                    inst = fn(e)
                    if inc is not None:
                        sname, amt = inc
                        assert inst is not None, f"inc but no inst on {eng}"
                        inst.then_inc(self._sems[sname], amt)

            getattr(blk, eng)(body)
        return self.nc

    def run(self, in_maps, core_ids=None, **kw):
        nc = self.finalize()
        res = run_bass_kernel_spmd(
            nc, in_maps,
            core_ids=list(range(len(in_maps))) if core_ids is None else core_ids, **kw)
        self.stack.__exit__(None, None, None)
        return res


def cdiv(a, b):
    return (a + b - 1) // b


def make_ctx(P):
    ctx = {}
    ones = P.sbuf("c_ones128", [128, 1], F32)
    P.op("vector", lambda v: v.memset(ones[:, :], 1.0), inc=True)
    bc1 = P.sbuf("c_bcast1", [1, 128], F32)
    tok = P.op("vector", lambda v: v.memset(bc1[:, :], 1.0), inc=True)
    ctx["ones128"] = ones
    ctx["bcast1"] = bc1
    ctx["const_tok"] = tok
    ctx["cf"] = {}
    ctx["P"] = P
    for v in (1e-5, 1e-6, 1.0):
        constf(ctx, v)
    return ctx


def constf(ctx, val):
    val = float(val)
    if val not in ctx["cf"]:
        assert len(ctx["P"].scopes) == 1, f"constf({val}) inside scope"
        P = ctx["P"]
        t = P.sbuf(f"cf_{len(ctx['cf'])}", [128, 1], F32, root=True)
        tok = P.op("vector", lambda v: v.memset(t[:, :], val), inc=True)
        ctx["cf"][val] = (t, tok)
    return ctx["cf"][val]



def load_weight_bf16(P, w_ap, K, N, name, waits=()):
    kt = K // 128
    assert K % 128 == 0
    wsb = P.sbuf(f"w_{name}", [128, kt * N], BF16)
    if w_ap.dtype == BF16:
        sem = P.dsems(1)[0]
        tok = None
        for j in range(kt):
            tok = P.dma("sync", wsb[:, j * N:(j + 1) * N],
                        w_ap[j * 128:(j + 1) * 128, 0:N], waits=waits, sem=sem)
        return wsb, tok
    CH = min(4096, N)
    stage = [P.sbuf(f"wst_{name}", [128, CH], F32) for _ in range(2)]
    ssem = P.dsems(2)
    free = [tuple(waits), tuple(waits)]
    tok = None
    si = 0
    for j in range(kt):
        for c0 in range(0, N, CH):
            cw = min(CH, N - c0)
            buf = stage[si % 2]
            ld = P.dma("sync", buf[:, 0:cw], w_ap[j * 128:(j + 1) * 128, c0:c0 + cw],
                       waits=free[si % 2], sem=ssem[si % 2])
            tok = P.op("vector",
                       lambda v, buf=buf, cw=cw, j=j, c0=c0:
                       v.tensor_copy(wsb[:, j * N + c0:j * N + c0 + cw], buf[:, 0:cw]),
                       waits=[ld], inc=True)
            free[si % 2] = (tok,)
            si += 1
    return wsb, tok


def load_vec_cols(P, v_ap, n, name, waits=(), dtype=F32):
    assert n % 128 == 0
    nt = n // 128
    sb = P.sbuf(f"vc_{name}", [128, nt], dtype)
    src = v_ap.rearrange("(a p) -> p a", p=128)
    tok = P.dma("sync", sb[:, 0:nt], src, waits=waits,
                sem=P.dsems(1)[0], allow_slow_non_contiguous=True)
    return sb, tok



def emit_ln(P, ctx, x_ap, out_ap, Ttot, out_dtype=BF16, gamma=None, beta=None,
            rms=False, eps=1e-5, wait_x=(), name="ln", kt=KT):
    D_ = kt * 128
    ones = ctx["ones128"]
    bc1 = ctx["bcast1"]
    ctok = ctx["const_tok"]
    xt = [P.sbuf(f"{name}_x", [128, kt * TC], F32) for _ in range(2)]
    sq = [P.sbuf(f"{name}_sq", [128, kt * TC], F32) for _ in range(2)]
    yt = [P.sbuf(f"{name}_y", [128, kt * TC], out_dtype) for _ in range(2)]
    st_m = P.psum(f"{name}_pm", [1, TC], F32)
    st_q = P.psum(f"{name}_pq", [1, TC], F32)
    mrow = P.sbuf(f"{name}_mr", [1, 2 * TC], F32)
    tmp_row = P.sbuf(f"{name}_tr", [1, TC], F32)
    mb = P.psum(f"{name}_mb", [128, TC], F32)
    rb = P.psum(f"{name}_rb", [128, TC], F32)
    free_x = [(), ()]
    free_sq = [(), ()]
    free_y = [(), ()]
    sx = P.dsems(2)
    sy = P.dsems(2)
    stat_free = ()
    mb_free = ()
    last = None
    for c in range(cdiv(Ttot, TC)):
        t0, tw = c * TC, min(TC, Ttot - c * TC)
        pp = c % 2
        xb, sqb, yb = xt[pp], sq[pp], yt[pp]
        lds = [P.dma("sync", xb[:, j * TC:j * TC + tw],
                     x_ap[j * 128:(j + 1) * 128, t0:t0 + tw],
                     waits=free_x[pp] + tuple(wait_x), sem=sx[pp]) for j in range(kt)]
        sqtok = P.op("scalar", lambda s, sqb=sqb, xb=xb, tw=tw: s.activation(
            sqb[:, 0:kt * TC], xb[:, 0:kt * TC], AF.Square),
            waits=lds + list(free_sq[pp]), inc=True)
        pe_m = pe_q = None
        for j in range(kt):
            pe_m = P.op("tensor", lambda t, j=j, xb=xb, tw=tw: t.matmul(
                st_m[0:1, 0:tw], ones[:, 0:1], xb[:, j * TC:j * TC + tw],
                start=(j == 0), stop=(j == kt - 1)),
                waits=list(lds) + list(stat_free) + [ctok], inc=(j == kt - 1))
        for j in range(kt):
            pe_q = P.op("tensor", lambda t, j=j, sqb=sqb, tw=tw: t.matmul(
                st_q[0:1, 0:tw], ones[:, 0:1], sqb[:, j * TC:j * TC + tw],
                start=(j == 0), stop=(j == kt - 1)),
                waits=[sqtok], inc=(j == kt - 1))
        v1 = P.op("vector", lambda v, tw=tw: v.tensor_scalar_mul(
            mrow[:, 0:tw], st_m[:, 0:tw], 1.0 / D_), waits=[pe_m], inc=True)
        v2 = P.op("vector", lambda v, tw=tw: v.tensor_scalar_mul(
            mrow[:, TC:TC + tw], st_q[:, 0:tw], 1.0 / D_), waits=[pe_q], inc=True)
        stat_free = (v1, v2)
        if not rms:
            v3 = P.op("vector", lambda v, tw=tw: v.tensor_tensor(
                tmp_row[:, 0:tw], mrow[:, 0:tw], mrow[:, 0:tw], ALU.mult),
                waits=[v1], inc=True)
            var_tok = P.op("vector", lambda v, tw=tw: v.tensor_sub(
                mrow[:, TC:TC + tw], mrow[:, TC:TC + tw], tmp_row[:, 0:tw]),
                waits=[v2, v3], inc=True)
        else:
            var_tok = v2
        eps_t, eps_tok = constf(ctx, eps)
        s1 = P.op("scalar", lambda s, tw=tw: s.activation(
            mrow[:, TC:TC + tw], mrow[:, TC:TC + tw], AF.Ln, bias=eps_t[0:1, 0:1]),
            waits=[var_tok, eps_tok], inc=True)
        s2 = P.op("scalar", lambda s, tw=tw: s.activation(
            mrow[:, TC:TC + tw], mrow[:, TC:TC + tw], AF.Exp, scale=-0.5),
            waits=[s1], inc=True)
        b_m = P.op("tensor", lambda t, tw=tw: t.matmul(
            mb[:, 0:tw], bc1[0:1, :], mrow[:, 0:tw], start=True, stop=True),
            waits=[v1] + list(mb_free), inc=True) if not rms else None
        b_r = P.op("tensor", lambda t, tw=tw: t.matmul(
            rb[:, 0:tw], bc1[0:1, :], mrow[:, TC:TC + tw], start=True, stop=True),
            waits=[s2] + list(mb_free), inc=True)
        ytoks = []
        for j in range(kt):
            if not rms:
                w1 = P.op("vector", lambda v, j=j, xb=xb, sqb=sqb, tw=tw: v.tensor_sub(
                    sqb[:, j * TC:j * TC + tw], xb[:, j * TC:j * TC + tw], mb[:, 0:tw]),
                    waits=[b_m, pe_q], inc=True)
                src = sqb
            else:
                w1 = None
                src = xb
            if gamma is None:
                w2 = P.op("vector", lambda v, j=j, src=src, yb=yb, tw=tw: v.tensor_tensor(
                    yb[:, j * TC:j * TC + tw], src[:, j * TC:j * TC + tw],
                    rb[:, 0:tw], ALU.mult),
                    waits=[w1, b_r] + list(free_y[pp]), inc=True)
            else:
                w2 = P.op("vector", lambda v, j=j, src=src, sqb=sqb, tw=tw: v.tensor_tensor(
                    sqb[:, j * TC:j * TC + tw], src[:, j * TC:j * TC + tw],
                    rb[:, 0:tw], ALU.mult),
                    waits=[w1, b_r, pe_q], inc=True)
                if beta is not None:
                    w2 = P.op("vector", lambda v, j=j, sqb=sqb, yb=yb, tw=tw: v.tensor_scalar(
                        yb[:, j * TC:j * TC + tw], sqb[:, j * TC:j * TC + tw],
                        gamma[:, j:j + 1], beta[:, j:j + 1], ALU.mult, ALU.add),
                        waits=[w2] + list(free_y[pp]), inc=True)
                else:
                    w2 = P.op("vector", lambda v, j=j, sqb=sqb, yb=yb, tw=tw: v.tensor_scalar_mul(
                        yb[:, j * TC:j * TC + tw], sqb[:, j * TC:j * TC + tw],
                        gamma[:, j:j + 1]),
                        waits=[w2] + list(free_y[pp]), inc=True)
            ytoks.append(w2)
        mb_free = tuple(ytoks)
        free_x[pp] = tuple(ytoks)
        free_sq[pp] = tuple(ytoks)
        sts = [P.dma("scalar", out_ap[j * 128:(j + 1) * 128, t0:t0 + tw],
                     yb[:, j * TC:j * TC + tw], waits=[ytoks[j]], sem=sy[pp]) for j in range(kt)]
        free_y[pp] = tuple(sts)
        last = sts[-1]
    return last



def emit_linear_fm(P, ctx, x_ap, Ttot, K, N, w, name, *, w_dtype=F32,
                   bias=None, act=None, act_scale=1.0, out_ap=None,
                   out_dtype=BF16, resid=None, wait_x=(), wait_w=(),
                   x_dtype=BF16, evac_extra=None):
    kt = K // 128
    ot = N // 128
    assert K % 128 == 0 and N % 128 == 0
    if isinstance(w, tuple):
        wsb, wtok = w
    elif w_dtype == BF16:
        wsb = P.sbuf(f"w_{name}", [128, kt * N], BF16)
        wtok = None
        wls = P.dsems(1)[0]
        for j in range(kt):
            wtok = P.dma("sync", wsb[:, j * N:(j + 1) * N],
                         w[j * 128:(j + 1) * 128, 0:N], waits=wait_w, sem=wls)
    else:
        wsb, wtok = load_weight_bf16(P, w, K, N, name, waits=wait_w)
    bias_sb, bias_tok = bias if bias is not None else (None, None)

    xb = [P.sbuf(f"{name}_x", [128, kt * TC], x_dtype) for _ in range(2)]
    ob = [P.sbuf(f"{name}_o", [128, ot * TC], out_dtype) for _ in range(2)]
    ps = [P.psum(f"{name}_ps", [128, TC], F32) for _ in range(2)]
    sx = P.dsems(2)
    so = P.dsems(2)
    if resid is not None:
        xold_ap, xnew_ap, alpha = resid
        rb = [P.sbuf(f"{name}_r", [128, ot * TC], F32) for _ in range(2)]
        sr = P.dsems(2)
        rfree = [(), ()]
    free_x = [(), ()]
    free_o = [(), ()]
    ps_free = [(), ()]
    last = None
    for c in range(cdiv(Ttot, TC)):
        t0, tw = c * TC, min(TC, Ttot - c * TC)
        pp = c % 2
        lds = [P.dma("sync", xb[pp][:, j * TC:j * TC + tw],
                     x_ap[j * 128:(j + 1) * 128, t0:t0 + tw],
                     waits=free_x[pp] + tuple(wait_x), sem=sx[pp]) for j in range(kt)]
        if resid is not None:
            rlds = [P.dma("gpsimd", rb[pp][:, o * TC:o * TC + tw],
                          xold_ap[o * 128:(o + 1) * 128, t0:t0 + tw],
                          waits=rfree[pp], sem=sr[pp]) for o in range(ot)]
        otoks = []
        for o in range(ot):
            pb = ps[o % 2]
            mmtok = None
            for j in range(kt):
                mmtok = P.op("tensor", lambda t, o=o, j=j, pb=pb, pp=pp, tw=tw: t.matmul(
                    pb[:, 0:tw],
                    wsb[:, j * N + o * 128:j * N + o * 128 + 128],
                    xb[pp][:, j * TC:j * TC + tw],
                    start=(j == 0), stop=(j == kt - 1)),
                    waits=([wtok] if j == 0 else []) + (list(lds) if j == 0 else [])
                          + (list(ps_free[o % 2]) if j == 0 else []),
                    inc=(j == kt - 1))
            afn = act if act is not None else AF.Identity
            bap = bias_sb[:, o:o + 1] if bias_sb is not None else 0.0
            ev = P.op("scalar", lambda s, o=o, pb=pb, pp=pp, tw=tw, afn=afn, bap=bap: s.activation(
                ob[pp][:, o * TC:o * TC + tw], pb[:, 0:tw], afn,
                bias=bap, scale=act_scale),
                waits=[mmtok, bias_tok] + list(free_o[pp]), inc=True)
            ps_free[o % 2] = (ev,)
            if evac_extra is not None:
                ev = evac_extra(P, o, c, tw, ob[pp], ev)
            if resid is not None:
                ev = P.op("vector", lambda v, o=o, pp=pp, tw=tw: v.scalar_tensor_tensor(
                    rb[pp][:, o * TC:o * TC + tw], ob[pp][:, o * TC:o * TC + tw],
                    alpha, rb[pp][:, o * TC:o * TC + tw], ALU.mult, ALU.add),
                    waits=[ev] + rlds, inc=True)
                st = P.dma("scalar", xnew_ap[o * 128:(o + 1) * 128, t0:t0 + tw],
                           rb[pp][:, o * TC:o * TC + tw], waits=[ev], sem=so[pp])
            else:
                st = P.dma("scalar", out_ap[o * 128:(o + 1) * 128, t0:t0 + tw],
                           ob[pp][:, o * TC:o * TC + tw], waits=[ev], sem=so[pp])
            otoks.append(ev)
            last = st
        free_x[pp] = tuple(otoks)
        free_o[pp] = (last,)
        if resid is not None:
            rfree[pp] = tuple(otoks) + (last,)
    return last



def emit_compose_M(P, ctx, cst, wr_ap, wi_ap, m_ap, name="mcomp"):
    n = 1024
    kt = 8
    wr, twr = load_weight_bf16(P, wr_ap, n, n, f"{name}_wr")
    wi, twi = load_weight_bf16(P, wi_ap, n, n, f"{name}_wi")
    sfl = P.dsems(1)[0]
    fr = P.sbuf(f"{name}_fr", [128, kt * n], BF16)
    fi = P.sbuf(f"{name}_fi", [128, kt * n], BF16)
    fin = P.sbuf(f"{name}_fin", [128, kt * n], BF16)
    ftoks = []
    for j in range(kt):
        ftoks.append(P.dma("sync", fr[:, j * n:(j + 1) * n],
                           cst["F_r"][j * 128:(j + 1) * 128, :], sem=sfl))
        ftoks.append(P.dma("sync", fi[:, j * n:(j + 1) * n],
                           cst["F_i"][j * 128:(j + 1) * 128, :], sem=sfl))
        ftoks.append(P.dma("sync", fin[:, j * n:(j + 1) * n],
                           cst["F_i_neg"][j * 128:(j + 1) * 128, :], sem=sfl))
    ar = P.sbuf(f"{name}_ar", [128, kt * n], BF16)
    ai = P.sbuf(f"{name}_ai", [128, kt * n], BF16)
    ps = [P.psum(f"{name}_ps", [128, TC], F32) for _ in range(2)]
    ps_free = [(), ()]
    atoks = []
    pi = 0
    for (dst, w1, f1, w2, f2) in ((ar, wr, fr, wi, fin), (ai, wi, fr, wr, fi)):
        for o in range(8):
            for dchunk in range(2):
                pb = ps[pi % 2]
                mm = None
                for j in range(kt):
                    mm = P.op("tensor", lambda t, pb=pb, w1=w1, f1=f1, o=o, j=j, dchunk=dchunk: t.matmul(
                        pb[:, 0:TC], w1[:, j * n + o * 128:j * n + o * 128 + 128],
                        f1[:, j * n + dchunk * TC:j * n + (dchunk) * TC + TC],
                        start=(j == 0), stop=False),
                        waits=([twr, twi] + ftoks + list(ps_free[pi % 2])) if j == 0 else [])
                for j in range(kt):
                    mm = P.op("tensor", lambda t, pb=pb, w2=w2, f2=f2, o=o, j=j, dchunk=dchunk: t.matmul(
                        pb[:, 0:TC], w2[:, j * n + o * 128:j * 1024 + o * 128 + 128],
                        f2[:, j * n + dchunk * TC:j * n + dchunk * TC + TC],
                        start=False, stop=(j == kt - 1)),
                        waits=[], inc=(j == kt - 1))
                ev = P.op("vector", lambda v, dst=dst, pb=pb, o=o, dchunk=dchunk: v.tensor_copy(
                    dst[:, o * n + dchunk * TC:o * n + dchunk * TC + TC], pb[:, 0:TC]),
                    waits=[mm], inc=True)
                ps_free[pi % 2] = (ev,)
                atoks.append(ev)
                pi += 1
    gr = P.sbuf(f"{name}_gr", [128, kt * n], BF16)
    gin = P.sbuf(f"{name}_gin", [128, kt * n], BF16)
    sgl = P.dsems(1)[0]
    gtoks = []
    for j in range(kt):
        gtoks.append(P.dma("sync", gr[:, j * n:(j + 1) * n],
                           cst["G_r"][j * 128:(j + 1) * 128, :], sem=sgl))
        gtoks.append(P.dma("sync", gin[:, j * n:(j + 1) * n],
                           cst["G_i_neg"][j * 128:(j + 1) * 128, :], sem=sgl))
    mb = P.sbuf(f"{name}_m", [128, TC], BF16)
    smo = P.dsems(1)[0]
    mfree = ()
    last = None
    for dt_ in range(8):
        for oc in range(2):
            pb = ps[pi % 2]
            mm = None
            for j in range(kt):
                mm = P.op("tensor", lambda t, pb=pb, j=j, dt_=dt_, oc=oc: t.matmul(
                    pb[:, 0:TC], ar[:, j * n + dt_ * 128:j * n + dt_ * 128 + 128],
                    gr[:, j * n + oc * TC:j * n + oc * TC + TC],
                    start=(j == 0), stop=False),
                    waits=(atoks + gtoks + list(ps_free[pi % 2])) if j == 0 else [])
            for j in range(kt):
                mm = P.op("tensor", lambda t, pb=pb, j=j, dt_=dt_, oc=oc: t.matmul(
                    pb[:, 0:TC], ai[:, j * n + dt_ * 128:j * n + dt_ * 128 + 128],
                    gin[:, j * n + oc * TC:j * n + oc * TC + TC],
                    start=False, stop=(j == kt - 1)),
                    waits=[], inc=(j == kt - 1))
            ev = P.op("vector", lambda v, pb=pb: v.tensor_copy(mb[:, 0:TC], pb[:, 0:TC]),
                      waits=[mm] + list(mfree), inc=True)
            ps_free[pi % 2] = (ev,)
            st = P.dma("scalar", m_ap[dt_ * 128:dt_ * 128 + 128, oc * TC:oc * TC + TC],
                       mb[:, 0:TC], waits=[ev], sem=smo)
            mfree = (st,)
            last = st
            pi += 1
    return last



def make_phi_elu_extra(P, name):
    scr = P.sbuf(f"{name}_phi", [128, TC], F32)

    def cb(P_, o, c, tw, buf, tok):
        t1 = P_.op("vector", lambda v, o=o, tw=tw, buf=buf: v.tensor_scalar_min(
            scr[:, 0:tw], buf[:, o * TC:o * TC + tw], 0.0), waits=[tok], inc=True)
        t2 = P_.op("scalar", lambda s, tw=tw: s.activation(
            scr[:, 0:tw], scr[:, 0:tw], AF.Exp), waits=[t1], inc=True)
        t3 = P_.op("vector", lambda v, o=o, tw=tw, buf=buf: v.scalar_tensor_tensor(
            buf[:, o * TC:o * TC + tw], buf[:, o * TC:o * TC + tw], 0.0,
            scr[:, 0:tw], ALU.max, ALU.add), waits=[t2], inc=True)
        return t3

    return cb


def emit_kv_colsum(P, ctx, kf_ap, vf_ap, Ttot, name="kv"):
    kvh = P.sbuf(f"{name}_kv", [128, KT], F32)
    ksh = P.sbuf(f"{name}_ks", [128, KT], F32)
    z0 = P.op("vector", lambda v: v.memset(kvh[:, :], 0.0), inc=True)
    z1 = P.op("vector", lambda v: v.memset(ksh[:, :], 0.0), inc=True)
    kb = [P.sbuf(f"{name}_k", [128, KT * TC], F32) for _ in range(2)]
    vb = [P.sbuf(f"{name}_v", [128, KT * TC], F32) for _ in range(2)]
    prod = P.sbuf(f"{name}_p", [128, TC], F32)
    col = P.sbuf(f"{name}_c", [128, 2], F32)
    sk = P.dsems(2)
    sv = P.dsems(2)
    free = [(z0, z1), (z0, z1)]
    tok = (None,)
    for c in range(cdiv(Ttot, TC)):
        t0, tw = c * TC, min(TC, Ttot - c * TC)
        pp = c % 2
        lk = [P.dma("sync", kb[pp][:, j * TC:j * TC + tw],
                    kf_ap[j * 128:(j + 1) * 128, t0:t0 + tw],
                    waits=free[pp], sem=sk[pp]) for j in range(KT)]
        lv = [P.dma("gpsimd", vb[pp][:, j * TC:j * TC + tw],
                    vf_ap[j * 128:(j + 1) * 128, t0:t0 + tw],
                    waits=free[pp], sem=sv[pp]) for j in range(KT)]
        jtoks = []
        for j in range(KT):
            m1 = P.op("vector", lambda v, j=j, pp=pp, tw=tw: v.tensor_mul(
                prod[:, 0:tw], kb[pp][:, j * TC:j * TC + tw], vb[pp][:, j * TC:j * TC + tw]),
                waits=lk + lv + list(tok), inc=True)
            r1 = P.op("vector", lambda v, tw=tw: v.reduce_sum(
                col[:, 0:1], prod[:, 0:tw], AX.X), waits=[m1], inc=True)
            r2 = P.op("vector", lambda v, j=j, pp=pp, tw=tw: v.reduce_sum(
                col[:, 1:2], kb[pp][:, j * TC:j * TC + tw], AX.X), waits=[r1], inc=True)
            a1 = P.op("vector", lambda v, j=j: v.tensor_add(
                kvh[:, j:j + 1], kvh[:, j:j + 1], col[:, 0:1]), waits=[r1], inc=True)
            a2 = P.op("vector", lambda v, j=j: v.tensor_add(
                ksh[:, j:j + 1], ksh[:, j:j + 1], col[:, 1:2]), waits=[r2], inc=True)
            tok = (a2,)
            jtoks.append(a2)
        free[pp] = tuple(jtoks)
    return kvh, ksh, tok[0]


def emit_l3_z(P, ctx, qf_ap, kvh, ksh, out_ap, Ttot, wait, name="l3z"):
    qb = [P.sbuf(f"{name}_q", [128, KT * TC], F32) for _ in range(2)]
    ob = [P.sbuf(f"{name}_o", [128, KT * TC], BF16) for _ in range(2)]
    den = P.sbuf(f"{name}_d", [128, TC], F32)
    num = P.sbuf(f"{name}_n", [128, TC], F32)
    sq = P.dsems(2)
    so = P.dsems(2)
    free_q = [wait, wait]
    free_o = [(), ()]
    last = None
    prev = None
    for c in range(cdiv(Ttot, TC)):
        t0, tw = c * TC, min(TC, Ttot - c * TC)
        pp = c % 2
        lq = [P.dma("sync", qb[pp][:, j * TC:j * TC + tw],
                    qf_ap[j * 128:(j + 1) * 128, t0:t0 + tw],
                    waits=free_q[pp], sem=sq[pp]) for j in range(KT)]
        jt = []
        for j in range(KT):
            d1 = P.op("vector", lambda v, j=j, pp=pp, tw=tw: v.tensor_scalar(
                den[:, 0:tw], qb[pp][:, j * TC:j * TC + tw],
                ksh[:, j:j + 1], 1e-6, ALU.mult, ALU.add),
                waits=lq + ([prev] if prev else []), inc=True)
            d2 = P.op("vector", lambda v, tw=tw: v.reciprocal(
                den[:, 0:tw], den[:, 0:tw]), waits=[d1], inc=True)
            n1 = P.op("vector", lambda v, j=j, pp=pp, tw=tw: v.tensor_scalar_mul(
                num[:, 0:tw], qb[pp][:, j * TC:j * TC + tw],
                kvh[:, j:j + 1]), waits=lq + [d2], inc=True)
            o1 = P.op("vector", lambda v, j=j, pp=pp, tw=tw: v.tensor_mul(
                ob[pp][:, j * TC:j * TC + tw], num[:, 0:tw], den[:, 0:tw]),
                waits=[n1] + list(free_o[pp]), inc=True)
            prev = o1
            jt.append(o1)
        free_q[pp] = tuple(jt)
        sts = [P.dma("scalar", out_ap[j * 128:(j + 1) * 128, t0:t0 + tw],
                     ob[pp][:, j * TC:j * TC + tw], waits=[jt[j]], sem=so[pp])
               for j in range(KT)]
        free_o[pp] = tuple(sts)
        last = sts[-1]
    return last


def emit_l3_attn(P, ctx, ln_ap, xold_ap, xnew_ap, Ttot, w, name="l3a"):
    qf = P.dram(f"{name}_qf", [D, Ttot], F32)
    kf = P.dram(f"{name}_kf", [D, Ttot], F32)
    vf = P.dram(f"{name}_vf", [D, Ttot], F32)
    va = P.dram(f"{name}_va", [D, Ttot], BF16)
    with P.scope():
        bq, tbq = load_vec_cols(P, w["bq"], D, f"{name}bq")
        phi = make_phi_elu_extra(P, f"{name}q")
        t1 = emit_linear_fm(P, ctx, ln_ap, Ttot, D, D, w["wq"], f"{name}q",
                            bias=(bq, tbq), out_ap=qf, out_dtype=F32,
                            evac_extra=phi)
    P.barrier()
    with P.scope():
        bk, tbk = load_vec_cols(P, w["bk"], D, f"{name}bk")
        phi = make_phi_elu_extra(P, f"{name}k")
        t2 = emit_linear_fm(P, ctx, ln_ap, Ttot, D, D, w["wk"], f"{name}k",
                            bias=(bk, tbk), out_ap=kf, out_dtype=F32,
                            evac_extra=phi)
    P.barrier()
    with P.scope():
        bv, tbv = load_vec_cols(P, w["bv"], D, f"{name}bv")
        t3 = emit_linear_fm(P, ctx, ln_ap, Ttot, D, D, w["wv"], f"{name}v",
                            bias=(bv, tbv), out_ap=vf, out_dtype=F32)
    P.barrier()
    with P.scope():
        kvh, ksh, t4 = emit_kv_colsum(P, ctx, kf, vf, Ttot, name=f"{name}kv")
        t5 = emit_l3_z(P, ctx, qf, kvh, ksh, va, Ttot, (t4,), name=f"{name}z")
    P.barrier()
    with P.scope():
        bo, tbo = load_vec_cols(P, w["bo"], D, f"{name}bo")
        t6 = emit_linear_fm(P, ctx, va, Ttot, D, D, w["wo"], f"{name}o",
                            bias=(bo, tbo), act=AF.Gelu_apprx_tanh,
                            resid=(xold_ap, xnew_ap, 1.0))
    P.barrier()
    return t6



def emit_l2_attn(P, ctx, ln_ap, xold_ap, xnew_ap, Ttot, w, name="l2a"):
    H = 16
    scale = 64.0 ** -0.25
    phiq_ap = P.dram(f"{name}_pq", [D, Ttot], BF16)
    nch = cdiv(Ttot, TC)
    assert Ttot % 128 == 0

    with P.scope():
        wq, twq = load_weight_bf16(P, w["wqT"], D, D, f"{name}wq")
        wk, twk = load_weight_bf16(P, w["wkT"], D, D, f"{name}wk")
        wv, twv = load_weight_bf16(P, w["wvT"], D, D, f"{name}wv")
        bq, tbq = load_vec_cols(P, w["bq"], D, f"{name}bq")
        bqs = P.sbuf(f"{name}_bqs", [128, KT], F32)
        tbqs = P.op("vector", lambda v: v.tensor_scalar_mul(
            bqs[:, 0:KT], bq[:, 0:KT], scale), waits=[tbq], inc=True)
        bvr = P.sbuf(f"{name}_bvr", [1, D], F32)
        tbv = P.dma("sync", bvr[0:1, :], w["bv"].unsqueeze(0), sem=P.dsems(1)[0])
        bvb = P.sbuf(f"{name}_bvb", [128, D], F32)
        tb = None
        with P.scope():
            bvps = P.psum(f"{name}_bvp", [128, TC], F32)
            for half in range(2):
                t1 = P.op("tensor", lambda t, half=half: t.matmul(
                    bvps[:, 0:TC], ctx["bcast1"][0:1, :], bvr[0:1, half * TC:half * TC + TC],
                    start=True, stop=True), waits=[tbv, ctx["const_tok"], tb], inc=True)
                tb = P.op("vector", lambda v, half=half: v.tensor_copy(
                    bvb[:, half * TC:half * TC + TC], bvps[:, 0:TC]), waits=[t1], inc=True)
        P.barrier()
        wtoks = [twq, twk, twv, tbqs, tb]

        lnb = [P.sbuf(f"{name}_ln", [128, KT * TC], BF16) for _ in range(2)]
        pqb = [P.sbuf(f"{name}_pqo", [128, KT * TC], BF16) for _ in range(2)]
        ktb = P.sbuf(f"{name}_kt", [128, D], BF16)
        vtb = P.sbuf(f"{name}_vt", [128, D], BF16)
        scr = P.sbuf(f"{name}_scr", [128, TC], F32)
        scr2 = P.sbuf(f"{name}_scr2", [128, TC], F32)
        psf = [P.psum(f"{name}_psf", [128, TC], F32) for _ in range(2)]
        pst = [P.psum(f"{name}_pst", [128, 2 * TC], F32) for _ in range(2)]
        kvp = P.psum(f"{name}_kvp", [128, 2 * TC], F32)
        sln = P.dsems(2)
        spq = P.dsems(2)
        free_ln = [(), ()]
        free_pq = [(), ()]
        psf_free = [(), ()]
        pst_free = [(), ()]
        kt_free = ()
        vt_free = ()
        scr_free = ()
        scr2_free = ()
        bank_first = {0: True, 1: True}
        kv_last_mm = None
        for c in range(nch):
            t0, tw = c * TC, min(TC, Ttot - c * TC)
            pp = c % 2
            lds = [P.dma("sync", lnb[pp][:, j * TC:j * TC + tw],
                         ln_ap[j * 128:(j + 1) * 128, t0:t0 + tw],
                         waits=free_ln[pp], sem=sln[pp]) for j in range(KT)]
            pq_toks = []
            pq_last = None
            for o in range(KT):
                pb = psf[o % 2]
                mm = None
                for j in range(KT):
                    mm = P.op("tensor", lambda t, o=o, j=j, pb=pb, pp=pp, tw=tw: t.matmul(
                        pb[:, 0:tw], wq[:, j * D + o * 128:j * D + o * 128 + 128],
                        lnb[pp][:, j * TC:j * TC + tw],
                        start=(j == 0), stop=(j == KT - 1)),
                        waits=(wtoks + lds + list(psf_free[o % 2])) if j == 0 else [],
                        inc=(j == KT - 1))
                e1 = P.op("scalar", lambda s, o=o, pb=pb, tw=tw: s.activation(
                    scr[:, 0:tw], pb[:, 0:tw], AF.Silu,
                    bias=bqs[:, o:o + 1], scale=scale),
                    waits=[mm] + list(scr_free), inc=True)
                psf_free[o % 2] = (e1,)
                e2 = P.op("vector", lambda v, o=o, pp=pp, tw=tw: v.tensor_scalar_add(
                    pqb[pp][:, o * TC:o * TC + tw], scr[:, 0:tw], 1.0),
                    waits=[e1] + list(free_pq[pp]), inc=True)
                scr_free = (e2,)
                st = P.dma("scalar", phiq_ap[o * 128:(o + 1) * 128, t0:t0 + tw],
                           pqb[pp][:, o * TC:o * TC + tw], waits=[e2], sem=spq[pp])
                pq_toks.append(e2)
                pq_last = st
            free_pq[pp] = (pq_last,)
            kv_toks = []
            last_iter_c = (c == nch - 1)
            for ts_ in range(tw // 128):
                last_iter = last_iter_c and (ts_ == tw // 128 - 1)
                ktoks = []
                for which in ("k", "v"):
                    pb = pst[0 if which == "k" else 1]
                    wmat = wk if which == "k" else wv
                    for nh in range(2):
                        mm = None
                        for j in range(KT):
                            mm = P.op("tensor", lambda t, pb=pb, wmat=wmat, j=j, nh=nh,
                                      pp=pp, ts_=ts_: t.matmul(
                                pb[:, nh * TC:nh * TC + TC],
                                lnb[pp][:, j * TC + ts_ * 128:j * TC + ts_ * 128 + 128],
                                wmat[:, j * D + nh * TC:j * D + nh * TC + TC],
                                start=(j == 0), stop=(j == KT - 1)),
                                waits=(list(pst_free[0 if which == "k" else 1]) + lds)
                                      if (j == 0 and nh == 0) else [],
                                inc=(j == KT - 1))
                        if which == "k":
                            ev = P.op("scalar", lambda s, nh=nh, pb=pb: s.activation(
                                scr2[:, 0:TC], pb[:, nh * TC:nh * TC + TC], AF.Silu,
                                scale=scale),
                                waits=[mm] + list(scr2_free), inc=True)
                            ev = P.op("vector", lambda v, nh=nh: v.tensor_scalar_add(
                                ktb[:, nh * TC:nh * TC + TC], scr2[:, 0:TC], 1.0),
                                waits=[ev] + list(kt_free), inc=True)
                            scr2_free = (ev,)
                        else:
                            ev = P.op("vector", lambda v, nh=nh, pb=pb: v.tensor_tensor(
                                vtb[:, nh * TC:nh * TC + TC], pb[:, nh * TC:nh * TC + TC],
                                bvb[:, nh * TC:nh * TC + TC], ALU.add),
                                waits=[mm] + list(vt_free), inc=True)
                        ktoks.append(ev)
                for m in range(8):
                    bank = 0 if m < 4 else 1
                    for hh in range(2):
                        h = 2 * m + hh
                        st_flag = bank_first[bank]
                        bank_first[bank] = False
                        sp_flag = last_iter and hh == 1 and m in (3, 7)
                        kv_last_mm = P.op("tensor", lambda t, m=m, h=h, hh=hh, st_flag=st_flag, sp_flag=sp_flag: t.matmul(
                            kvp[:, m * 128 + hh * 64:m * 128 + hh * 64 + 64],
                            ktb[:, m * 128:m * 128 + 128],
                            vtb[:, h * 64:h * 64 + 64],
                            start=st_flag, stop=sp_flag),
                            waits=ktoks, inc=(m == 7 and hh == 1))
                kv_toks.append(kv_last_mm)
                kt_free = (kv_last_mm,)
                vt_free = (kv_last_mm,)
            free_ln[pp] = tuple(pq_toks) + tuple(kv_toks)
        kvsb = P.sbuf(f"{name}_kvsb", [128, 2 * TC], BF16)
        kv_ev = P.op("vector", lambda v: v.tensor_copy(kvsb[:, :], kvp[:, :]),
                     waits=[kv_last_mm], inc=True)
        z1 = P.op("vector", lambda v: v.memset(
            kvsb[64:128, :].rearrange("p (m c) -> p m c", c=128)[:, :, 0:64], 0.0),
            waits=[kv_ev], inc=True)
        kv_ev = P.op("vector", lambda v: v.memset(
            kvsb[0:64, :].rearrange("p (m c) -> p m c", c=128)[:, :, 64:128], 0.0),
            waits=[z1], inc=True)
    P.barrier()
    with P.scope():
        wo, two = load_weight_bf16(P, w["woT"], D, D, f"{name}wo")
        bo, tbo = load_vec_cols(P, w["bo"], D, f"{name}bo")
        pqc = [P.sbuf(f"{name}_pqc", [128, KT * TC], BF16) for _ in range(2)]
        wvs = [P.sbuf(f"{name}_wvs", [128, KT * TC], BF16) for _ in range(2)]
        rb = [P.sbuf(f"{name}_rb", [128, KT * TC], F32) for _ in range(2)]
        ob = [P.sbuf(f"{name}_ob", [128, KT * TC], F32) for _ in range(2)]
        psh = [P.psum(f"{name}_psh", [128, TC], F32) for _ in range(2)]
        pso = [P.psum(f"{name}_pso", [128, TC], F32) for _ in range(2)]
        spc = P.dsems(2)
        srb = P.dsems(2)
        sst = P.dsems(2)
        free_pqc = [(), ()]
        free_rb = [(), ()]
        free_ob = [(), ()]
        psh_free = [(), ()]
        pso_free = [(), ()]
        last = None
        for c in range(nch):
            t0, tw = c * TC, min(TC, Ttot - c * TC)
            pp = c % 2
            lds = [P.dma("sync", pqc[pp][:, j * TC:j * TC + tw],
                         phiq_ap[j * 128:(j + 1) * 128, t0:t0 + tw],
                         waits=free_pqc[pp], sem=spc[pp]) for j in range(KT)]
            rlds = [P.dma("gpsimd", rb[pp][:, j * TC:j * TC + tw],
                          xold_ap[j * 128:(j + 1) * 128, t0:t0 + tw],
                          waits=free_rb[pp], sem=srb[pp]) for j in range(KT)]
            wv_toks = []
            for o in range(KT):
                pb = psh[o % 2]
                mm = P.op("tensor", lambda t, o=o, pb=pb, pp=pp, tw=tw: t.matmul(
                    pb[:, 0:tw], kvsb[:, o * 128:o * 128 + 128],
                    pqc[pp][:, o * TC:o * TC + tw],
                    start=True, stop=True),
                    waits=[kv_ev] + lds + list(psh_free[o % 2]), inc=True)
                ev = P.op("scalar", lambda s, o=o, pp=pp, pb=pb, tw=tw: s.activation(
                    wvs[pp][:, o * TC:o * TC + tw], pb[:, 0:tw], AF.Identity),
                    waits=[mm], inc=True)
                psh_free[o % 2] = (ev,)
                wv_toks.append(ev)
            otoks = []
            for o in range(KT):
                pb = pso[o % 2]
                mm = None
                for j in range(KT):
                    mm = P.op("tensor", lambda t, o=o, j=j, pb=pb, pp=pp, tw=tw: t.matmul(
                        pb[:, 0:tw], wo[:, j * D + o * 128:j * D + o * 128 + 128],
                        wvs[pp][:, j * TC:j * TC + tw],
                        start=(j == 0), stop=(j == KT - 1)),
                        waits=([two] + wv_toks + list(pso_free[o % 2])) if j == 0 else [],
                        inc=(j == KT - 1))
                e1 = P.op("scalar", lambda s, o=o, pb=pb, pp=pp, tw=tw: s.activation(
                    ob[pp][:, o * TC:o * TC + tw], pb[:, 0:tw], AF.Identity,
                    bias=bo[:, o:o + 1]),
                    waits=[mm, tbo] + list(free_ob[pp]), inc=True)
                pso_free[o % 2] = (e1,)
                e2 = P.op("vector", lambda v, o=o, pp=pp, tw=tw: v.tensor_add(
                    rb[pp][:, o * TC:o * TC + tw], rb[pp][:, o * TC:o * TC + tw],
                    ob[pp][:, o * TC:o * TC + tw]),
                    waits=[e1] + rlds, inc=True)
                st = P.dma("scalar", xnew_ap[o * 128:(o + 1) * 128, t0:t0 + tw],
                           rb[pp][:, o * TC:o * TC + tw], waits=[e2], sem=sst[pp])
                otoks.append(e2)
                last = st
            free_pqc[pp] = tuple(wv_toks)
            free_rb[pp] = tuple(otoks) + (last,)
            free_ob[pp] = tuple(otoks)
    P.barrier()
    return last



QMID = (0.25, 0.75, 1.25, 1.75, 2.5, 3.5, 5.0)


def make_quant(P, name, C):
    nb = C // 16
    am = P.sbuf(f"{name}_am", [128, nb], F32)
    thr = P.sbuf(f"{name}_th", [128, nb], F32)
    absx = P.sbuf(f"{name}_ab", [128, C], BF16)
    accA = P.sbuf(f"{name}_aa", [128, C], BF16)
    accB = P.sbuf(f"{name}_ab2", [128, C], BF16)
    ind = P.sbuf(f"{name}_in", [128, C], BF16)
    i7 = P.sbuf(f"{name}_i7", [128, C], BF16)
    sgn = P.sbuf(f"{name}_sg", [128, C], BF16)
    state = {"free": ()}

    def qfn(x_ap, out_ap, cols, waits=(), out_waits=()):
        b = cols // 16
        x3 = x_ap.rearrange("p (g k) -> p g k", k=16)
        waits = tuple(waits) + state["free"]
        t0 = P.op("vector", lambda v: v.tensor_reduce(
            am[:, 0:b].unsqueeze(2), x3, AX.X, ALU.max, apply_absolute_value=True),
            waits=waits, inc=True)
        t1 = P.op("vector", lambda v: v.tensor_scalar(
            am[:, 0:b], am[:, 0:b], 1.0 / 6.0, 1e-12, ALU.mult, ALU.max),
            waits=[t0], inc=True)
        ts_ = P.op("scalar", lambda s: s.sign(sgn[:, 0:cols], x_ap), waits=waits, inc=True)
        ta = P.op("scalar", lambda s: s.activation(
            absx[:, 0:cols], x_ap, AF.Abs), waits=waits, inc=True)
        toks = []
        prev_thr = t1
        for j in range(7):
            dst = accA if j == 0 else (accB if j == 4 else (i7 if j == 6 else ind))
            tt_ = P.op("vector", lambda v, j=j: v.tensor_scalar_mul(
                thr[:, 0:b], am[:, 0:b], QMID[j]), waits=[prev_thr], inc=True)
            thr3 = thr[:, 0:b].unsqueeze(2).broadcast_to([128, b, 16])
            g = P.op("vector", lambda v, dst=dst, thr3=thr3: v.tensor_tensor(
                dst[:, 0:cols].rearrange("p (g k) -> p g k", k=16),
                absx[:, 0:cols].rearrange("p (g k) -> p g k", k=16),
                thr3, ALU.is_gt), waits=[tt_, ta], inc=True)
            prev_thr = g
            if j in (1, 2, 3):
                g = P.op("vector", lambda v: v.tensor_add(
                    accA[:, 0:cols], accA[:, 0:cols], ind[:, 0:cols]),
                    waits=[g], inc=True)
            elif j == 5:
                g = P.op("vector", lambda v: v.tensor_add(
                    accB[:, 0:cols], accB[:, 0:cols], ind[:, 0:cols]),
                    waits=[g], inc=True)
            prev_thr = g
            toks.append(g)
        q1 = P.op("vector", lambda v: v.scalar_tensor_tensor(
            accA[:, 0:cols], accA[:, 0:cols], 0.5, accB[:, 0:cols],
            ALU.mult, ALU.add), waits=toks, inc=True)
        q2 = P.op("vector", lambda v: v.scalar_tensor_tensor(
            accA[:, 0:cols], i7[:, 0:cols], 2.0, accA[:, 0:cols],
            ALU.mult, ALU.add), waits=[q1], inc=True)
        q3 = P.op("vector", lambda v: v.tensor_mul(
            accA[:, 0:cols], accA[:, 0:cols], sgn[:, 0:cols]), waits=[q2, ts_], inc=True)
        am3 = am[:, 0:b].unsqueeze(2).broadcast_to([128, b, 16])
        q4 = P.op("vector", lambda v, am3=am3: v.tensor_tensor(
            out_ap.rearrange("p (g k) -> p g k", k=16),
            accA[:, 0:cols].rearrange("p (g k) -> p g k", k=16),
            am3, ALU.mult), waits=[q3] + list(out_waits), inc=True)
        state["free"] = (q4,)
        return q4

    return qfn




def emit_l1_wprep(P, ctx, w, wq, ident_ap, name="l1w"):
    idb = P.sbuf(f"{name}_id", [128, 128], BF16)
    tid = P.dma("sync", idb[:, :], ident_ap, sem=P.dsems(1)[0])
    last = None
    with P.scope():
        qfn = make_quant(P, f"{name}q", 2816)
        stg = P.sbuf(f"{name}_st", [128, 2816], F32)
        qin = P.sbuf(f"{name}_qi", [128, 2816], BF16)
        qout = P.sbuf(f"{name}_qo", [128, 2816], BF16)
        trp = [P.psum(f"{name}_tp", [128, 128], BF16) for _ in range(2)]
        trev = [P.sbuf(f"{name}_te", [128, 128], BF16) for _ in range(2)]
        sld = P.dsems(1)[0]
        sst = P.dsems(2)
        free_st = ()
        free_ev = [(), ()]
        trp_free = [(), ()]
        for (src, dst, R, C) in ((w["w_gate"], wq["wqT_gate"], 2816, 1024),
                                 (w["w_up"], wq["wqT_up"], 2816, 1024),
                                 (w["w_down"], wq["wqT_down"], 1024, 2816)):
            for r in range(R // 128):
                ld = P.dma("sync", stg[:, 0:C], src[r * 128:(r + 1) * 128, 0:C],
                           waits=free_st, sem=sld)
                ct = P.op("vector", lambda v, C=C: v.tensor_copy(qin[:, 0:C], stg[:, 0:C]),
                          waits=[ld], inc=True)
                qt = qfn(qin[:, 0:C], qout[:, 0:C], C, waits=[ct])
                free_st = (qt,)
                for kb in range(C // 128):
                    pi = kb % 2
                    tt_ = P.op("tensor", lambda t, kb=kb, pi=pi: t.transpose(
                        trp[pi][:, :], qout[:, kb * 128:kb * 128 + 128], idb[:, :]),
                        waits=[qt, tid] + list(trp_free[pi]), inc=True)
                    ev = P.op("scalar", lambda s, pi=pi: s.copy(
                        trev[pi][:, :], trp[pi][:, :]),
                        waits=[tt_] + list(free_ev[pi]), inc=True)
                    trp_free[pi] = (ev,)
                    st = P.dma("scalar", dst[kb * 128:(kb + 1) * 128, r * 128:(r + 1) * 128],
                               trev[pi][:, :], waits=[ev], sem=sst[pi])
                    free_ev[pi] = (st,)
                    last = st
    P.barrier()
    return last


def emit_l1_ffn(P, ctx, ln_ap, xold_ap, xnew_ap, Ttot, wq, ident_ap, name="l1f"):
    I = 2816
    NC5 = [(i * TC, min(TC, I - i * TC)) for i in range(cdiv(I, TC))]
    idb = P.sbuf(f"{name}_id", [128, 128], BF16)
    tid = P.dma("sync", idb[:, :], ident_ap, sem=P.dsems(1)[0])
    qfn = make_quant(P, f"{name}q", I)
    lnb = P.sbuf(f"{name}_ln", [128, KT * TC], BF16)
    lntok = P.sbuf(f"{name}_lt", [128, D], BF16)
    xqtok = P.sbuf(f"{name}_xq", [128, D], BF16)
    xqfm = P.sbuf(f"{name}_xf", [128, KT * TC], BF16)
    gsb = P.sbuf(f"{name}_gs", [128, I], F32)
    htok = P.sbuf(f"{name}_ht", [128, I], BF16)
    hqtok = P.sbuf(f"{name}_hq", [128, I], BF16)
    hqfm = P.sbuf(f"{name}_hf", [128, 22 * TC], BF16)
    rb = P.sbuf(f"{name}_rb", [128, KT * TC], F32)
    ob = P.sbuf(f"{name}_ob", [128, KT * TC], F32)
    wbuf = [P.sbuf(f"{name}_wb", [128, KT * TC], BF16) for _ in range(2)]
    wdbuf = [P.sbuf(f"{name}_wd", [128, 22 * 128], BF16) for _ in range(2)]
    trp = [P.psum(f"{name}_tp", [128, 128], BF16) for _ in range(2)]
    trev = [P.sbuf(f"{name}_te", [128, 128], BF16) for _ in range(2)]
    psg = P.psum(f"{name}_pg", [128, TC], F32)
    psu = P.psum(f"{name}_pu", [128, TC], F32)
    psd = [P.psum(f"{name}_pd", [128, TC], F32) for _ in range(2)]
    sln = P.dsems(2)
    sw = P.dsems(2)
    swd = P.dsems(2)
    srb = P.dsems(1)[0]
    sst = P.dsems(1)[0]
    free_ln = ()
    free_rb = ()
    free_w = [(), ()]
    free_wd = [(), ()]
    trp_free = [(), ()]
    trev_free = [(), ()]
    psg_free = ()
    psu_free = ()
    psd_free = [(), ()]
    xqfm_free = ()
    hqfm_free = ()
    xqtok_free = ()
    hqtok_free = ()
    gs_free = ()
    ht_free = ()
    hq_free = ()
    ob_free = ()
    last = None
    nch = cdiv(Ttot, TC)
    for c in range(nch):
        t0, tw = c * TC, min(TC, Ttot - c * TC)
        nts = tw // 128
        lds = [P.dma("sync", lnb[:, j * TC:j * TC + tw],
                     ln_ap[j * 128:(j + 1) * 128, t0:t0 + tw],
                     waits=free_ln, sem=sln[c % 2]) for j in range(KT)]
        rlds = [P.dma("gpsimd", rb[:, j * TC:j * TC + tw],
                      xold_ap[j * 128:(j + 1) * 128, t0:t0 + tw],
                      waits=free_rb, sem=srb) for j in range(KT)]
        xq_toks = []
        xqtok_free = xqtok_free[-8:]
        hqtok_free = hqtok_free[-22:]
        for ts_ in range(nts):
            evs = []
            for j in range(KT):
                pi = j % 2
                tt_ = P.op("tensor", lambda t, j=j, ts_=ts_, pi=pi: t.transpose(
                    trp[pi][:, :], lnb[:, j * TC + ts_ * 128:j * TC + ts_ * 128 + 128],
                    idb[:, :]),
                    waits=lds + [tid] + list(trp_free[pi]), inc=True)
                ev = P.op("scalar", lambda s, j=j, pi=pi: s.copy(
                    lntok[:, j * 128:j * 128 + 128], trp[pi][:, :]),
                    waits=[tt_] + list(trev_free[pi]), inc=True)
                trp_free[pi] = (ev,)
                evs.append(ev)
            qt = qfn(lntok[:, 0:D], xqtok[:, 0:D], D, waits=evs,
                     out_waits=xqtok_free)
            trev_free = [(qt,), (qt,)]
            for j in range(KT):
                pi = j % 2
                tt_ = P.op("tensor", lambda t, j=j, pi=pi: t.transpose(
                    trp[pi][:, :], xqtok[:, j * 128:j * 128 + 128], idb[:, :]),
                    waits=[qt] + list(trp_free[pi]), inc=True)
                ev = P.op("scalar", lambda s, j=j, ts_=ts_, pi=pi: s.copy(
                    xqfm[:, j * TC + ts_ * 128:j * TC + ts_ * 128 + 128], trp[pi][:, :]),
                    waits=[tt_] + list(xqfm_free), inc=True)
                trp_free[pi] = (ev,)
                xqtok_free = xqtok_free + (tt_,)
                xq_toks.append(ev)
        h_toks = []
        for ts_ in range(nts):
            ts_toks = []
            for ni, (n0, nw) in enumerate(NC5):
                wl = []
                for which in range(2):
                    wsrc = wq["wqT_gate"] if which == 0 else wq["wqT_up"]
                    wl.append([P.dma("sync" if which == 0 else "gpsimd",
                                     wbuf[which][:, j * TC:j * TC + nw],
                                     wsrc[j * 128:(j + 1) * 128, n0:n0 + nw],
                                     waits=free_w[which], sem=sw[which])
                               for j in range(KT)])
                mmg = mmu = None
                for j in range(KT):
                    mmg = P.op("tensor", lambda t, j=j, ts_=ts_, nw=nw: t.matmul(
                        psg[:, 0:nw], xqfm[:, j * TC + ts_ * 128:j * TC + ts_ * 128 + 128],
                        wbuf[0][:, j * TC:j * TC + nw],
                        start=(j == 0), stop=(j == KT - 1)),
                        waits=(xq_toks + wl[0] + list(psg_free)) if j == 0 else [],
                        inc=(j == KT - 1))
                for j in range(KT):
                    mmu = P.op("tensor", lambda t, j=j, ts_=ts_, nw=nw: t.matmul(
                        psu[:, 0:nw], xqfm[:, j * TC + ts_ * 128:j * TC + ts_ * 128 + 128],
                        wbuf[1][:, j * TC:j * TC + nw],
                        start=(j == 0), stop=(j == KT - 1)),
                        waits=(wl[1] + list(psu_free)) if j == 0 else [],
                        inc=(j == KT - 1))
                free_w[0] = (mmg,)
                free_w[1] = (mmu,)
                e1 = P.op("scalar", lambda s, n0=n0, nw=nw: s.activation(
                    gsb[:, n0:n0 + nw], psg[:, 0:nw], AF.Silu),
                    waits=[mmg] + list(gs_free), inc=True)
                psg_free = (e1,)
                e2 = P.op("vector", lambda v, n0=n0, nw=nw: v.tensor_tensor(
                    htok[:, n0:n0 + nw], psu[:, 0:nw], gsb[:, n0:n0 + nw], ALU.mult),
                    waits=[e1, mmu] + list(ht_free), inc=True)
                psu_free = (e2,)
                ts_toks.append(e2)
            qt2 = qfn(htok[:, 0:I], hqtok[:, 0:I], I, waits=ts_toks,
                      out_waits=hqtok_free)
            gs_free = (qt2,)
            ht_free = (qt2,)
            for g in range(22):
                pi = g % 2
                tt_ = P.op("tensor", lambda t, g=g, pi=pi: t.transpose(
                    trp[pi][:, :], hqtok[:, g * 128:g * 128 + 128], idb[:, :]),
                    waits=[qt2] + list(trp_free[pi]), inc=True)
                ev = P.op("scalar", lambda s, g=g, ts_=ts_, pi=pi: s.copy(
                    hqfm[:, g * TC + ts_ * 128:g * TC + ts_ * 128 + 128], trp[pi][:, :]),
                    waits=[tt_] + list(hqfm_free), inc=True)
                trp_free[pi] = (ev,)
                hqtok_free = hqtok_free + (tt_,)
                h_toks.append(ev)
        hq_free = tuple(h_toks)
        otoks = []
        for o in range(KT):
            pi = o % 2
            wld = [P.dma("sync", wdbuf[pi][:, g * 128:g * 128 + 128],
                         wq["wqT_down"][g * 128:(g + 1) * 128, o * 128:(o + 1) * 128],
                         waits=free_wd[pi], sem=swd[pi]) for g in range(22)]
            pb = psd[pi]
            mm = None
            for g in range(22):
                mm = P.op("tensor", lambda t, g=g, o=o, pb=pb, pi=pi, tw=tw: t.matmul(
                    pb[:, 0:tw], wdbuf[pi][:, g * 128:g * 128 + 128],
                    hqfm[:, g * TC:g * TC + tw],
                    start=(g == 0), stop=(g == 21)),
                    waits=(h_toks + wld + list(psd_free[pi])) if g == 0 else [],
                    inc=(g == 21))
            free_wd[pi] = (mm,)
            e1 = P.op("scalar", lambda s, o=o, pb=pb, tw=tw: s.activation(
                ob[:, o * TC:o * TC + tw], pb[:, 0:tw], AF.Identity),
                waits=[mm] + list(ob_free), inc=True)
            psd_free[pi] = (e1,)
            e2 = P.op("vector", lambda v, o=o, tw=tw: v.scalar_tensor_tensor(
                rb[:, o * TC:o * TC + tw], ob[:, o * TC:o * TC + tw], 0.5,
                rb[:, o * TC:o * TC + tw], ALU.mult, ALU.add),
                waits=[e1] + rlds, inc=True)
            st = P.dma("scalar", xnew_ap[o * 128:(o + 1) * 128, t0:t0 + tw],
                       rb[:, o * TC:o * TC + tw], waits=[e2], sem=sst)
            otoks.append(e2)
            last = st
        ob_free = tuple(otoks)
        free_ln = tuple(xq_toks)
        free_rb = tuple(otoks) + (last,)
        xqfm_free = (mmg, mmu)
        hqfm_free = (mm,)
        gs_free = gs_free
        ht_free = ht_free
    return last


def emit_l1_conv(P, ctx, h_ap, x0_ap, x1_ap, Ttot, wconv_ap, convb, name="l1c"):
    cb_sb, cb_tok = convb
    wsb = P.sbuf(f"{name}_w", [128, 4 * 2 * D], BF16)
    wtok = None
    with P.scope():
        stg = P.sbuf(f"{name}_ws", [128, D], F32)
        sld = P.dsems(1)[0]
        free = ()
        for k in range(4):
            for ct in range(2):
                ld = P.dma("sync", stg[:, 0:D],
                           wconv_ap[k, ct * 128:(ct + 1) * 128, 0:D],
                           waits=free, sem=sld)
                wtok = P.op("vector", lambda v, k=k, ct=ct: v.tensor_copy(
                    wsb[:, (k * 2 + ct) * D:(k * 2 + ct) * D + D], stg[:, 0:D]),
                    waits=[ld], inc=True)
                free = (wtok,)
    hb = [P.sbuf(f"{name}_h", [128, KT * (TC + 3)], BF16) for _ in range(2)]
    rb = [P.sbuf(f"{name}_r", [128, KT * TC], F32) for _ in range(2)]
    ps = [P.psum(f"{name}_ps", [128, TC], F32) for _ in range(2)]
    sh = P.dsems(2)
    sr = P.dsems(2)
    sst = P.dsems(2)
    free_h = [(wtok,), (wtok,)]
    free_r = [(wtok,), (wtok,)]
    ps_free = [(), ()]
    CW = TC + 3
    last = None
    for c in range(cdiv(Ttot, TC)):
        t0, tw = c * TC, min(TC, Ttot - c * TC)
        pp = c % 2
        hl = []
        for j in range(KT):
            if c == 0:
                hl.append(P.dma("sync", hb[pp][:, j * CW + 3:j * CW + 3 + tw],
                                h_ap[j * 128:(j + 1) * 128, 0:tw],
                                waits=free_h[pp], sem=sh[pp]))
            else:
                hl.append(P.dma("sync", hb[pp][:, j * CW:j * CW + 3 + tw],
                                h_ap[j * 128:(j + 1) * 128, t0 - 3:t0 + tw],
                                waits=free_h[pp], sem=sh[pp]))
        if c == 0:
            zt = P.op("vector", lambda v, pp=pp: v.memset(
                hb[pp][:, 0:KT * CW].rearrange("p (j w) -> p j w", w=CW)[:, :, 0:3], 0.0),
                waits=list(free_h[pp]), inc=True)
            hl.append(zt)
        rl = [P.dma("gpsimd", rb[pp][:, j * TC:j * TC + tw],
                    x0_ap[j * 128:(j + 1) * 128, t0:t0 + tw],
                    waits=free_r[pp], sem=sr[pp]) for j in range(KT)]
        otoks = []
        for o in range(KT):
            g = o // 2
            pb = ps[o % 2]
            mm = None
            first = True
            for k in range(4):
                for ct in range(2):
                    cj = g * 2 + ct
                    mm = P.op("tensor", lambda t, o=o, k=k, ct=ct, cj=cj, pb=pb, pp=pp, tw=tw: t.matmul(
                        pb[:, 0:tw],
                        wsb[:, (k * 2 + ct) * D + o * 128:(k * 2 + ct) * D + o * 128 + 128],
                        hb[pp][:, cj * CW + k:cj * CW + k + tw],
                        start=(k == 0 and ct == 0), stop=(k == 3 and ct == 1)),
                        waits=(hl + [wtok] + list(ps_free[o % 2])) if first else [],
                        inc=(k == 3 and ct == 1))
                    first = False
            e1 = P.op("vector", lambda v, o=o, pb=pb, pp=pp, tw=tw: v.tensor_scalar(
                pb[:, 0:tw], pb[:, 0:tw], cb_sb[:, o:o + 1], None, ALU.add),
                waits=[mm, cb_tok], inc=True)
            e2 = P.op("vector", lambda v, o=o, pb=pb, pp=pp, tw=tw: v.tensor_add(
                rb[pp][:, o * TC:o * TC + tw], rb[pp][:, o * TC:o * TC + tw], pb[:, 0:tw]),
                waits=[e1] + rl, inc=True)
            ps_free[o % 2] = (e2,)
            st = P.dma("scalar", x1_ap[o * 128:(o + 1) * 128, t0:t0 + tw],
                       rb[pp][:, o * TC:o * TC + tw], waits=[e2], sem=sst[pp])
            otoks.append(e2)
            last = st
        free_h[pp] = tuple(otoks)
        free_r[pp] = tuple(otoks) + (last,)
    return last


def emit_l1_pool(P, ctx, h_ap, x1_ap, x2_ap, Ttot, w, name="l1p"):
    wqkv, twq = load_weight_bf16(P, w["Wqkv3T"], D, 3 * D, f"{name}w")
    bqkv, tb = load_vec_cols(P, w["bqkv3"], 3 * D, f"{name}b")
    isc_sb = P.sbuf(f"{name}_is", [128, 4096], F32)
    tis = P.dma("sync", isc_sb[:, 0:Ttot], w["inv_scale_b"][:, 0:Ttot], sem=P.dsems(1)[0])
    lnb = [P.sbuf(f"{name}_ln", [128, KT * TC], BF16) for _ in range(2)]
    qkv = P.sbuf(f"{name}_qkv", [128, 3 * KT * TC], F32)
    ones = P.sbuf(f"{name}_1", [128, TC], F32)
    t1s = P.op("vector", lambda v: v.memset(ones[:, :], 1.0), inc=True)
    carry = P.sbuf(f"{name}_cy", [128, 2 * KT], F32)
    tcy = P.op("vector", lambda v: v.memset(carry[:, :], 0.0), inc=True)
    scr = P.sbuf(f"{name}_sc", [128, TC], F32)
    rb = [P.sbuf(f"{name}_r", [128, KT * TC], F32) for _ in range(2)]
    ps = [P.psum(f"{name}_ps", [128, TC], F32) for _ in range(2)]
    sln = P.dsems(2)
    sr = P.dsems(2)
    sst = P.dsems(2)
    free_ln = [(), ()]
    free_r = [(), ()]
    ps_free = [(), ()]
    qkv_free = ()
    scr_free = ()
    carry_tok = tcy
    last = None
    for c in range(cdiv(Ttot, TC)):
        t0, tw = c * TC, min(TC, Ttot - c * TC)
        pp = c % 2
        lds = [P.dma("sync", lnb[pp][:, j * TC:j * TC + tw],
                     h_ap[j * 128:(j + 1) * 128, t0:t0 + tw],
                     waits=free_ln[pp], sem=sln[pp]) for j in range(KT)]
        rl = [P.dma("gpsimd", rb[pp][:, j * TC:j * TC + tw],
                    x1_ap[j * 128:(j + 1) * 128, t0:t0 + tw],
                    waits=free_r[pp], sem=sr[pp]) for j in range(KT)]
        evs = []
        for o in range(3 * KT):
            pb = ps[o % 2]
            mm = None
            for j in range(KT):
                mm = P.op("tensor", lambda t, o=o, j=j, pb=pb, pp=pp, tw=tw: t.matmul(
                    pb[:, 0:tw], wqkv[:, j * 3 * D + o * 128:j * 3 * D + o * 128 + 128],
                    lnb[pp][:, j * TC:j * TC + tw],
                    start=(j == 0), stop=(j == KT - 1)),
                    waits=([twq] + lds + list(ps_free[o % 2])) if j == 0 else [],
                    inc=(j == KT - 1))
            ev = P.op("scalar", lambda s, o=o, pb=pb, tw=tw: s.activation(
                qkv[:, o * TC:o * TC + tw], pb[:, 0:tw], AF.Identity,
                bias=bqkv[:, o:o + 1]),
                waits=[mm, tb] + list(qkv_free), inc=True)
            ps_free[o % 2] = (ev,)
            evs.append(ev)
        otoks = []
        for j in range(KT):
            qs = qkv[:, j * TC:j * TC + tw]
            ks = qkv[:, (KT + j) * TC:(KT + j) * TC + tw]
            vs = qkv[:, (2 * KT + j) * TC:(2 * KT + j) * TC + tw]
            s1 = P.op("vector", lambda v, qs=qs, tw=tw, j=j: v.tensor_tensor_scan(
                qs, ones[:, 0:tw], qs, carry[:, j:j + 1], ALU.mult, ALU.add),
                waits=evs + [t1s, carry_tok], inc=True)
            c1 = P.op("vector", lambda v, qs=qs, tw=tw, j=j: v.tensor_copy(
                carry[:, j:j + 1], qs[:, tw - 1:tw]), waits=[s1], inc=True)
            s2 = P.op("vector", lambda v, qs=qs, tw=tw, t0=t0: v.tensor_tensor(
                scr[:, 0:tw], qs, isc_sb[:, t0:t0 + tw], ALU.mult),
                waits=[s1, tis] + list(scr_free), inc=True)
            s3 = P.op("vector", lambda v, ks=ks, tw=tw: v.tensor_mul(
                scr[:, 0:tw], scr[:, 0:tw], ks), waits=[s2], inc=True)
            s4 = P.op("vector", lambda v, tw=tw, j=j: v.tensor_tensor_scan(
                scr[:, 0:tw], ones[:, 0:tw], scr[:, 0:tw], carry[:, KT + j:KT + j + 1],
                ALU.mult, ALU.add), waits=[s3, c1], inc=True)
            c2 = P.op("vector", lambda v, tw=tw, j=j: v.tensor_copy(
                carry[:, KT + j:KT + j + 1], scr[:, tw - 1:tw]), waits=[s4], inc=True)
            carry_tok = c2
            s5 = P.op("vector", lambda v, tw=tw, t0=t0: v.tensor_tensor(
                scr[:, 0:tw], scr[:, 0:tw], isc_sb[:, t0:t0 + tw], ALU.mult),
                waits=[s4, c2], inc=True)
            s6 = P.op("vector", lambda v, vs=vs, tw=tw: v.tensor_mul(
                scr[:, 0:tw], scr[:, 0:tw], vs), waits=[s5], inc=True)
            s7 = P.op("vector", lambda v, j=j, pp=pp, tw=tw: v.tensor_add(
                rb[pp][:, j * TC:j * TC + tw], rb[pp][:, j * TC:j * TC + tw],
                scr[:, 0:tw]), waits=[s6] + rl, inc=True)
            scr_free = (s7,)
            st = P.dma("scalar", x2_ap[j * 128:(j + 1) * 128, t0:t0 + tw],
                       rb[pp][:, j * TC:j * TC + tw], waits=[s7], sem=sst[pp])
            otoks.append(s7)
            last = st
        qkv_free = tuple(otoks)
        free_ln[pp] = tuple(evs)
        free_r[pp] = tuple(otoks) + (last,)
    return last


def emit_add3(P, ctx, a_ap, b_ap, c_ap, out_ap, Ttot, name="a3", alpha_c=1.0):
    ab = [P.sbuf(f"{name}_a", [128, KT * TC], F32) for _ in range(2)]
    bb = [P.sbuf(f"{name}_b", [128, KT * TC], F32) for _ in range(2)]
    cb = [P.sbuf(f"{name}_c", [128, KT * TC], F32) for _ in range(2)]
    sa = P.dsems(2)
    sb_ = P.dsems(2)
    sc_ = P.dsems(2)
    so = P.dsems(1)[0]
    free = [(), ()]
    last = None
    for c in range(cdiv(Ttot, TC)):
        t0, tw = c * TC, min(TC, Ttot - c * TC)
        pp = c % 2
        la = [P.dma("sync", ab[pp][:, j * TC:j * TC + tw],
                    a_ap[j * 128:(j + 1) * 128, t0:t0 + tw], waits=free[pp], sem=sa[pp])
              for j in range(KT)]
        lb = [P.dma("gpsimd", bb[pp][:, j * TC:j * TC + tw],
                    b_ap[j * 128:(j + 1) * 128, t0:t0 + tw], waits=free[pp], sem=sb_[pp])
              for j in range(KT)]
        lc = [P.dma("scalar", cb[pp][:, j * TC:j * TC + tw],
                    c_ap[j * 128:(j + 1) * 128, t0:t0 + tw], waits=free[pp], sem=sc_[pp])
              for j in range(KT)]
        toks = []
        for j in range(KT):
            t1 = P.op("vector", lambda v, j=j, pp=pp, tw=tw: v.scalar_tensor_tensor(
                cb[pp][:, j * TC:j * TC + tw], cb[pp][:, j * TC:j * TC + tw], alpha_c,
                bb[pp][:, j * TC:j * TC + tw], ALU.mult, ALU.add),
                waits=la + lb + lc, inc=True)
            t2 = P.op("vector", lambda v, j=j, pp=pp, tw=tw: v.tensor_add(
                cb[pp][:, j * TC:j * TC + tw], cb[pp][:, j * TC:j * TC + tw],
                ab[pp][:, j * TC:j * TC + tw]), waits=[t1], inc=True)
            st = P.dma("gpsimd", out_ap[j * 128:(j + 1) * 128, t0:t0 + tw],
                       cb[pp][:, j * TC:j * TC + tw], waits=[t2], sem=so)
            toks.append(t2)
            last = st
        free[pp] = tuple(toks) + (last,)
    return last


def emit_l1_attn(P, ctx, ln_ap, xold_ap, xnew_ap, Ttot, w, name="l1a"):
    h1 = P.dram(f"{name}_h1", [D, Ttot], BF16)
    x1 = P.dram(f"{name}_x1", [D, Ttot], F32)
    h2 = P.dram(f"{name}_h2", [D, Ttot], BF16)
    x2 = P.dram(f"{name}_x2", [D, Ttot], F32)
    h3 = P.dram(f"{name}_h3", [D, Ttot], BF16)
    hb = P.dram(f"{name}_hb", [4 * D, Ttot], BF16)
    hu = P.dram(f"{name}_hu", [D, Ttot], F32)
    with P.scope():
        g1, tg1 = load_vec_cols(P, w["ln_conv_g"], D, f"{name}g1")
        b1, tb1 = load_vec_cols(P, w["ln_conv_b"], D, f"{name}b1")
        t = emit_ln(P, ctx, ln_ap, h1, Ttot, out_dtype=BF16, gamma=g1, beta=b1,
                    eps=1e-5, wait_x=(tg1, tb1), name=f"{name}ln1")
    P.barrier()
    with P.scope():
        cb, tcb = load_vec_cols(P, w["conv_b"], D, f"{name}cb")
        t = emit_l1_conv(P, ctx, h1, ln_ap, x1, Ttot, w["wconv4"], (cb, tcb),
                         name=f"{name}cv")
    P.barrier()
    with P.scope():
        g2, tg2 = load_vec_cols(P, w["ln_attn_g"], D, f"{name}g2")
        b2, tb2 = load_vec_cols(P, w["ln_attn_b"], D, f"{name}b2")
        t = emit_ln(P, ctx, x1, h2, Ttot, out_dtype=BF16, gamma=g2, beta=b2,
                    eps=1e-5, wait_x=(tg2, tb2), name=f"{name}ln2")
    P.barrier()
    with P.scope():
        t = emit_l1_pool(P, ctx, h2, x1, x2, Ttot, w, name=f"{name}pl")
    P.barrier()
    with P.scope():
        g3, tg3 = load_vec_cols(P, w["ln_boom_g"], D, f"{name}g3")
        b3, tb3 = load_vec_cols(P, w["ln_boom_b"], D, f"{name}b3")
        t = emit_ln(P, ctx, x2, h3, Ttot, out_dtype=BF16, gamma=g3, beta=b3,
                    eps=1e-5, wait_x=(tg3, tb3), name=f"{name}ln3")
    P.barrier()
    with P.scope():
        bbv, tbb = load_vec_cols(P, w["bb"], 4 * D, f"{name}bb")
        t = emit_linear_fm(P, ctx, h3, Ttot, D, 4 * D, w["WbT"], f"{name}mb",
                           bias=(bbv, tbb), act=AF.Gelu, out_ap=hb, out_dtype=BF16)
    P.barrier()
    with P.scope():
        buv, tbu = load_vec_cols(P, w["bu"], D, f"{name}bu")
        t = emit_linear_fm(P, ctx, hb, Ttot, 4 * D, D, w["WuT"], f"{name}mu",
                           bias=(buv, tbu), out_ap=hu, out_dtype=F32)
    P.barrier()
    with P.scope():
        t = emit_add3(P, ctx, xold_ap, x2, hu, xnew_ap, Ttot, name=f"{name}a3")
    P.barrier()
    return t


def emit_l2_ffn(P, ctx, lnss_ap, hss_ap, w, name="l2f", TL=512, NB=8):
    xw_dram = P.dram(f"{name}_xw", [NB, 3 * D, TL], BF16)
    bfic, tbf = load_vec_cols(P, w["bfic"], 3 * D, f"{name}bf")
    inc_, tinc = load_vec_cols(P, w["in_c"], D, f"{name}ic")
    with P.scope():
        wx, twx = load_weight_bf16(P, w["WxT"], D, 3 * D, f"{name}wx")
        t1 = None
        for t in range(NB):
            with P.scope():
                t1 = emit_linear_fm(P, ctx, lnss_ap[t], TL, D, 3 * D, (wx, twx),
                                    f"{name}s1_{t}", out_ap=xw_dram[t], out_dtype=BF16)
            P.barrier()
    P.barrier()
    cst = P.sbuf(f"{name}_c", [128, KT * TL], F32)
    cbf = P.sbuf(f"{name}_cb", [128, KT * TL], BF16)
    stage2 = P.scope()
    stage2.__enter__()
    ut, tut = load_weight_bf16(P, w["UT"], D, 3 * D, f"{name}ut")
    fb = P.sbuf(f"{name}_f", [128, KT * TL], F32)
    ib = P.sbuf(f"{name}_i", [128, KT * TL], F32)
    gb = P.sbuf(f"{name}_g", [128, KT * TL], F32)
    scr = P.sbuf(f"{name}_s", [128, TL], F32)
    scr2 = P.sbuf(f"{name}_s2", [128, KT * TL], F32)
    xwb = [P.sbuf(f"{name}_xwb", [128, TL], F32) for _ in range(2)]
    xwl = [P.sbuf(f"{name}_xwl", [128, TL], BF16) for _ in range(2)]
    ps = [P.psum(f"{name}_ps", [128, TL], F32) for _ in range(2)]
    sxw = P.dsems(2)
    ctoks = [P.op("scalar", lambda s, j=j: s.activation(
        cst[:, j * TL:(j + 1) * TL], cst[:, j * TL:(j + 1) * TL], AF.Identity,
        bias=inc_[:, j:j + 1], scale=0.0), waits=[tinc], inc=True) for j in range(KT)]
    ps_free = [(), ()]
    xw_free = [(), ()]
    xwo_free = [(), ()]
    scr_free = ()
    gate_free = ()
    for t in range(NB):
        casts = [P.op("vector", lambda v, j=j: v.tensor_copy(
            cbf[:, j * TL:(j + 1) * TL], cst[:, j * TL:(j + 1) * TL]),
            waits=ctoks, inc=True) for j in range(KT)]
        gtoks = []
        for o in range(24):
            pi = o % 2
            lx = P.dma("sync", xwl[pi][:, 0:TL],
                       xw_dram[t, o * 128:(o + 1) * 128, 0:TL],
                       waits=xw_free[pi], sem=sxw[pi])
            pb = ps[pi]
            mm = None
            for j in range(KT):
                mm = P.op("tensor", lambda tt, o=o, j=j, pb=pb: tt.matmul(
                    pb[:, 0:TL], ut[:, j * 3 * D + o * 128:j * 3 * D + o * 128 + 128],
                    cbf[:, j * TL:(j + 1) * TL],
                    start=(j == 0), stop=(j == KT - 1)),
                    waits=([tut] + casts + list(ps_free[pi])) if j == 0 else [],
                    inc=(j == KT - 1))
            a1 = P.op("vector", lambda v, pb=pb, pi=pi: v.tensor_add(
                xwb[pi][:, 0:TL], xwl[pi][:, 0:TL], pb[:, 0:TL]),
                waits=[mm, lx] + list(xwo_free[pi]), inc=True)
            ps_free[pi] = (a1,)
            dst, af = ((fb, AF.Sigmoid) if o < 8 else
                       ((ib, AF.Sigmoid) if o < 16 else (gb, AF.Tanh)))
            oo = o % 8
            a2 = P.op("scalar", lambda s, dst=dst, af=af, oo=oo, o=o, pi=pi: s.activation(
                dst[:, oo * TL:oo * TL + TL], xwb[pi][:, 0:TL], af,
                bias=bfic[:, o:o + 1]),
                waits=[a1, tbf] + list(gate_free), inc=True)
            xw_free[pi] = (a1,)
            xwo_free[pi] = (a2,)
            gtoks.append(a2)
        ctoks = []
        for j in range(KT):
            m1 = P.op("vector", lambda v, j=j: v.tensor_tensor(
                scr2[:, j * TL:(j + 1) * TL], ib[:, j * TL:(j + 1) * TL],
                gb[:, j * TL:(j + 1) * TL], ALU.mult),
                waits=gtoks + list(scr_free), inc=True)
            m2 = P.op("vector", lambda v, j=j: v.tensor_mul(
                cst[:, j * TL:(j + 1) * TL], fb[:, j * TL:(j + 1) * TL],
                cst[:, j * TL:(j + 1) * TL]), waits=[m1] + casts, inc=True)
            m3 = P.op("vector", lambda v, j=j: v.tensor_add(
                cst[:, j * TL:(j + 1) * TL], cst[:, j * TL:(j + 1) * TL],
                scr2[:, j * TL:(j + 1) * TL]), waits=[m2], inc=True)
            ctoks.append(m3)
        scr_free = tuple(ctoks)
        gate_free = tuple(ctoks)
    stage2.__exit__(None, None, None)
    P.barrier()
    with P.scope():
        wo, two = load_weight_bf16(P, w["woT"], D, D, f"{name}wo")
        uo, tuo = load_weight_bf16(P, w["uoT"], D, D, f"{name}uo")
        bo, tbo = load_vec_cols(P, w["bo"], D, f"{name}bo")
        cfb = cbf
        casts = [P.op("vector", lambda v, j=j: v.tensor_copy(
            cfb[:, j * TL:(j + 1) * TL], cst[:, j * TL:(j + 1) * TL]),
            waits=ctoks, inc=True) for j in range(KT)]
        cuo = P.sbuf(f"{name}_cuo", [128, KT * TL], F32)
        pso = [P.psum(f"{name}_pso", [128, TL], F32) for _ in range(2)]
        pso_free = [(), ()]
        for o in range(KT):
            pi = o % 2
            pb = pso[pi]
            mm = None
            for j in range(KT):
                mm = P.op("tensor", lambda tt, o=o, j=j, pb=pb: tt.matmul(
                    pb[:, 0:TL], uo[:, j * D + o * 128:j * D + o * 128 + 128],
                    cfb[:, j * TL:(j + 1) * TL],
                    start=(j == 0), stop=(j == KT - 1)),
                    waits=([tuo] + casts + list(pso_free[pi])) if j == 0 else [],
                    inc=(j == KT - 1))
            ev = P.op("vector", lambda v, o=o, pb=pb: v.tensor_copy(
                cuo[:, o * TL:(o + 1) * TL], pb[:, 0:TL]), waits=[mm], inc=True)
            pso_free[pi] = (ev,)
        cuo_tok = ev
        lnb = P.sbuf(f"{name}_lnb", [128, KT * TL], BF16)
        hb = P.sbuf(f"{name}_hb", [128, KT * TL], F32)
        sln = P.dsems(1)[0]
        sho = P.dsems(1)[0]
        ln_free = ()
        hb_free = ()
        last = None
        for t in range(NB):
            lds = [P.dma("sync", lnb[:, j * TL:(j + 1) * TL],
                         lnss_ap[t, j * 128:(j + 1) * 128, 0:TL],
                         waits=ln_free, sem=sln) for j in range(KT)]
            otoks = []
            for o in range(KT):
                pi = o % 2
                pb = pso[pi]
                mm = None
                for j in range(KT):
                    mm = P.op("tensor", lambda tt, o=o, j=j, pb=pb: tt.matmul(
                        pb[:, 0:TL], wo[:, j * D + o * 128:j * D + o * 128 + 128],
                        lnb[:, j * TL:(j + 1) * TL],
                        start=(j == 0), stop=(j == KT - 1)),
                        waits=([two, cuo_tok] + lds + list(pso_free[pi])) if j == 0 else [],
                        inc=(j == KT - 1))
                a1 = P.op("vector", lambda v, o=o, pb=pb: v.tensor_add(
                    pb[:, 0:TL], pb[:, 0:TL], cuo[:, o * TL:(o + 1) * TL]),
                    waits=[mm], inc=True)
                a2 = P.op("scalar", lambda s, o=o, pb=pb: s.activation(
                    hb[:, o * TL:(o + 1) * TL], pb[:, 0:TL], AF.Tanh,
                    bias=bo[:, o:o + 1]),
                    waits=[a1, tbo] + list(hb_free), inc=True)
                pso_free[pi] = (a2,)
                st = P.dma("scalar", hss_ap[t, o * 128:(o + 1) * 128, 0:TL],
                           hb[:, o * TL:(o + 1) * TL], waits=[a2], sem=sho)
                otoks.append(a2)
                last = st
            ln_free = tuple(otoks)
            hb_free = tuple(otoks) + (last,)
    P.barrier()
    return last


DI = 2048
DIT = DI // 128


def emit_l0_conv(P, ctx, xz_ap, u_ap, Ttot, w, name="l0c"):
    cw, tcw = load_vec_cols(P, w["convw4"], 4 * DI, f"{name}w")
    cb, tcb = load_vec_cols(P, w["conv_b"], DI, f"{name}b")
    CW = TC + 3
    ub = [P.sbuf(f"{name}_u", [128, DIT * CW], BF16) for _ in range(2)]
    acc = P.sbuf(f"{name}_a", [128, TC], F32)
    ob = [P.sbuf(f"{name}_o", [128, DIT * TC], BF16) for _ in range(2)]
    su = P.dsems(2)
    so = P.dsems(2)
    free_u = [(), ()]
    free_o = [(), ()]
    acc_free = ()
    last = None
    for c in range(cdiv(Ttot, TC)):
        t0, tw = c * TC, min(TC, Ttot - c * TC)
        pp = c % 2
        hl = []
        for j in range(DIT):
            if c == 0:
                hl.append(P.dma("sync", ub[pp][:, j * CW + 3:j * CW + 3 + tw],
                                xz_ap[j * 128:(j + 1) * 128, 0:tw],
                                waits=free_u[pp], sem=su[pp]))
            else:
                hl.append(P.dma("sync", ub[pp][:, j * CW:j * CW + 3 + tw],
                                xz_ap[j * 128:(j + 1) * 128, t0 - 3:t0 + tw],
                                waits=free_u[pp], sem=su[pp]))
        if c == 0:
            zt = P.op("vector", lambda v, pp=pp: v.memset(
                ub[pp][:, 0:DIT * CW].rearrange("p (j w) -> p j w", w=CW)[:, :, 0:3], 0.0),
                waits=list(free_u[pp]), inc=True)
            hl.append(zt)
        otoks = []
        for j in range(DIT):
            t1 = P.op("vector", lambda v, j=j, pp=pp, tw=tw: v.tensor_scalar_mul(
                acc[:, 0:tw], ub[pp][:, j * CW + 0:j * CW + 0 + tw], cw[:, 0 * 16 + j:0 * 16 + j + 1]),
                waits=hl + [tcw] + list(acc_free), inc=True)
            for k in (1, 2, 3):
                t1 = P.op("vector", lambda v, j=j, k=k, pp=pp, tw=tw: v.scalar_tensor_tensor(
                    acc[:, 0:tw], ub[pp][:, j * CW + k:j * CW + k + tw],
                    cw[:, k * 16 + j:k * 16 + j + 1], acc[:, 0:tw],
                    ALU.mult, ALU.add), waits=[t1], inc=True)
            t2 = P.op("scalar", lambda s, j=j, pp=pp, tw=tw: s.activation(
                ob[pp][:, j * TC:j * TC + tw], acc[:, 0:tw], AF.Silu,
                bias=cb[:, j:j + 1]),
                waits=[t1, tcb] + list(free_o[pp]), inc=True)
            acc_free = (t2,)
            st = P.dma("scalar", u_ap[j * 128:(j + 1) * 128, t0:t0 + tw],
                       ob[pp][:, j * TC:j * TC + tw], waits=[t2], sem=so[pp])
            otoks.append(t2)
            last = st
        free_u[pp] = tuple(otoks)
        free_o[pp] = (last,)
    return last


def emit_l0_scan(P, ctx, u_ap, xz_ap, ln_ap, xin2_ap, Ttot, w, name="l0s"):
    xpT, txp = load_weight_bf16(P, w["xpT"], DI, 128, f"{name}xp")
    dtwT, tdtw = load_weight_bf16(P, w["dtwT"], 64 * 2 // 128 * 64 if False else 64, DI, f"{name}dtw") if False else (None, None)
    dtw = P.sbuf(f"{name}_dtw", [128, DI], BF16)
    with P.scope():
        stg = P.sbuf(f"{name}_dst", [128, DI], F32)
        l1 = P.dma("sync", stg[64:128, :], w["dtwT"], sem=P.dsems(1)[0])
        tdtw = P.op("vector", lambda v: v.tensor_copy(dtw[64:128, :], stg[64:128, :]),
                    waits=[l1], inc=True)
    dtb, tdtb = load_vec_cols(P, w["dtb"], DI, f"{name}dtb", waits=(tdtw,))
    dsk, tdsk = load_vec_cols(P, w["Dskip"], DI, f"{name}dsk", waits=(tdtw,))
    opT, topw = load_weight_bf16(P, w["outprojT"], DI, D, f"{name}op", waits=(tdtw,))
    sc_l = P.dsems(1)[0]
    rm = P.sbuf(f"{name}_rm", [128, 16 * 128], BF16)
    ym = P.sbuf(f"{name}_ym", [128, 16 * 128], BF16)
    eb = P.sbuf(f"{name}_eb", [48, 128], BF16)
    als = P.sbuf(f"{name}_als", [128, 256], F32)
    ct1 = P.dma("sync", rm[:, :], w["Rm"], sem=sc_l, waits=(tdtw,))
    ct2 = P.dma("sync", ym[:, :], w["Ym"], sem=sc_l)
    ct3 = P.dma("sync", eb[0:16, :], w["EB"], sem=sc_l)
    ct3b = P.dma("sync", eb[32:48, :], w["EB"], sem=sc_l)
    ct4 = P.dma("sync", als[:, :], w["A_scan"], sem=sc_l)
    a1 = P.op("scalar", lambda s: s.activation(als[:, :], als[:, :], AF.Exp),
              waits=[ct1, ct2, ct3, ct3b, ct4], inc=True)
    atok = P.op("vector", lambda v: v.tensor_scalar_mul(als[:, :], als[:, :], -1.0),
                waits=[a1], inc=True)
    carry = P.sbuf(f"{name}_cy", [128, 256], F32)
    tcy = P.op("vector", lambda v: v.memset(carry[:, :], 0.0), inc=True)
    ub = P.sbuf(f"{name}_u", [128, DIT * TC], BF16)
    zb = P.sbuf(f"{name}_z", [128, DIT * TC], BF16)
    lnb = P.sbuf(f"{name}_ln", [128, KT * TC], F32)
    dbc = P.sbuf(f"{name}_dbc", [128, TC], BF16)
    dtbf = P.sbuf(f"{name}_dtbf", [128, DIT * TC], BF16)
    dts = P.sbuf(f"{name}_dt", [128, DIT * TC], F32)
    dut = P.sbuf(f"{name}_du", [128, TC], BF16)
    bbc = P.sbuf(f"{name}_bb", [128, TC], F32)
    cbc = P.sbuf(f"{name}_cb", [128, TC], F32)
    asb = P.sbuf(f"{name}_as", [128, TC], F32)
    bsb = P.sbuf(f"{name}_bs", [128, TC], F32)
    ssb = P.sbuf(f"{name}_ss", [128, TC], F32)
    scb = P.sbuf(f"{name}_scb", [128, TC], BF16)
    yvb = P.sbuf(f"{name}_yvb", [128, TC], F32)
    ygs = P.sbuf(f"{name}_yg", [128, DIT * TC], BF16)
    zss = P.sbuf(f"{name}_zs", [128, TC], F32)
    scr = P.sbuf(f"{name}_scr", [128, TC], F32)
    xo = P.sbuf(f"{name}_xo", [128, KT * TC], F32)
    ps_g = [P.psum(f"{name}_pg", [128, TC], F32) for _ in range(2)]
    ps_dt = P.psum(f"{name}_pdt", [128, TC], F32)
    ps_du = P.psum(f"{name}_pdu", [128, TC], F32)
    ps_y = P.psum(f"{name}_py", [128, TC], F32)
    ps_bc = P.psum(f"{name}_pbc", [128, TC], F32)
    sus = P.dsems(2)
    szs = P.dsems(2)
    slns = P.dsems(2)
    sxo = P.dsems(1)[0]
    free_u = ()
    free_z = ()
    free_ln = ()
    psg_free = [(), ()]
    dbc_free = ()
    dt_free = ()
    dtbf_free = ()
    bc_free = ()
    scan_free = ()
    y_free = ()
    yg_free = ()
    xo_free = ()
    carry_tok = tcy
    last = None
    for c in range(cdiv(Ttot, TC)):
        t0, tw = c * TC, min(TC, Ttot - c * TC)
        lu = [P.dma("sync", ub[:, j * TC:j * TC + tw],
                    u_ap[j * 128:(j + 1) * 128, t0:t0 + tw],
                    waits=free_u, sem=sus[c % 2]) for j in range(DIT)]
        lz = [P.dma("gpsimd", zb[:, j * TC:j * TC + tw],
                    xz_ap[(DIT + j) * 128:(DIT + j + 1) * 128, t0:t0 + tw],
                    waits=free_z, sem=szs[c % 2]) for j in range(DIT)]
        lln = [P.dma("scalar", lnb[:, j * TC:j * TC + tw],
                     ln_ap[j * 128:(j + 1) * 128, t0:t0 + tw],
                     waits=free_ln, sem=slns[c % 2]) for j in range(KT)]
        mm = None
        for j in range(DIT):
            mm = P.op("tensor", lambda t, j=j, tw=tw: t.matmul(
                ps_bc[:, 0:tw], xpT[:, j * 128:(j + 1) * 128],
                ub[:, j * TC:j * TC + tw],
                start=(j == 0), stop=(j == DIT - 1)),
                waits=([txp] + lu + list(bc_free)) if j == 0 else [],
                inc=(j == DIT - 1))
        dv = P.op("vector", lambda v, tw=tw: v.tensor_copy(dbc[:, 0:tw], ps_bc[:, 0:tw]),
                  waits=[mm] + list(dbc_free), inc=True)
        bm = P.op("tensor", lambda t, tw=tw: t.matmul(
            ps_bc[:, 0:tw], eb[0:16, :], dbc[0:16, 0:tw], start=True, stop=True),
            waits=[dv], inc=True)
        bv = P.op("vector", lambda v, tw=tw: v.tensor_copy(bbc[:, 0:tw], ps_bc[:, 0:tw]),
                  waits=[bm] + list(bc_free), inc=True)
        cm = P.op("tensor", lambda t, tw=tw: t.matmul(
            ps_bc[:, 0:tw], eb[32:48, :], dbc[32:48, 0:tw], start=True, stop=True),
            waits=[bv], inc=True)
        cv = P.op("vector", lambda v, tw=tw: v.tensor_copy(cbc[:, 0:tw], ps_bc[:, 0:tw]),
                  waits=[cm], inc=True)
        bc_free = (cv,)
        dt_toks = []
        for jt in range(DIT):
            pi = jt % 2
            pb = ps_g[pi]
            m1 = P.op("tensor", lambda t, jt=jt, pb=pb, tw=tw: t.matmul(
                pb[:, 0:tw], dtw[64:128, jt * 128:(jt + 1) * 128], dbc[64:128, 0:tw],
                start=True, stop=True),
                waits=[tdtw, dv] + list(psg_free[pi]), inc=True)
            r1 = P.op("vector", lambda v, jt=jt, pb=pb, tw=tw: v.tensor_scalar(
                dts[:, jt * TC:jt * TC + tw], pb[:, 0:tw], dtb[:, jt:jt + 1], None,
                ALU.add), waits=[m1, tdtb] + list(dt_free), inc=True)
            a2 = P.op("scalar", lambda s, tw=tw: s.activation(
                scr[:, 0:tw], dts[0:128, 0:tw] if False else dts[:, 0:tw], AF.Abs),
                waits=[r1], inc=True) if False else None
            a2 = P.op("scalar", lambda s, jt=jt, tw=tw: s.activation(
                scr[:, 0:tw], dts[:, jt * TC:jt * TC + tw], AF.Abs),
                waits=[r1] + list(scan_free), inc=True)
            a3 = P.op("scalar", lambda s, tw=tw: s.activation(
                scr[:, 0:tw], scr[:, 0:tw], AF.Exp, scale=-1.0), waits=[a2], inc=True)
            one_t, one_tok = constf(ctx, 1.0)
            a4 = P.op("scalar", lambda s, tw=tw, one_t=one_t: s.activation(
                scr[:, 0:tw], scr[:, 0:tw], AF.Ln, bias=one_t[:, 0:1]),
                waits=[a3, one_tok], inc=True)
            r2 = P.op("vector", lambda v, jt=jt, tw=tw: v.tensor_scalar_max(
                dts[:, jt * TC:jt * TC + tw], dts[:, jt * TC:jt * TC + tw], 0.0),
                waits=[a2], inc=True)
            r3 = P.op("vector", lambda v, jt=jt, tw=tw: v.tensor_add(
                dts[:, jt * TC:jt * TC + tw], dts[:, jt * TC:jt * TC + tw],
                scr[:, 0:tw]), waits=[r2, a4], inc=True)
            r4 = P.op("vector", lambda v, jt=jt, tw=tw: v.tensor_copy(
                dtbf[:, jt * TC:jt * TC + tw], dts[:, jt * TC:jt * TC + tw]),
                waits=[r3] + list(dtbf_free), inc=True)
            psg_free[pi] = (r1,)
            scan_free = (r4,)
            dt_toks.append(r4)
        yg_toks = []
        prev_yg = None
        for jt in range(DIT):
            du = P.op("vector", lambda v, jt=jt, tw=tw: v.tensor_tensor(
                dut[:, 0:tw], dts[:, jt * TC:jt * TC + tw], ub[:, jt * TC:jt * TC + tw],
                ALU.mult), waits=[dt_toks[jt]] + list(dt_free), inc=True)
            ymm_last = None
            prev_ae = None
            for m in range(16):
                j = jt * 16 + m
                b1 = P.op("tensor", lambda t, jt=jt, m=m, tw=tw: t.matmul(
                    ps_dt[:, 0:tw], rm[:, m * 128:(m + 1) * 128],
                    dtbf[:, jt * TC:jt * TC + tw], start=True, stop=True),
                    waits=[dt_toks[jt], atok, prev_ae] + list(dt_free), inc=True)
                b2 = P.op("tensor", lambda t, m=m, tw=tw: t.matmul(
                    ps_du[:, 0:tw], rm[:, m * 128:(m + 1) * 128], dut[:, 0:tw],
                    start=True, stop=True), waits=[du] + list(y_free), inc=True)
                ae = P.op("scalar", lambda s, j=j, tw=tw: s.activation(
                    asb[:, 0:tw], ps_dt[:, 0:tw], AF.Exp, scale=als[:, j:j + 1]),
                    waits=[b1] + list(scan_free), inc=True)
                prev_ae = ae
                be = P.op("vector", lambda v, tw=tw: v.tensor_tensor(
                    bsb[:, 0:tw], ps_du[:, 0:tw], bbc[:, 0:tw], ALU.mult),
                    waits=[b2, bv] + list(scan_free), inc=True)
                sc_ = P.op("vector", lambda v, j=j, tw=tw: v.tensor_tensor_scan(
                    ssb[:, 0:tw], asb[:, 0:tw], bsb[:, 0:tw], carry[:, j:j + 1],
                    ALU.mult, ALU.add), waits=[ae, be, carry_tok], inc=True)
                cy = P.op("vector", lambda v, j=j, tw=tw: v.tensor_copy(
                    carry[:, j:j + 1], ssb[:, tw - 1:tw]), waits=[sc_], inc=True)
                carry_tok = cy
                sm = P.op("gpsimd", lambda g, tw=tw: g.tensor_tensor(
                    scb[:, 0:tw], ssb[:, 0:tw], cbc[:, 0:tw], ALU.mult),
                    waits=[sc_, cv], inc=True)
                ymm_last = P.op("tensor", lambda t, m=m, tw=tw: t.matmul(
                    ps_y[:, 0:tw], ym[:, m * 128:(m + 1) * 128], scb[:, 0:tw],
                    start=(m == 0), stop=(m == 15)),
                    waits=[sm] + (list(yg_free) if m == 0 else []), inc=(m == 15))
                scan_free = (ymm_last,)
                y_free = (ymm_last,)
            zs = P.op("scalar", lambda s, jt=jt, tw=tw: s.activation(
                zss[:, 0:tw], zb[:, jt * TC:jt * TC + tw], AF.Silu),
                waits=lz + list(yg_free), inc=True)
            yv = P.op("vector", lambda v, jt=jt, tw=tw: v.scalar_tensor_tensor(
                yvb[:, 0:tw], ub[:, jt * TC:jt * TC + tw], dsk[:, jt:jt + 1],
                ps_y[:, 0:tw], ALU.mult, ALU.add),
                waits=[ymm_last, tdsk, sm, prev_yg], inc=True)
            yg = P.op("vector", lambda v, jt=jt, tw=tw: v.tensor_tensor(
                ygs[:, jt * TC:jt * TC + tw], yvb[:, 0:tw], zss[:, 0:tw], ALU.mult),
                waits=[yv, zs] + list(yg_free), inc=True)
            prev_yg = yg
            yg_free = (yg,)
            yg_toks.append(yg)
        free_u = tuple(yg_toks)
        free_z = tuple(yg_toks)
        dt_free = tuple(yg_toks)
        dtbf_free = tuple(yg_toks)
        otoks = []
        for o in range(KT):
            pi = o % 2
            pb = ps_g[pi]
            mm = None
            for j in range(DIT):
                mm = P.op("tensor", lambda t, o=o, j=j, pb=pb, tw=tw: t.matmul(
                    pb[:, 0:tw], opT[:, j * D + o * 128:j * D + o * 128 + 128],
                    ygs[:, j * TC:j * TC + tw],
                    start=(j == 0), stop=(j == DIT - 1)),
                    waits=([topw] + yg_toks + list(psg_free[pi])) if j == 0 else [],
                    inc=(j == DIT - 1))
            e1 = P.op("vector", lambda v, o=o, pb=pb, tw=tw: v.tensor_add(
                xo[:, o * TC:o * TC + tw], lnb[:, o * TC:o * TC + tw], pb[:, 0:tw]),
                waits=[mm] + lln + list(xo_free), inc=True)
            psg_free[pi] = (e1,)
            st = P.dma("scalar", xin2_ap[o * 128:(o + 1) * 128, t0:t0 + tw],
                       xo[:, o * TC:o * TC + tw], waits=[e1], sem=sxo)
            otoks.append(e1)
            last = st
        free_ln = tuple(otoks)
        xo_free = tuple(otoks) + (last,)
        dbc_free = (dv,) if False else tuple(dt_toks) + (bm, cm)
    return last


def emit_l0_attn(P, ctx, ln_ap, xold_ap, xnew_ap, Ttot, w, name="l0a"):
    x0 = P.dram(f"{name}_x0", [D, Ttot], BF16)
    xz = P.dram(f"{name}_xz", [2 * DI, Ttot], BF16)
    u = P.dram(f"{name}_u", [DI, Ttot], BF16)
    xin2 = P.dram(f"{name}_xi", [D, Ttot], F32)
    z2 = P.dram(f"{name}_z2", [D, Ttot], BF16)
    h1 = P.dram(f"{name}_h1", [DI, Ttot], BF16)
    h2 = P.dram(f"{name}_h2", [DI, Ttot], BF16)
    h3 = P.dram(f"{name}_h3", [D, Ttot], F32)
    with P.scope():
        g1, tg1 = load_vec_cols(P, w["rms1_w"], D, f"{name}r1")
        emit_ln(P, ctx, ln_ap, x0, Ttot, out_dtype=BF16, gamma=g1, rms=True,
                eps=1e-6, wait_x=(tg1,), name=f"{name}rm1")
    P.barrier()
    with P.scope():
        emit_linear_fm(P, ctx, x0, Ttot, D, 2 * DI, w["inprojT"], f"{name}ip",
                       out_ap=xz, out_dtype=BF16)
    P.barrier()
    with P.scope():
        emit_l0_conv(P, ctx, xz, u, Ttot, w, name=f"{name}cv")
    P.barrier()
    with P.scope():
        emit_l0_scan(P, ctx, u, xz, ln_ap, xin2, Ttot, w, name=f"{name}sc")
    P.barrier()
    with P.scope():
        g2, tg2 = load_vec_cols(P, w["rms2_w"], D, f"{name}r2")
        emit_ln(P, ctx, xin2, z2, Ttot, out_dtype=BF16, gamma=g2, rms=True,
                eps=1e-6, wait_x=(tg2,), name=f"{name}rm2")
    P.barrier()
    with P.scope():
        b1, tb1 = load_vec_cols(P, w["fc1_b"], DI, f"{name}b1")
        emit_linear_fm(P, ctx, z2, Ttot, D, DI, w["fc1T"], f"{name}f1",
                       bias=(b1, tb1), act=AF.Relu, out_ap=h1, out_dtype=BF16)
    P.barrier()
    with P.scope():
        b2, tb2 = load_vec_cols(P, w["fc2_b"], DI, f"{name}b2")
        emit_linear_fm(P, ctx, h1, Ttot, DI, DI, w["fc2T"], f"{name}f2",
                       bias=(b2, tb2), act=AF.Relu, out_ap=h2, out_dtype=BF16)
    P.barrier()
    with P.scope():
        b3, tb3 = load_vec_cols(P, w["fc3_b"], D, f"{name}b3")
        emit_linear_fm(P, ctx, h2, Ttot, DI, D, w["fc3T"], f"{name}f3",
                       bias=(b3, tb3), out_ap=h3, out_dtype=F32)
    P.barrier()
    with P.scope():
        t = emit_add3(P, ctx, xold_ap, xin2, h3, xnew_ap, Ttot, name=f"{name}a3")
    P.barrier()
    return t


def emit_shift(P, ctx, x_ap, out_ap, Ttot, name="shf"):
    zc = P.sbuf(f"{name}_z", [128, 1], F32)
    zt = P.op("vector", lambda v: v.memset(zc[:, :], 0.0), inc=True)
    sd, zsem = P.dsems(2)
    toks = []
    for j in range(KT):
        if j < 4:
            toks.append(P.dma("sync", out_ap[j * 128:(j + 1) * 128, 0:Ttot],
                              x_ap[j * 128:(j + 1) * 128, 0:Ttot], sem=sd))
        else:
            toks.append(P.dma("sync", out_ap[j * 128:(j + 1) * 128, 1:Ttot],
                              x_ap[j * 128:(j + 1) * 128, 0:Ttot - 1], sem=sd))
            toks.append(P.dma("scalar", out_ap[j * 128:(j + 1) * 128, 0:1],
                              zc[:, 0:1], waits=[zt], sem=zsem,
                              allow_slow_non_contiguous=True))
    return toks


def emit_l3f_attn(P, ctx, xs_ap, x_ap, xout_ap, Ttot, w, li, name="l3fa"):
    import os as _os
    nch = cdiv(Ttot, TC)
    wqkv, twq = load_weight_bf16(P, w["Wqkv"][li], D, 3 * D, f"{name}w")
    wo_sb = P.sbuf(f"{name}_wo", [64, 16 * D], BF16)
    slw = P.dsems(1)[0]
    two = None
    import os as _os
    if "wo" in _os.environ.get("L3FA_SKIP", ""):
        two = P.op("vector", lambda v: v.memset(wo_sb[0:64, :], 0.01), inc=True)
    else:
        for k16 in range(16):
            two = P.dma("sync", wo_sb[0:64, k16 * D:(k16 + 1) * D],
                        w["Wo"][li][64 * k16:64 * k16 + 64, 0:D], sem=slw)
    idb = P.sbuf(f"{name}_id", [128, 128], BF16)
    selo = P.sbuf(f"{name}_sel", [128, 8 * 16], BF16)
    selt = P.sbuf(f"{name}_selt", [16, 8 * 128], BF16)
    rot = P.sbuf(f"{name}_rot", [128, 128], BF16)
    msk = P.sbuf(f"{name}_msk", [128, 3 * 384], F32)
    if "csk" in _os.environ.get("L3FA_SKIP", ""):
        qs_sb = P.sbuf(f"{name}_qsf", [128, KT], F32)
        ks_sb = P.sbuf(f"{name}_ksf", [128, KT], F32)
        tqs = tks = P.op("vector", lambda v: v.memset(qs_sb[:, :], 1.0), waits=[two], inc=True)
        tid = P.op("vector", lambda v: v.memset(idb[:, :], 0.0), inc=True)
        tc1 = P.op("vector", lambda v: v.memset(selo[:, :], 0.0), inc=True)
        tc2 = P.op("vector", lambda v: v.memset(selt[:, :], 0.0), inc=True)
        tc3 = P.op("vector", lambda v: v.memset(rot[:, :], 0.0), inc=True)
        tc4 = P.op("vector", lambda v: v.memset(msk[:, :], 0.0), inc=True)
    else:
        qs_sb, tqs = load_vec_cols(P, w["q_scale_x"][li], D, f"{name}qs", waits=(two,))
        ks_sb, tks = load_vec_cols(P, w["k_scale_x"][li], D, f"{name}ks", waits=(two,))
        tid = P.dma("sync", idb[:, :], w["ident_bf"], sem=P.dsems(1)[0], waits=(two,))
        sc_l = P.dsems(1)[0]
        tc1 = P.dma("sync", selo[:, :], w["sel_o"], sem=sc_l, waits=(two,))
        tc2 = P.dma("sync", selt[:, :], w["sel_t"], sem=sc_l)
        tc3 = P.dma("sync", rot[:, :], w["rot_m"], sem=sc_l)
        tc4 = P.dma("sync", msk[:, :], w["band_masks"], sem=sc_l)
    ctoks = [twq, two, tqs, tks, tid, tc1, tc2, tc3, tc4]

    CW = TC + 1
    xb = [P.sbuf(f"{name}_x", [128, KT * CW], F32)] * 2
    xbf = P.sbuf(f"{name}_xbf", [128, KT * TC], BF16)
    rb = [P.sbuf(f"{name}_r", [128, KT * TC], F32)] * 2
    qf = P.sbuf(f"{name}_qf", [128, KT * TC], BF16)
    kbuf = P.sbuf(f"{name}_kb", [128, KT * 768], BF16)
    vbuf = P.sbuf(f"{name}_vb", [128, 6 * D], BF16)
    rope = P.sbuf(f"{name}_rp", [128, 4 * TC], F32)
    scr = P.sbuf(f"{name}_s1", [128, TC], F32)
    scr2 = P.sbuf(f"{name}_s2", [128, TC], F32)
    qa = P.sbuf(f"{name}_qa", [128, TC], BF16)
    ssn = P.sbuf(f"{name}_ssn", [16, TC], F32)
    ssnb = P.sbuf(f"{name}_ssnb", [16, TC], BF16)
    scrb = P.sbuf(f"{name}_scrb", [128, TC], BF16)
    smm = P.sbuf(f"{name}_smm", [128, 768], F32)
    pex = P.sbuf(f"{name}_pex", [128, 768], BF16)
    nmx = P.sbuf(f"{name}_nmx", [128, 4], F32)
    ptr = P.sbuf(f"{name}_ptr", [128, 3 * 128], BF16)
    afh = P.sbuf(f"{name}_af", [64, 16 * TC], BF16)
    psP = [P.psum(f"{name}_pP", [128, TC], F32) for _ in range(2)]
    psQ0 = P.psum(f"{name}_pQ0", [128, TC], F32)
    psQ1 = P.psum(f"{name}_pQ1", [128, TC], F32)
    psQ2 = P.psum(f"{name}_pQ2", [128, TC], F32)
    ps_tr = P.psum(f"{name}_ptp", [128, 128], BF16)
    sx = P.dsems(2)
    sr = P.dsems(2)
    srp = P.dsems(2)
    sst = P.dsems(2)
    zv = P.op("vector", lambda v: v.memset(kbuf[:, :], 0.0), inc=True)
    zv2 = P.op("vector", lambda v: v.memset(vbuf[:, :], 0.0), inc=True)
    free_x = [(zv, zv2), (zv, zv2)]
    free_r = [(), ()]
    free_rp = [(), ()]
    psf_free = [(), ()]
    q0_free = ()
    q1_free = ()
    q2_free = ()
    tr_free = ()
    smm_free = ()
    pex_free = ()
    nmx_free = ()
    ptr_free = ()
    af_free = ()
    qa_free = ()
    scr_free = ()
    scr2_free = ()
    scrb_free = ()
    ssnb_free = ()
    xbf_free = ()
    last = None
    for c in range(nch):
        t0, tw = c * TC, min(TC, Ttot - c * TC)
        pp = c % 2
        nts = tw // 128
        if c > 0:
            rk = P.op("vector", lambda v: v.tensor_copy(
                kbuf[:, :].rearrange("p (j w) -> p j w", w=768)[:, :, 0:256],
                kbuf[:, :].rearrange("p (j w) -> p j w", w=768)[:, :, 512:768]),
                waits=list(q0_free) + list(q1_free) + prev_qk, inc=True)
            rv = P.op("vector", lambda v: v.tensor_copy(
                vbuf[:, 0:2 * D], vbuf[:, 4 * D:6 * D]),
                waits=list(af_free) + prev_pv, inc=True)
            halo_toks = [rk, rv]
        else:
            halo_toks = [zv, zv2]
        lds = [P.dma("sync", xb[pp][:, j * CW:j * CW + tw],
                     xs_ap[j * 128:(j + 1) * 128, t0:t0 + tw],
                     waits=free_x[pp], sem=sx[pp]) for j in range(KT)]
        rl = [P.dma("gpsimd", rb[pp][:, j * TC:j * TC + tw],
                    x_ap[j * 128:(j + 1) * 128, t0:t0 + tw],
                    waits=free_r[pp], sem=sr[pp]) for j in range(KT)]
        import os as _os
        if "lrp" in _os.environ.get("L3FA_SKIP", ""):
            lrp = []
        else:
            lrp = [P.dma("scalar", rope[:, i * TC:i * TC + tw],
                         w["rope4"][i, 0:128, t0:t0 + tw],
                         waits=free_rp[pp], sem=srp[pp]) for i in range(4)]
        cst = [P.op("vector", lambda v, j=j, pp=pp, tw=tw: v.tensor_copy(
            xbf[:, j * TC:j * TC + tw], xb[pp][:, j * CW:j * CW + tw]),
            waits=lds + list(xbf_free), inc=True) for j in range(KT)]
        qk_toks = []
        if "qsk" in _os.environ.get("L3FA_SKIP", ""):
            qk_toks = [P.op("vector", lambda v: v.memset(qf[:, 0:KT * TC], 0.0),
                           waits=cst + list(q0_free), inc=True)]
        for which in () if "qsk" in _os.environ.get("L3FA_SKIP", "") else (0, 1):
            ss_last = None
            raw_toks = []
            for o in range(KT):
                pb = psP[o % 2]
                mm = None
                for j in range(KT):
                    mm = P.op("tensor", lambda t, o=o, j=j, pb=pb, tw=tw, which=which: t.matmul(
                        pb[:, 0:tw],
                        wqkv[:, j * 3 * D + which * D + o * 128:j * 3 * D + which * D + o * 128 + 128],
                        xbf[:, j * TC:j * TC + tw],
                        start=(j == 0), stop=(j == KT - 1)),
                        waits=(ctoks + cst + list(psf_free[o % 2])) if j == 0 else [],
                        inc=(j == KT - 1))
                import os as _os
                _noss = "ssk" in _os.environ.get("L3FA_SKIP", "")
                tgt = (qf[:, o * TC:o * TC + tw] if which == 0
                       else kbuf[:, o * 768 + 256:o * 768 + 256 + tw])
                st_ = P.op("vector", lambda v, pb=pb, tgt=tgt: v.tensor_copy(
                    tgt, pb[:, 0:tgt.shape[-1]]),
                    waits=[mm] + halo_toks + list(qa_free), inc=True)
                e1 = P.op("scalar", lambda s, tgt=tgt, tw=tw: s.activation(
                    scrb[:, 0:tw], tgt, AF.Square),
                    waits=[st_] + list(scrb_free), inc=True)
                raw_toks.append(st_)
                psf_free[o % 2] = (e1, st_)
                if _noss:
                    ss_last = e1
                    scrb_free = (e1,)
                else:
                    ss_last = P.op("tensor", lambda t, o=o, tw=tw: t.matmul(
                        psQ2[0:16, 0:tw], selo[:, o * 16:(o + 1) * 16], scrb[:, 0:tw],
                        start=(o == 0), stop=(o == KT - 1)),
                        waits=([e1] + (list(q2_free) if o == 0 else [])),
                        inc=(o == KT - 1))
                    scrb_free = (ss_last,)
            if _noss:
                qk_toks += raw_toks
                continue
            n1 = P.op("vector", lambda v, tw=tw: v.tensor_copy(
                ssn[:, 0:tw], psQ2[0:16, 0:tw]), waits=[ss_last], inc=True)
            q2_free = (n1,)
            n2 = P.op("scalar", lambda s, tw=tw: s.activation(
                ssn[:, 0:tw], ssn[:, 0:tw], AF.Sqrt), waits=[n1], inc=True)
            n3 = P.op("vector", lambda v, tw=tw: v.tensor_scalar_max(
                ssn[:, 0:tw], ssn[:, 0:tw], 1e-12), waits=[n2], inc=True)
            n4 = P.op("vector", lambda v, tw=tw: v.reciprocal(
                ssn[:, 0:tw], ssn[:, 0:tw]), waits=[n3], inc=True)
            n5 = P.op("vector", lambda v, tw=tw: v.tensor_copy(
                ssnb[:, 0:tw], ssn[:, 0:tw]), waits=[n4] + list(ssnb_free), inc=True)
            import os as _os
            if "rope" in _os.environ.get("L3FA_SKIP", ""):
                qk_toks += [n5] + raw_toks
                continue
            for o in range(KT):
                tgt = (qf[:, o * TC:o * TC + tw] if which == 0
                       else kbuf[:, o * 768 + 256:o * 768 + 256 + tw])
                bmx = P.op("tensor", lambda t, o=o, tw=tw: t.matmul(
                    psQ0[:, 0:tw], selt[:, o * 128:(o + 1) * 128], ssnb[:, 0:tw],
                    start=True, stop=True), waits=[n5] + list(q0_free), inc=True)
                u1 = P.op("vector", lambda v, tw=tw, tgt=tgt: v.tensor_tensor(
                    scr[:, 0:tw], tgt, psQ0[:, 0:tw], ALU.mult),
                    waits=[bmx, raw_toks[o]] + list(scr_free), inc=True)
                q0_free = (u1,)
                sccol = qs_sb if which == 0 else ks_sb
                u2 = P.op("vector", lambda v, o=o, tw=tw, sccol=sccol: v.tensor_scalar_mul(
                    qa[:, 0:tw], scr[:, 0:tw], sccol[:, o:o + 1]),
                    waits=[u1, tqs, tks] + list(qa_free), inc=True)
                rtm = P.op("tensor", lambda t, tw=tw: t.matmul(
                    psQ1[:, 0:tw], rot[:, :], qa[:, 0:tw], start=True, stop=True),
                    waits=[u2] + list(q1_free), inc=True)
                ci = 0 if which == 0 else 2
                u3 = P.op("vector", lambda v, tw=tw, ci=ci: v.tensor_tensor(
                    scr[:, 0:tw], qa[:, 0:tw], rope[:, ci * TC:ci * TC + tw], ALU.mult),
                    waits=[rtm] + lrp + [u1], inc=True)
                u4 = P.op("vector", lambda v, tw=tw, ci=ci: v.tensor_tensor(
                    scr2[:, 0:tw], psQ1[:, 0:tw], rope[:, (ci + 1) * TC:(ci + 1) * TC + tw],
                    ALU.mult), waits=[rtm] + lrp + list(scr2_free), inc=True)
                u5 = P.op("vector", lambda v, tw=tw, tgt=tgt: v.tensor_add(
                    tgt, scr[:, 0:tw], scr2[:, 0:tw]), waits=[u3, u4], inc=True)
                q1_free = (u4,)
                scr_free = (u5,)
                scr2_free = (u5,)
                qa_free = (u3, rtm)
                ssnb_free = (bmx,)
                qk_toks.append(u5)
        prev_qk = qk_toks
        v_toks = []
        _nov = "vsk" in _os.environ.get("L3FA_SKIP", "")
        for ts_ in range(0 if _nov else nts):
            for nh in range(2):
                mm = None
                for j in range(KT):
                    mm = P.op("tensor", lambda t, ts_=ts_, nh=nh, j=j: t.matmul(
                        psQ0[:, 0:512],
                        xbf[:, j * TC + ts_ * 128:j * TC + ts_ * 128 + 128],
                        wqkv[:, j * 3 * D + 2 * D + nh * 512:j * 3 * D + 2 * D + nh * 512 + 512],
                        start=(j == 0), stop=(j == KT - 1)),
                        waits=(cst + list(q0_free)) if j == 0 else [],
                        inc=(j == KT - 1))
                ev = P.op("scalar", lambda s, ts_=ts_, nh=nh: s.activation(
                    vbuf[:, (2 + ts_) * D + nh * 512:(2 + ts_) * D + nh * 512 + 512],
                    psQ0[:, 0:512], AF.Identity),
                    waits=[mm] + halo_toks + list(af_free), inc=True)
                q0_free = (ev,)
                v_toks.append(ev)
        if _nov:
            v_toks = [P.op("vector", lambda v: v.memset(vbuf[:, 0:D], 0.0),
                           waits=list(af_free) + halo_toks, inc=True)]
        xbf_free = tuple(v_toks) + tuple(qk_toks)
        free_x[0] = free_x[1] = tuple(cst)
        free_rp[pp] = tuple(qk_toks)
        import os as _os
        _skip_attn = "attn" in _os.environ.get("L3FA_SKIP", "")
        af_toks = []
        prev_pv = []
        if _skip_attn:
            za = P.op("vector", lambda v: v.memset(afh[0:64, :], 0.0),
                      waits=list(af_free) + qk_toks + v_toks, inc=True)
            af_toks = [za]
            prev_pv = [za]
        for ts_ in range(0 if _skip_attn else nts):
            gq = c * 4 + ts_
            mvar = 0 if gq == 0 else (1 if gq == 1 else 2)
            for hp in range(8):
                sims = []
                for hh in range(2):
                    h = 2 * hp + hh
                    kt_, bs = h // 2, 64 * (h % 2)
                    pq = psQ0 if hh == 0 else psQ1
                    smm_ = P.op("tensor", lambda t, kt_=kt_, bs=bs, ts_=ts_, pq=pq: t.matmul(
                        pq[:, 0:384],
                        qf[bs:bs + 64, kt_ * TC + ts_ * 128:kt_ * TC + ts_ * 128 + 128],
                        kbuf[bs:bs + 64, kt_ * 768 + ts_ * 128:kt_ * 768 + ts_ * 128 + 384],
                        start=True, stop=True),
                        waits=qk_toks + (list(q0_free) if hh == 0 else list(q1_free)),
                        inc=True)
                    sims.append(smm_)
                mks = []
                for hh in range(2):
                    pq = psQ0 if hh == 0 else psQ1
                    mks.append(P.op("vector", lambda v, hh=hh, mvar=mvar, pq=pq: v.scalar_tensor_tensor(
                        smm[:, hh * 384:hh * 384 + 384], pq[:, 0:384],
                        8.0, msk[:, mvar * 384:mvar * 384 + 384], ALU.mult, ALU.add),
                        waits=sims + list(smm_free), inc=True))
                q0_free = (mks[0],)
                q1_free = (mks[1],)
                r1 = P.op("vector", lambda v: v.tensor_reduce(
                    nmx[:, 0:2].unsqueeze(2), smm[:, 0:768].rearrange("p (a b) -> p a b", b=384),
                    AX.X, ALU.max, negate=True), waits=mks + list(nmx_free), inc=True)
                exps = []
                for hh in range(2):
                    exps.append(P.op("scalar", lambda s, hh=hh: s.activation(
                        pex[:, hh * 384:hh * 384 + 384], smm[:, hh * 384:hh * 384 + 384],
                        AF.Exp, bias=nmx[:, hh:hh + 1], accum_out=nmx[:, 2 + hh:3 + hh]),
                        waits=[r1] + list(pex_free), inc=True))
                smm_free = tuple(exps)
                r2 = P.op("vector", lambda v: v.reciprocal(
                    nmx[:, 2:4], nmx[:, 2:4]), waits=exps, inc=True)
                nmx_free = (r2,)
                sc1 = []
                for hh in range(2):
                    sc1.append(P.op("vector", lambda v, hh=hh: v.tensor_scalar_mul(
                        pex[:, hh * 384:hh * 384 + 384], pex[:, hh * 384:hh * 384 + 384],
                        nmx[:, 2 + hh:3 + hh]), waits=[r2], inc=True))
                pex_free = tuple(sc1)
                for hh in range(2):
                    h = 2 * hp + hh
                    pv_bank = psQ2 if hh == 0 else psP[0]
                    pv_bank_free = q2_free if hh == 0 else psf_free[0]
                    pvmm = None
                    for b3 in range(3):
                        tr1 = P.op("tensor", lambda t, hh=hh, b3=b3: t.transpose(
                            ps_tr[:, :], pex[:, hh * 384 + b3 * 128:hh * 384 + b3 * 128 + 128],
                            idb[:, :]), waits=[sc1[hh]] + list(tr_free), inc=True)
                        tr2 = P.op("scalar", lambda s, b3=b3: s.copy(
                            ptr[:, b3 * 128:b3 * 128 + 128], ps_tr[:, :]),
                            waits=[tr1] + list(ptr_free), inc=True)
                        tr_free = (tr2,)
                        pvmm = P.op("tensor", lambda t, h=h, b3=b3, ts_=ts_, pv_bank=pv_bank: t.matmul(
                            pv_bank[0:64, 0:128],
                            vbuf[:, (ts_ + b3) * D + h * 64:(ts_ + b3) * D + h * 64 + 64],
                            ptr[:, b3 * 128:b3 * 128 + 128],
                            start=(b3 == 0), stop=(b3 == 2)),
                            waits=([tr2] + v_toks + (list(pv_bank_free) if b3 == 0 else [])),
                            inc=(b3 == 2))
                    ptr_free = (pvmm,)
                    prev_pv.append(pvmm)
                    ev = P.op("scalar", lambda s, h=h, ts_=ts_, pv_bank=pv_bank: s.activation(
                        afh[0:64, h * TC + ts_ * 128:h * TC + ts_ * 128 + 128],
                        pv_bank[0:64, 0:128], AF.Identity),
                        waits=[pvmm] + list(af_free), inc=True)
                    if hh == 0:
                        q2_free = (ev,)
                    else:
                        psf_free[0] = (ev,)
                    af_toks.append(ev)
        otoks = []
        for o in range(KT):
            pb = psP[o % 2]
            mm = None
            for k16 in range(16):
                mm = P.op("tensor", lambda t, o=o, k16=k16, pb=pb, tw=tw: t.matmul(
                    pb[:, 0:tw], wo_sb[0:64, k16 * D + o * 128:k16 * D + o * 128 + 128],
                    afh[0:64, k16 * TC:k16 * TC + tw],
                    start=(k16 == 0), stop=(k16 == 15)),
                    waits=(af_toks + list(psf_free[o % 2])) if k16 == 0 else [],
                    inc=(k16 == 15))
            e1 = P.op("vector", lambda v, o=o, pp=pp, pb=pb, tw=tw: v.tensor_add(
                rb[pp][:, o * TC:o * TC + tw], rb[pp][:, o * TC:o * TC + tw],
                pb[:, 0:tw]), waits=[mm] + rl, inc=True)
            psf_free[o % 2] = (e1,)
            st = P.dma("scalar", xout_ap[o * 128:(o + 1) * 128, t0:t0 + tw],
                       rb[pp][:, o * TC:o * TC + tw], waits=[e1], sem=sst[pp])
            otoks.append(e1)
            last = st
        af_free = tuple(otoks)
        free_r[0] = free_r[1] = tuple(otoks) + (last,)
    return last


def emit_l3f_ffn(P, ctx, xs_ap, x_ap, xout_ap, Ttot, w, li, name="l3ff"):
    lnf = P.dram(f"{name}_ln", [D, Ttot], BF16)
    hg = P.dram(f"{name}_hg", [2816, Ttot], BF16)
    with P.scope():
        g1, tg1 = load_vec_cols(P, w["ln_g"][li], D, f"{name}g")
        b1, tb1 = load_vec_cols(P, w["ln_b"][li], D, f"{name}b")
        emit_ln(P, ctx, xs_ap, lnf, Ttot, out_dtype=BF16, gamma=g1, beta=b1,
                eps=1e-5, wait_x=(tg1, tb1), name=f"{name}l")
    P.barrier()
    with P.scope():
        w1, tw1 = load_weight_bf16(P, w["W1p"][li], D, 5632, f"{name}w1")
        b1p, tb1p = load_vec_cols(P, w["b1p"][li], 5632, f"{name}b1")
        lnb = [P.sbuf(f"{name}_x", [128, KT * TC], BF16) for _ in range(2)]
        asb = P.sbuf(f"{name}_a", [128, TC], F32)
        gsb = P.sbuf(f"{name}_gg", [128, TC], F32)
        hb = [P.sbuf(f"{name}_h", [128, 22 * TC], BF16) for _ in range(2)]
        ps = [P.psum(f"{name}_ps", [128, TC], F32) for _ in range(2)]
        sx = P.dsems(2)
        sh = P.dsems(2)
        free_x = [(), ()]
        free_h = [(), ()]
        ps_free = [(), ()]
        a_free = ()
        g_free = ()
        last1 = None
        for c in range(cdiv(Ttot, TC)):
            t0, tw = c * TC, min(TC, Ttot - c * TC)
            pp = c % 2
            lds = [P.dma("sync", lnb[pp][:, j * TC:j * TC + tw],
                         lnf[j * 128:(j + 1) * 128, t0:t0 + tw],
                         waits=free_x[pp], sem=sx[pp]) for j in range(KT)]
            htoks = []
            for o in range(22):
                pb = ps[0]
                mm = None
                for j in range(KT):
                    mm = P.op("tensor", lambda t, o=o, j=j, pb=pb, pp=pp, tw=tw: t.matmul(
                        pb[:, 0:tw], w1[:, j * 5632 + o * 128:j * 5632 + o * 128 + 128],
                        lnb[pp][:, j * TC:j * TC + tw],
                        start=(j == 0), stop=(j == KT - 1)),
                        waits=([tw1] + lds + list(ps_free[0])) if j == 0 else [],
                        inc=(j == KT - 1))
                ea = P.op("scalar", lambda s, o=o, pb=pb, tw=tw: s.activation(
                    asb[:, 0:tw], pb[:, 0:tw], AF.Identity,
                    bias=b1p[:, o:o + 1]),
                    waits=[mm, tb1p] + list(a_free), inc=True)
                ps_free[0] = (ea,)
                pb = ps[1]
                mm = None
                for j in range(KT):
                    mm = P.op("tensor", lambda t, o=o, j=j, pb=pb, pp=pp, tw=tw: t.matmul(
                        pb[:, 0:tw], w1[:, j * 5632 + (22 + o) * 128:j * 5632 + (22 + o) * 128 + 128],
                        lnb[pp][:, j * TC:j * TC + tw],
                        start=(j == 0), stop=(j == KT - 1)),
                        waits=list(ps_free[1]) if j == 0 else [],
                        inc=(j == KT - 1))
                eg = P.op("scalar", lambda s, o=o, pb=pb, tw=tw: s.activation(
                    gsb[:, 0:tw], pb[:, 0:tw], AF.Gelu,
                    bias=b1p[:, 22 + o:23 + o]),
                    waits=[mm] + list(g_free), inc=True)
                ps_free[1] = (eg,)
                hm = P.op("vector", lambda v, o=o, pp=pp, tw=tw: v.tensor_mul(
                    hb[pp][:, o * TC:o * TC + tw], asb[:, 0:tw], gsb[:, 0:tw]),
                    waits=[ea, eg] + list(free_h[pp]), inc=True)
                a_free = (hm,)
                g_free = (hm,)
                st = P.dma("scalar", hg[o * 128:(o + 1) * 128, t0:t0 + tw],
                           hb[pp][:, o * TC:o * TC + tw], waits=[hm], sem=sh[pp])
                htoks.append(hm)
                last1 = st
            free_x[pp] = tuple(htoks)
            free_h[pp] = (last1,)
    P.barrier()
    with P.scope():
        b2, tb2 = load_vec_cols(P, w["b2"][li], D, f"{name}b2")
        t = emit_linear_fm(P, ctx, hg, Ttot, 2816, D, w["W2p"][li], f"{name}w2",
                           bias=(b2, tb2), resid=(x_ap, xout_ap, 1.0))
    P.barrier()
    return t


def emit_l3_ffn_block(P, ctx, ln_ap, xold_ap, xnew_ap, Ttot, w, name="l3f"):
    xa = P.dram(f"{name}_xa", [D, Ttot], F32)
    xb_ = P.dram(f"{name}_xb", [D, Ttot], F32)
    xsh = P.dram(f"{name}_xs", [D, Ttot], F32)
    cur, nxt = ln_ap, xa
    t = None
    for li in range(6):
        with P.scope():
            ts_ = emit_shift(P, ctx, cur, xsh, Ttot, name=f"{name}s{li}a")
        P.barrier()
        with P.scope():
            t = emit_l3f_attn(P, ctx, xsh, cur, nxt, Ttot, w, li, name=f"{name}a{li}")
        P.barrier()
        cur, nxt = nxt, (xb_ if nxt is xa else xa)
        with P.scope():
            ts_ = emit_shift(P, ctx, cur, xsh, Ttot, name=f"{name}s{li}b")
        P.barrier()
        with P.scope():
            t = emit_l3f_ffn(P, ctx, xsh, cur, nxt, Ttot, w, li, name=f"{name}f{li}")
        P.barrier()
        cur, nxt = nxt, (xb_ if nxt is xa else xa)
    with P.scope():
        t = emit_add3(P, ctx, xold_ap, None, cur, xnew_ap, Ttot,
                      name=f"{name}fin", alpha_c=0.5) if False else None
        t = emit_axpy(P, ctx, xold_ap, cur, xnew_ap, Ttot, 0.5, name=f"{name}fin")
    P.barrier()
    return t


def emit_axpy(P, ctx, a_ap, c_ap, out_ap, Ttot, alpha, name="axpy"):
    ab = [P.sbuf(f"{name}_a", [128, KT * TC], F32) for _ in range(2)]
    cb = [P.sbuf(f"{name}_c", [128, KT * TC], F32) for _ in range(2)]
    sa = P.dsems(2)
    sc_ = P.dsems(2)
    so = P.dsems(1)[0]
    free = [(), ()]
    last = None
    for c in range(cdiv(Ttot, TC)):
        t0, tw = c * TC, min(TC, Ttot - c * TC)
        pp = c % 2
        la = [P.dma("sync", ab[pp][:, j * TC:j * TC + tw],
                    a_ap[j * 128:(j + 1) * 128, t0:t0 + tw], waits=free[pp], sem=sa[pp])
              for j in range(KT)]
        lc = [P.dma("gpsimd", cb[pp][:, j * TC:j * TC + tw],
                    c_ap[j * 128:(j + 1) * 128, t0:t0 + tw], waits=free[pp], sem=sc_[pp])
              for j in range(KT)]
        toks = []
        for j in range(KT):
            t1 = P.op("vector", lambda v, j=j, pp=pp, tw=tw: v.scalar_tensor_tensor(
                cb[pp][:, j * TC:j * TC + tw], cb[pp][:, j * TC:j * TC + tw], alpha,
                ab[pp][:, j * TC:j * TC + tw], ALU.mult, ALU.add),
                waits=la + lc, inc=True)
            st = P.dma("scalar", out_ap[j * 128:(j + 1) * 128, t0:t0 + tw],
                       cb[pp][:, j * TC:j * TC + tw], waits=[t1], sem=so)
            toks.append(t1)
            last = st
        free[pp] = tuple(toks) + (last,)
    return last


def emit_cast_w(P, ctx, pairs, name="cw"):
    stg = [P.sbuf(f"{name}_s", [128, 4096], F32) for _ in range(2)]
    ob = [P.sbuf(f"{name}_o", [128, 4096], BF16) for _ in range(2)]
    sl = P.dsems(2)
    so = P.dsems(2)
    free = [(), ()]
    ofree = [(), ()]
    last = None
    i = 0
    for (srcap, dstap, R, C) in pairs:
        assert R % 128 == 0
        for r in range(R // 128):
            for c0 in range(0, C, 4096):
                cw_ = min(4096, C - c0)
                pi = i % 2
                ld = P.dma("sync", stg[pi][:, 0:cw_],
                           srcap[r * 128:(r + 1) * 128, c0:c0 + cw_],
                           waits=free[pi], sem=sl[pi])
                ct = P.op("vector", lambda v, pi=pi, cw_=cw_: v.tensor_copy(
                    ob[pi][:, 0:cw_], stg[pi][:, 0:cw_]),
                    waits=[ld] + list(ofree[pi]), inc=True)
                free[pi] = (ct,)
                st = P.dma("scalar", dstap[r * 128:(r + 1) * 128, c0:c0 + cw_],
                           ob[pi][:, 0:cw_], waits=[ct], sem=so[pi])
                ofree[pi] = (st,)
                last = st
                i += 1
    return last


def emit_axpy_slab(P, ctx, a_ap, c3d_ap, out_ap, Ttot, alpha, name="axps"):
    ab = [P.sbuf(f"{name}_a", [128, KT * TC], F32) for _ in range(2)]
    cb = [P.sbuf(f"{name}_c", [128, KT * TC], F32) for _ in range(2)]
    sa = P.dsems(2)
    sc_ = P.dsems(2)
    so = P.dsems(1)[0]
    free = [(), ()]
    last = None
    for c in range(cdiv(Ttot, TC)):
        t0, tw = c * TC, min(TC, Ttot - c * TC)
        pp = c % 2
        la = [P.dma("sync", ab[pp][:, j * TC:j * TC + tw],
                    a_ap[j * 128:(j + 1) * 128, t0:t0 + tw], waits=free[pp], sem=sa[pp])
              for j in range(KT)]
        lc = [P.dma("gpsimd", cb[pp][:, j * TC:j * TC + tw],
                    c3d_ap[c, j * 128:(j + 1) * 128, 0:tw], waits=free[pp], sem=sc_[pp])
              for j in range(KT)]
        toks = []
        for j in range(KT):
            t1 = P.op("vector", lambda v, j=j, pp=pp, tw=tw: v.scalar_tensor_tensor(
                cb[pp][:, j * TC:j * TC + tw], cb[pp][:, j * TC:j * TC + tw], alpha,
                ab[pp][:, j * TC:j * TC + tw], ALU.mult, ALU.add),
                waits=la + lc, inc=True)
            st = P.dma("scalar", out_ap[j * 128:(j + 1) * 128, t0:t0 + tw],
                       cb[pp][:, j * TC:j * TC + tw], waits=[t1], sem=so)
            toks.append(t1)
            last = st
        free[pp] = tuple(toks) + (last,)
    return last



def host_prep(I):
    import numpy as _np
    H = {}
    ct = lambda a: _np.ascontiguousarray(a, dtype=_np.float32)
    T_ = lambda a: _np.ascontiguousarray(_np.asarray(a).T, dtype=_np.float32)
    t = _np.arange(1024)
    ang = -2 * _np.pi * _np.outer(t, t) / 1024.0
    F = _np.exp(1j * ang)
    G = _np.exp(-1j * ang) / 1024.0
    H["F_r"] = _np.real(F).astype(ml_dtypes.bfloat16)
    H["F_i"] = _np.imag(F).astype(ml_dtypes.bfloat16)
    H["F_i_neg"] = (-_np.imag(F)).astype(ml_dtypes.bfloat16)
    H["G_r"] = _np.real(G).astype(ml_dtypes.bfloat16)
    H["G_i_neg"] = (-_np.imag(G)).astype(ml_dtypes.bfloat16)
    H["l0f_wr"] = ct(I["L0_ffn_w_freq_real"])
    H["l0f_wi"] = ct(I["L0_ffn_w_freq_imag"])
    H["l0f_bias"] = ct(I["L0_ffn_bias"])
    H["rms1_w"] = ct(I["L0_attn_rms1_w"])
    H["rms2_w"] = ct(I["L0_attn_rms2_w"])
    H["inprojT"] = T_(I["L0_attn_in_proj_w"])
    cw = _np.asarray(I["L0_attn_conv_w"])
    cw4 = _np.zeros(4 * DI, _np.float32)
    for k in range(4):
        cw4[k * DI:(k + 1) * DI] = cw[:, k]
    H["convw4"] = cw4
    H["conv_b"] = ct(I["L0_attn_conv_b"])
    xp = _np.asarray(I["L0_attn_x_proj_w"])
    xpT = _np.zeros((DI, 128), _np.float32)
    xpT[:, 0:16] = xp[64:80].T
    xpT[:, 32:48] = xp[80:96].T
    xpT[:, 64:128] = xp[0:64].T
    H["xpT"] = xpT
    H["dtwT"] = T_(I["L0_attn_dt_proj_w"])
    H["dtb"] = ct(I["L0_attn_dt_proj_b"])
    al = _np.asarray(I["L0_attn_A_log"])
    p = _np.arange(128)
    jj = _np.arange(256)
    H["A_scan"] = al[8 * jj[None, :] + (p[:, None] // 16), p[:, None] % 16].astype(_np.float32)
    Rm = _np.zeros((128, 2048), ml_dtypes.bfloat16)
    Ym = _np.zeros((128, 2048), ml_dtypes.bfloat16)
    for m in range(16):
        for pp_ in range(128):
            k = 8 * m + pp_ // 16
            Rm[k, m * 128 + pp_] = 1
            Ym[pp_, m * 128 + k] = 1
    H["Rm"] = Rm
    H["Ym"] = Ym
    EB = _np.zeros((16, 128), ml_dtypes.bfloat16)
    for pp_ in range(128):
        EB[pp_ % 16, pp_] = 1
    H["EB"] = EB
    H["Dskip"] = ct(I["L0_attn_D_skip"])
    H["outprojT"] = T_(I["L0_attn_out_proj_w"])
    H["fc1T"] = T_(I["L0_attn_fc1_w"])
    H["fc1_b"] = ct(I["L0_attn_fc1_b"])
    H["fc2T"] = T_(I["L0_attn_fc2_w"])
    H["fc2_b"] = ct(I["L0_attn_fc2_b"])
    H["fc3T"] = T_(I["L0_attn_fc3_w"])
    H["fc3_b"] = ct(I["L0_attn_fc3_b"])
    H["w_gate"] = ct(I["L1_ffn_w_gate"])
    H["w_up"] = ct(I["L1_ffn_w_up"])
    H["w_down"] = ct(I["L1_ffn_w_down"])
    H["ident_bf"] = _np.eye(128, dtype=ml_dtypes.bfloat16)
    for nm in ("ln_conv_g", "ln_conv_b", "ln_attn_g", "ln_attn_b",
               "ln_boom_g", "ln_boom_b"):
        H[nm] = ct(I["L1_attn_" + nm])
    cw1 = _np.asarray(I["L1_attn_conv_w"])
    wc4 = _np.zeros((4, 256, 1024), _np.float32)
    for k in range(4):
        wc4[k] = cw1[:, :, k].T
    H["wconv4"] = wc4
    H["l1_conv_b"] = ct(I["L1_attn_conv_b"])
    H["Wqkv3T"] = _np.ascontiguousarray(_np.concatenate(
        [I["L1_attn_Wq"], I["L1_attn_Wk"], I["L1_attn_Wv"]], 0).T.astype(_np.float32))
    H["bqkv3"] = _np.concatenate(
        [I["L1_attn_bq"], I["L1_attn_bk"], I["L1_attn_bv"]]).astype(_np.float32)
    inv_scale = (1.0 / _np.arange(1, T + 1)).astype(_np.float32)
    H["inv_scale_b"] = _np.broadcast_to(inv_scale, (128, T)).copy()
    H["WbT"] = T_(I["L1_attn_Wb"])
    H["bb"] = ct(I["L1_attn_bb"])
    H["WuT"] = T_(I["L1_attn_Wu"])
    H["bu"] = ct(I["L1_attn_bu"])
    H["l2a_wqT"] = T_(I["L2_attn_Wq"])
    H["l2a_wkT"] = T_(I["L2_attn_Wk"])
    H["l2a_wvT"] = T_(I["L2_attn_Wv"])
    H["l2a_woT"] = T_(I["L2_attn_Wo"])
    H["l2a_bq"] = ct(I["L2_attn_bq"])
    H["l2a_bv"] = ct(I["L2_attn_bv"])
    H["l2a_bo"] = ct(I["L2_attn_bo"])
    H["WxT"] = _np.ascontiguousarray(_np.concatenate(
        [I["L2_ffn_wf"], I["L2_ffn_wi"], I["L2_ffn_wc"]], 0).T.astype(_np.float32))
    H["UT"] = _np.ascontiguousarray(_np.concatenate(
        [I["L2_ffn_uf"], I["L2_ffn_ui"], I["L2_ffn_uc"]], 0).T.astype(_np.float32))
    H["bfic"] = _np.concatenate(
        [I["L2_ffn_bf"], I["L2_ffn_bi"], I["L2_ffn_bc"]]).astype(_np.float32)
    H["l2f_woT"] = T_(I["L2_ffn_wo"])
    H["l2f_uoT"] = T_(I["L2_ffn_uo"])
    H["l2f_bo"] = ct(I["L2_ffn_bo"])
    H["in_c"] = ct(I["L2_ffn_in_c"])
    H["l3a_wq"] = ct(I["L3_attn_wq"])
    H["l3a_wk"] = ct(I["L3_attn_wk"])
    H["l3a_wv"] = ct(I["L3_attn_wv"])
    H["l3a_wo"] = ct(I["L3_attn_wo"])
    H["l3a_bq"] = ct(I["L3_attn_bq"])
    H["l3a_bk"] = ct(I["L3_attn_bk"])
    H["l3a_bv"] = ct(I["L3_attn_bv"])
    H["l3a_bo"] = ct(I["L3_attn_bo"])
    H["Wqkv6"] = ct(I["L3_ffn_Wqkv"])
    H["Wo6"] = ct(I["L3_ffn_Wo"])
    H["q_scale_x"] = _np.ascontiguousarray(_np.tile(I["L3_ffn_q_scale"], (1, 16)).astype(_np.float32))
    H["k_scale_x"] = _np.ascontiguousarray(_np.tile(I["L3_ffn_k_scale"], (1, 16)).astype(_np.float32))
    H["ln_g6"] = ct(I["L3_ffn_ln_g"])
    H["ln_b6"] = ct(I["L3_ffn_ln_b"])
    W1 = _np.asarray(I["L3_ffn_W1"])
    DIN = 2730
    W1p = _np.zeros((6, 1024, 5632), _np.float32)
    W1p[:, :, 0:DIN] = W1[:, :, 0:DIN]
    W1p[:, :, 2816:2816 + DIN] = W1[:, :, DIN:]
    H["W1p6"] = W1p
    b1 = _np.asarray(I["L3_ffn_b1"])
    b1p = _np.zeros((6, 5632), _np.float32)
    b1p[:, 0:DIN] = b1[:, 0:DIN]
    b1p[:, 2816:2816 + DIN] = b1[:, DIN:]
    H["b1p6"] = b1p
    W2p = _np.zeros((6, 2816, 1024), _np.float32)
    W2p[:, 0:DIN] = I["L3_ffn_W2"]
    H["W2p6"] = W2p
    H["b26"] = ct(I["L3_ffn_b2"])
    sel_o = _np.zeros((128, 8 * 16), ml_dtypes.bfloat16)
    sel_t = _np.zeros((16, 8 * 128), ml_dtypes.bfloat16)
    for o in range(8):
        for pp_ in range(128):
            c = pp_ // 64 + 2 * o
            sel_o[pp_, o * 16 + c] = 1
            sel_t[c, o * 128 + pp_] = 1
    H["sel_o"] = sel_o
    H["sel_t"] = sel_t
    rotm = _np.zeros((128, 128), ml_dtypes.bfloat16)
    for pp_ in range(128):
        if pp_ % 64 < 32:
            rotm[pp_ + 32, pp_] = -1
        else:
            rotm[pp_ - 32, pp_] = 1
    H["rot_m"] = rotm
    msk = _np.zeros((128, 3 * 384), _np.float32)
    NEG = -1e30
    pr = _np.arange(128)[:, None]
    fr = _np.arange(128)[None, :]
    tri_lo = _np.where(fr >= pr, 0.0, NEG)
    causal = _np.where(fr <= pr, 0.0, NEG)
    for g, m0 in enumerate([(NEG, NEG), (NEG, 0.0), (None, 0.0)]):
        base = g * 384
        msk[:, base:base + 128] = tri_lo if m0[0] is None else m0[0]
        msk[:, base + 128:base + 256] = m0[1]
        msk[:, base + 256:base + 384] = causal
    H["band_masks"] = msk
    DH, WIN = 64, 256
    half = _np.arange(0, DH, 2, dtype=_np.float64)
    inv_freq = 1.0 / 10000.0 ** (half / DH)
    tt = _np.arange(T, dtype=_np.float64)
    freqs = _np.concatenate([tt[:, None] * inv_freq[None, :]] * 2, -1)
    base_scale = (half + 0.4 * DH) / (1.4 * DH)
    power = (tt - T // 2) / (WIN // 2)
    scale = _np.concatenate([base_scale[None, :] ** power[:, None]] * 2, -1)
    cos, sin = _np.cos(freqs), _np.sin(freqs)
    tabs = [cos * scale, sin * scale, cos / scale, sin / scale]
    rope4 = _np.zeros((4, 128, T), _np.float32)
    for i, tb in enumerate(tabs):
        r = tb.T.astype(_np.float32)
        rope4[i, 0:64] = r
        rope4[i, 64:128] = r
    H["rope4"] = rope4
    return H


def _declare_inputs(P, H, names):
    out = {}
    for nm in names:
        arr = H[nm]
        dt = BF16 if arr.dtype == ml_dtypes.bfloat16 else F32
        out[nm] = P.dram(nm, list(arr.shape), dt, kind="ExternalInput")
    return out


A_NAMES = ["F_r", "F_i", "F_i_neg", "G_r", "G_i_neg", "l0f_wr", "l0f_wi",
           "l0f_bias", "rms1_w", "rms2_w", "inprojT", "convw4", "conv_b",
           "xpT", "dtwT", "dtb", "A_scan", "Rm", "Ym", "EB", "Dskip",
           "outprojT", "fc1T", "fc1_b", "fc2T", "fc2_b", "fc3T", "fc3_b",
           "w_gate", "w_up", "w_down", "ident_bf",
           "ln_conv_g", "ln_conv_b", "ln_attn_g", "ln_attn_b", "ln_boom_g",
           "ln_boom_b", "wconv4", "l1_conv_b", "Wqkv3T", "bqkv3",
           "inv_scale_b", "WbT", "bb", "WuT", "bu"]
B_NAMES = ["WxT", "UT", "bfic", "l2f_woT", "l2f_uoT", "l2f_bo", "in_c"]
C_NAMES = ["l2a_wqT", "l2a_wkT", "l2a_wvT", "l2a_woT", "l2a_bq", "l2a_bv", "l2a_bo"]
E_NAMES = ["l3a_wq", "l3a_wk", "l3a_wv", "l3a_wo", "l3a_bq", "l3a_bk", "l3a_bv",
           "l3a_bo", "Wqkv6", "Wo6", "q_scale_x", "k_scale_x", "ln_g6", "ln_b6",
           "W1p6", "b1p6", "W2p6", "b26", "ident_bf", "sel_o", "sel_t", "rot_m",
           "band_masks", "rope4"]


def build_launch_A():
    P = Prog("A")
    H = None
    w = _declare_inputs.__wrapped__ if False else None
    aps = {}
    x_in = P.dram("x", [D, T], F32, kind="ExternalInput")
    x2_out = P.dram("x2_out", [D, T], F32, kind="ExternalOutput")
    ln1_out = P.dram("ln1_out", [D, T], BF16, kind="ExternalOutput")
    import types
    Hshapes = globals().get("_HSHAPES")
    aps = {}
    for nm in A_NAMES:
        sh, isbf = Hshapes[nm]
        aps[nm] = P.dram(nm, list(sh), BF16 if isbf else F32, kind="ExternalInput")
    ctx = make_ctx(P)
    mmat = P.dram("Mmat", [D, D], BF16)
    xa = P.dram("xa", [D, T], F32)
    xb = P.dram("xb", [D, T], F32)
    xc = P.dram("xc", [D, T], F32)
    lnb = P.dram("lnb", [D, T], BF16)
    lnf = P.dram("lnf", [D, T], F32)
    wq = {"wqT_gate": P.dram("wqTg", [D, 2816], BF16),
          "wqT_up": P.dram("wqTu", [D, 2816], BF16),
          "wqT_down": P.dram("wqTd", [2816, D], BF16)}
    with P.scope():
        emit_compose_M(P, ctx, {k: aps[k] for k in
                                ("F_r", "F_i", "F_i_neg", "G_r", "G_i_neg")},
                       aps["l0f_wr"], aps["l0f_wi"], mmat)
    P.barrier()
    emit_l1_wprep(P, ctx, {"w_gate": aps["w_gate"], "w_up": aps["w_up"],
                           "w_down": aps["w_down"]}, wq, aps["ident_bf"])

    def l0f(xin, xout):
        with P.scope():
            emit_ln(P, ctx, xin, lnb, T, out_dtype=BF16, eps=1e-5, name="gl")
        P.barrier()
        with P.scope():
            bs, btok = load_vec_cols(P, aps["l0f_bias"], D, "b0f")
            b05 = P.sbuf("b05", [128, 8], F32)
            bt2 = P.op("vector", lambda v: v.tensor_scalar_mul(
                b05[:, 0:8], bs[:, 0:8], 0.5), waits=[btok], inc=True)
            emit_linear_fm(P, ctx, lnb, T, D, D, mmat, "l0f", w_dtype=BF16,
                           bias=(b05, bt2), act=AF.Identity, act_scale=0.5,
                           resid=(xin, xout, 1.0))
        P.barrier()

    w0 = {k: aps[k] for k in ("rms1_w", "rms2_w", "inprojT", "convw4", "conv_b",
                              "xpT", "dtwT", "dtb", "A_scan", "Rm", "Ym", "EB",
                              "Dskip", "outprojT", "fc1T", "fc1_b", "fc2T",
                              "fc2_b", "fc3T", "fc3_b")}
    w1a = {"ln_conv_g": aps["ln_conv_g"], "ln_conv_b": aps["ln_conv_b"],
           "ln_attn_g": aps["ln_attn_g"], "ln_attn_b": aps["ln_attn_b"],
           "ln_boom_g": aps["ln_boom_g"], "ln_boom_b": aps["ln_boom_b"],
           "wconv4": aps["wconv4"], "conv_b": aps["l1_conv_b"],
           "Wqkv3T": aps["Wqkv3T"], "bqkv3": aps["bqkv3"],
           "inv_scale_b": aps["inv_scale_b"], "WbT": aps["WbT"], "bb": aps["bb"],
           "WuT": aps["WuT"], "bu": aps["bu"]}
    l0f(x_in, xa)
    with P.scope():
        emit_ln(P, ctx, xa, lnf, T, out_dtype=F32, eps=1e-5, name="gl0a")
    P.barrier()
    emit_l0_attn(P, ctx, lnf, xa, xb, T, w0)
    l0f(xb, xc)
    with P.scope():
        emit_ln(P, ctx, xc, lnb, T, out_dtype=BF16, eps=1e-5, name="gl1f")
    P.barrier()
    with P.scope():
        emit_l1_ffn(P, ctx, lnb, xc, xa, T, wq, aps["ident_bf"], name="l1fa")
    P.barrier()
    with P.scope():
        emit_ln(P, ctx, xa, lnf, T, out_dtype=F32, eps=1e-5, name="gl1a")
    P.barrier()
    emit_l1_attn(P, ctx, lnf, xa, xb, T, w1a)
    with P.scope():
        emit_ln(P, ctx, xb, lnb, T, out_dtype=BF16, eps=1e-5, name="gl1f2")
    P.barrier()
    with P.scope():
        emit_l1_ffn(P, ctx, lnb, xb, x2_out, T, wq, aps["ident_bf"], name="l1fb")
    P.barrier()
    with P.scope():
        t = emit_ln(P, ctx, x2_out, ln1_out, T, out_dtype=BF16, eps=1e-5, name="gl2f")
    P.barrier()
    P.op("sync", lambda e: e.nop(), waits=[t])
    return P


def build_launch_B():
    P = Prog("B")
    Hshapes = globals().get("_HSHAPES")
    lnss = P.dram("lnss", [8, D, 512], BF16, kind="ExternalInput")
    hss = P.dram("hss", [8, D, 512], F32, kind="ExternalOutput")
    aps = {}
    for nm in B_NAMES:
        sh, isbf = Hshapes[nm]
        aps[nm] = P.dram(nm, list(sh), BF16 if isbf else F32, kind="ExternalInput")
    ctx = make_ctx(P)
    w = {"WxT": aps["WxT"], "UT": aps["UT"], "bfic": aps["bfic"],
         "woT": aps["l2f_woT"], "uoT": aps["l2f_uoT"], "bo": aps["l2f_bo"],
         "in_c": aps["in_c"]}
    t = emit_l2_ffn(P, ctx, lnss, hss, w)
    P.op("sync", lambda e: e.nop(), waits=[t])
    return P


def build_launch_C():
    P = Prog("C")
    Hshapes = globals().get("_HSHAPES")
    x_in = P.dram("x", [D, T], F32, kind="ExternalInput")
    h1 = P.dram("h1", [8, D, 512], F32, kind="ExternalInput")
    x3_out = P.dram("x3_out", [D, T], F32, kind="ExternalOutput")
    ln2_out = P.dram("ln2_out", [D, T], BF16, kind="ExternalOutput")
    aps = {}
    for nm in C_NAMES:
        sh, isbf = Hshapes[nm]
        aps[nm] = P.dram(nm, list(sh), BF16 if isbf else F32, kind="ExternalInput")
    ctx = make_ctx(P)
    xa = P.dram("xa", [D, T], F32)
    lnb = P.dram("lnb", [D, T], BF16)
    with P.scope():
        emit_axpy_slab(P, ctx, x_in, h1, xa, T, 0.5)
    P.barrier()
    with P.scope():
        emit_ln(P, ctx, xa, lnb, T, out_dtype=BF16, eps=1e-5, name="gl2a")
    P.barrier()
    w = {"wqT": aps["l2a_wqT"], "wkT": aps["l2a_wkT"], "wvT": aps["l2a_wvT"],
         "woT": aps["l2a_woT"], "bq": aps["l2a_bq"], "bv": aps["l2a_bv"],
         "bo": aps["l2a_bo"]}
    emit_l2_attn(P, ctx, lnb, xa, x3_out, T, w)
    with P.scope():
        t = emit_ln(P, ctx, x3_out, ln2_out, T, out_dtype=BF16, eps=1e-5, name="gl2f2")
    P.barrier()
    P.op("sync", lambda e: e.nop(), waits=[t])
    return P


def build_launch_E():
    P = Prog("E")
    Hshapes = globals().get("_HSHAPES")
    x_in = P.dram("x", [D, T], F32, kind="ExternalInput")
    h2 = P.dram("h2", [8, D, 512], F32, kind="ExternalInput")
    x_out = P.dram("x_out", [D, T], F32, kind="ExternalOutput")
    aps = {}
    for nm in E_NAMES:
        sh, isbf = Hshapes[nm]
        aps[nm] = P.dram(nm, list(sh), BF16 if isbf else F32, kind="ExternalInput")
    ctx = make_ctx(P)
    xa = P.dram("xa", [D, T], F32)
    xb = P.dram("xb", [D, T], F32)
    xc = P.dram("xc", [D, T], F32)
    lnf = P.dram("lnf", [D, T], F32)
    lnb = P.dram("lnb", [D, T], BF16)
    pairs = []
    wb16 = {}
    for nm, key, R, C in [("Wqkv", "Wqkv6", D, 3 * D), ("Wo", "Wo6", D, D),
                          ("W1p", "W1p6", D, 5632), ("W2p", "W2p6", 2816, D)]:
        dst = P.dram(nm + "_b", [6, R, C], BF16)
        for i in range(6):
            pairs.append((aps[key][i], dst[i], R, C))
        wb16[nm] = [dst[i] for i in range(6)]
    with P.scope():
        emit_cast_w(P, ctx, pairs)
    P.barrier()
    w3f = {"Wqkv": wb16["Wqkv"], "Wo": wb16["Wo"],
           "W1p": wb16["W1p"], "W2p": wb16["W2p"],
           "q_scale_x": [aps["q_scale_x"][i] for i in range(6)],
           "k_scale_x": [aps["k_scale_x"][i] for i in range(6)],
           "ln_g": [aps["ln_g6"][i] for i in range(6)],
           "ln_b": [aps["ln_b6"][i] for i in range(6)],
           "b1p": [aps["b1p6"][i] for i in range(6)],
           "b2": [aps["b26"][i] for i in range(6)],
           "ident_bf": aps["ident_bf"], "sel_o": aps["sel_o"],
           "sel_t": aps["sel_t"], "rot_m": aps["rot_m"],
           "band_masks": aps["band_masks"], "rope4": aps["rope4"]}
    w3a = {"wq": aps["l3a_wq"], "wk": aps["l3a_wk"], "wv": aps["l3a_wv"],
           "wo": aps["l3a_wo"], "bq": aps["l3a_bq"], "bk": aps["l3a_bk"],
           "bv": aps["l3a_bv"], "bo": aps["l3a_bo"]}
    with P.scope():
        emit_axpy_slab(P, ctx, x_in, h2, xa, T, 0.5)
    P.barrier()
    with P.scope():
        emit_ln(P, ctx, xa, lnf, T, out_dtype=F32, eps=1e-5, name="gl3f")
    P.barrier()
    emit_l3_ffn_block(P, ctx, lnf, xa, xb, T, w3f, name="l3fA")
    with P.scope():
        emit_ln(P, ctx, xb, lnb, T, out_dtype=BF16, eps=1e-5, name="gl3a")
    P.barrier()
    emit_l3_attn(P, ctx, lnb, xb, xc, T, w3a)
    with P.scope():
        emit_ln(P, ctx, xc, lnf, T, out_dtype=F32, eps=1e-5, name="gl3f2")
    P.barrier()
    t = emit_l3_ffn_block(P, ctx, lnf, xc, x_out, T, w3f, name="l3fB")
    P.op("sync", lambda e: e.nop(), waits=[t])
    return P


_HSHAPES = None


def kernel(**inputs):
    global _HSHAPES
    H = host_prep(inputs)
    _HSHAPES = {nm: (H[nm].shape, H[nm].dtype == ml_dtypes.bfloat16) for nm in H}
    x = np.asarray(inputs["L0_attn_x"], dtype=np.float32)
    xcores = [np.ascontiguousarray(x[i].T) for i in range(8)]

    PA = build_launch_A()
    shared = {nm: H[nm] for nm in A_NAMES}
    resA = PA.run([{"x": xcores[i], **shared} for i in range(8)])
    x2 = [resA.results[i]["x2_out"] for i in range(8)]
    ln1 = [resA.results[i]["ln1_out"] for i in range(8)]

    PB = build_launch_B()
    shared = {nm: H[nm] for nm in B_NAMES}
    inB = []
    for j in range(8):
        lnss = np.stack([ln1[i][:, 512 * j:512 * (j + 1)] for i in range(8)])
        inB.append({"lnss": np.ascontiguousarray(lnss), **shared})
    resB = PB.run(inB)
    h1ss = [resB.results[j]["hss"] for j in range(8)]

    PC = build_launch_C()
    shared = {nm: H[nm] for nm in C_NAMES}
    inC = []
    for i in range(8):
        h1 = np.stack([h1ss[j][i] for j in range(8)])
        inC.append({"x": x2[i], "h1": np.ascontiguousarray(h1), **shared})
    resC = PC.run(inC)
    x3 = [resC.results[i]["x3_out"] for i in range(8)]
    ln2 = [resC.results[i]["ln2_out"] for i in range(8)]

    PD = build_launch_B()
    shared = {nm: H[nm] for nm in B_NAMES}
    inD = []
    for j in range(8):
        lnss = np.stack([ln2[i][:, 512 * j:512 * (j + 1)] for i in range(8)])
        inD.append({"lnss": np.ascontiguousarray(lnss), **shared})
    resD = PD.run(inD)
    h2ss = [resD.results[j]["hss"] for j in range(8)]

    PE = build_launch_E()
    shared = {nm: H[nm] for nm in E_NAMES}
    inE = []
    for i in range(8):
        h2 = np.stack([h2ss[j][i] for j in range(8)])
        inE.append({"x": x3[i], "h2": np.ascontiguousarray(h2), **shared})
    resE = PE.run(inE)
    out = np.stack([resE.results[i]["x_out"].T for i in range(8)])
    return np.ascontiguousarray(out, dtype=np.float32)


# version 03
# speedup vs baseline: 1.0334x; 1.0334x over previous
import math
from contextlib import ExitStack

import numpy as np
import ml_dtypes

import concourse.bass as bass
import concourse.mybir as mybir
from concourse.bass_utils import run_bass_kernel_spmd

F32 = mybir.dt.float32
BF16 = mybir.dt.bfloat16
AF = mybir.ActivationFunctionType
ALU = mybir.AluOpType
AX = mybir.AxisListType

NCORES = 8
D = 1024
T = 4096
KT = D // 128
TC = 512

ENGINES = ("sync", "scalar", "vector", "tensor", "gpsimd")
DMA_QUEUES = {"sync": "dS", "scalar": "dA", "gpsimd": "dG"}


class Prog:

    def __init__(self, name="prog"):
        self.nc = bass.Bass("TRN2", target_bir_lowering=False)
        self.stack = ExitStack()
        self.stack.__enter__()
        self.scopes = [self.stack]
        self.gate = ()
        self.ops = {e: [] for e in ENGINES}
        self._sems = {}
        self.tot = {}
        self._names = set()
        for s in ("pe", "v", "s", "g", "sp", "dS", "dA", "dG"):
            self._get_sem(s)

    from contextlib import contextmanager as _cm

    @_cm
    def scope(self):
        es = ExitStack()
        es.__enter__()
        self.scopes.append(es)
        try:
            yield
        finally:
            self.scopes.pop()
            es.__exit__(None, None, None)

    def barrier(self):
        self.gate = tuple((s, t) for s, t in self.tot.items() if t > 0)

    def _get_sem(self, name):
        if name not in self._sems:
            self._sems[name] = self.stack.enter_context(self.nc.semaphore(name))
            self.tot[name] = 0
        return self._sems[name]

    def _uniq(self, name):
        i, n = 0, name
        while n in self._names:
            i += 1
            n = f"{name}_{i}"
        self._names.add(n)
        return n

    def sbuf(self, name, shape, dtype, root=False):
        scope = self.scopes[0] if root else self.scopes[-1]
        return scope.enter_context(
            self.nc.sbuf_tensor(self._uniq(name), list(shape), dtype))

    def psum(self, name, shape, dtype=F32):
        return self.scopes[-1].enter_context(
            self.nc.psum_tensor(self._uniq(name), list(shape), dtype))

    def dram(self, name, shape, dtype, kind=None):
        if kind is None:
            t = self.nc.dram_tensor(self._uniq(name), list(shape), dtype)
        else:
            t = self.nc.dram_tensor(name, list(shape), dtype, kind=kind)
        return t.ap()

    ENG_SEM = {"tensor": "pe", "vector": "v", "scalar": "s", "gpsimd": "g", "sync": "sp"}

    def op(self, eng, fn, waits=(), inc=False):
        semname = self.ENG_SEM[eng]
        tok = None
        if inc:
            self.tot[semname] += 1
            tok = (semname, self.tot[semname])
        self.ops[eng].append((fn, tuple(w for w in waits if w) + self.gate,
                              (semname, 1) if inc else None))
        return tok

    def dma(self, eng, out, in_, waits=(), sem=None, **kw):
        q = sem if sem is not None else DMA_QUEUES[eng]
        q = {"sync": "", "scalar": "A", "gpsimd": "G"}[eng] + q
        self._get_sem(q)
        waits = tuple(w for w in waits if w)
        if not hasattr(self, "_burst"):
            self._burst = {}
            self._last_q = {}
        if self._last_q.get(eng) != q:
            prev_end = self._burst.get(q, 0)
            if prev_end > 0:
                waits = waits + ((q, prev_end),)
        self._last_q[eng] = q
        self.tot[q] += 16
        self._burst[q] = self.tot[q]
        tok = (q, self.tot[q])

        def fn(e, out=out, in_=in_, kw=kw):
            return e.dma_start(out=out, in_=in_, **kw)

        self.ops[eng].append((fn, waits + self.gate, (q, 16)))
        return tok

    N_DSEM_POOL = 16

    def dsems(self, n):
        if not hasattr(self, "_dcur"):
            self._dcur = 0
        out = []
        for _ in range(n):
            name = f"dp{self._dcur % self.N_DSEM_POOL}"
            self._dcur += 1
            self._get_sem(name)
            out.append(name)
        return out

    def finalize(self):
        blk = self.stack.enter_context(self.nc.Block())
        for eng in ENGINES:
            if not self.ops[eng]:
                continue
            oplist = self.ops[eng]

            def body(e, oplist=oplist, eng=eng):
                satisfied = {}
                for fn, waits, inc in oplist:
                    merged = {}
                    for (sname, val) in waits:
                        merged[sname] = max(merged.get(sname, 0), val)
                    for sname, val in merged.items():
                        if satisfied.get(sname, 0) >= val:
                            continue
                        e.wait_ge(self._sems[sname], val)
                        satisfied[sname] = val
                    inst = fn(e)
                    if inc is not None:
                        sname, amt = inc
                        assert inst is not None, f"inc but no inst on {eng}"
                        inst.then_inc(self._sems[sname], amt)

            getattr(blk, eng)(body)
        return self.nc

    def run(self, in_maps, core_ids=None, **kw):
        nc = self.finalize()
        res = run_bass_kernel_spmd(
            nc, in_maps,
            core_ids=list(range(len(in_maps))) if core_ids is None else core_ids, **kw)
        self.stack.__exit__(None, None, None)
        return res


def cdiv(a, b):
    return (a + b - 1) // b


def make_ctx(P):
    ctx = {}
    ones = P.sbuf("c_ones128", [128, 1], F32)
    P.op("vector", lambda v: v.memset(ones[:, :], 1.0), inc=True)
    bc1 = P.sbuf("c_bcast1", [1, 128], F32)
    tok = P.op("vector", lambda v: v.memset(bc1[:, :], 1.0), inc=True)
    ctx["ones128"] = ones
    ctx["bcast1"] = bc1
    ctx["const_tok"] = tok
    ctx["cf"] = {}
    ctx["P"] = P
    for v in (1e-5, 1e-6, 1.0):
        constf(ctx, v)
    return ctx


def constf(ctx, val):
    val = float(val)
    if val not in ctx["cf"]:
        assert len(ctx["P"].scopes) == 1, f"constf({val}) inside scope"
        P = ctx["P"]
        t = P.sbuf(f"cf_{len(ctx['cf'])}", [128, 1], F32, root=True)
        tok = P.op("vector", lambda v: v.memset(t[:, :], val), inc=True)
        ctx["cf"][val] = (t, tok)
    return ctx["cf"][val]



def load_weight_bf16(P, w_ap, K, N, name, waits=()):
    kt = K // 128
    assert K % 128 == 0
    wsb = P.sbuf(f"w_{name}", [128, kt * N], BF16)
    if w_ap.dtype == BF16:
        sem = P.dsems(1)[0]
        tok = None
        for j in range(kt):
            tok = P.dma("sync", wsb[:, j * N:(j + 1) * N],
                        w_ap[j * 128:(j + 1) * 128, 0:N], waits=waits, sem=sem)
        return wsb, tok
    CH = min(4096, N)
    stage = [P.sbuf(f"wst_{name}", [128, CH], F32) for _ in range(2)]
    ssem = P.dsems(2)
    free = [tuple(waits), tuple(waits)]
    tok = None
    si = 0
    for j in range(kt):
        for c0 in range(0, N, CH):
            cw = min(CH, N - c0)
            buf = stage[si % 2]
            ld = P.dma("sync", buf[:, 0:cw], w_ap[j * 128:(j + 1) * 128, c0:c0 + cw],
                       waits=free[si % 2], sem=ssem[si % 2])
            tok = P.op("vector",
                       lambda v, buf=buf, cw=cw, j=j, c0=c0:
                       v.tensor_copy(wsb[:, j * N + c0:j * N + c0 + cw], buf[:, 0:cw]),
                       waits=[ld], inc=True)
            free[si % 2] = (tok,)
            si += 1
    return wsb, tok


def load_vec_cols(P, v_ap, n, name, waits=(), dtype=F32):
    assert n % 128 == 0
    nt = n // 128
    sb = P.sbuf(f"vc_{name}", [128, nt], dtype)
    src = v_ap.rearrange("(a p) -> p a", p=128)
    tok = P.dma("sync", sb[:, 0:nt], src, waits=waits,
                sem=P.dsems(1)[0], allow_slow_non_contiguous=True)
    return sb, tok



def emit_ln(P, ctx, x_ap, out_ap, Ttot, out_dtype=BF16, gamma=None, beta=None,
            rms=False, eps=1e-5, wait_x=(), name="ln", kt=KT, shift_half=False):
    D_ = kt * 128
    ones = ctx["ones128"]
    bc1 = ctx["bcast1"]
    ctok = ctx["const_tok"]
    xt = [P.sbuf(f"{name}_x", [128, kt * TC], F32) for _ in range(2)]
    sq = [P.sbuf(f"{name}_sq", [128, kt * TC], F32) for _ in range(2)]
    yt = [P.sbuf(f"{name}_y", [128, kt * TC], out_dtype) for _ in range(2)]
    st_m = P.psum(f"{name}_pm", [1, TC], F32)
    st_q = P.psum(f"{name}_pq", [1, TC], F32)
    mrow = P.sbuf(f"{name}_mr", [1, 2 * TC], F32)
    tmp_row = P.sbuf(f"{name}_tr", [1, TC], F32)
    mb = P.psum(f"{name}_mb", [128, TC], F32)
    rb = P.psum(f"{name}_rb", [128, TC], F32)
    free_x = [(), ()]
    free_sq = [(), ()]
    free_y = [(), ()]
    sx = P.dsems(2)
    sy = P.dsems(2)
    stat_free = ()
    mb_free = ()
    last = None
    for c in range(cdiv(Ttot, TC)):
        t0, tw = c * TC, min(TC, Ttot - c * TC)
        pp = c % 2
        xb, sqb, yb = xt[pp], sq[pp], yt[pp]
        lds = []
        for j in range(kt):
            if shift_half and j >= kt // 2:
                if c == 0:
                    lds.append(P.dma("sync", xb[:, j * TC + 1:j * TC + tw],
                                     x_ap[j * 128:(j + 1) * 128, 0:tw - 1],
                                     waits=free_x[pp] + tuple(wait_x), sem=sx[pp]))
                else:
                    lds.append(P.dma("sync", xb[:, j * TC:j * TC + tw],
                                     x_ap[j * 128:(j + 1) * 128, t0 - 1:t0 + tw - 1],
                                     waits=free_x[pp] + tuple(wait_x), sem=sx[pp]))
            else:
                lds.append(P.dma("sync", xb[:, j * TC:j * TC + tw],
                                 x_ap[j * 128:(j + 1) * 128, t0:t0 + tw],
                                 waits=free_x[pp] + tuple(wait_x), sem=sx[pp]))
        if shift_half and c == 0:
            lds.append(P.op("vector", lambda v, xb=xb: v.memset(
                xb[:, 0:kt * TC].rearrange("p (j t) -> p j t", t=TC)[:, kt // 2:, 0:1],
                0.0), waits=free_x[pp] + tuple(wait_x), inc=True))
        sqtok = P.op("scalar", lambda s, sqb=sqb, xb=xb, tw=tw: s.activation(
            sqb[:, 0:kt * TC], xb[:, 0:kt * TC], AF.Square),
            waits=lds + list(free_sq[pp]), inc=True)
        pe_m = pe_q = None
        for j in range(kt):
            pe_m = P.op("tensor", lambda t, j=j, xb=xb, tw=tw: t.matmul(
                st_m[0:1, 0:tw], ones[:, 0:1], xb[:, j * TC:j * TC + tw],
                start=(j == 0), stop=(j == kt - 1)),
                waits=list(lds) + list(stat_free) + [ctok], inc=(j == kt - 1))
        for j in range(kt):
            pe_q = P.op("tensor", lambda t, j=j, sqb=sqb, tw=tw: t.matmul(
                st_q[0:1, 0:tw], ones[:, 0:1], sqb[:, j * TC:j * TC + tw],
                start=(j == 0), stop=(j == kt - 1)),
                waits=[sqtok], inc=(j == kt - 1))
        v1 = P.op("vector", lambda v, tw=tw: v.tensor_scalar_mul(
            mrow[:, 0:tw], st_m[:, 0:tw], 1.0 / D_), waits=[pe_m], inc=True)
        v2 = P.op("vector", lambda v, tw=tw: v.tensor_scalar_mul(
            mrow[:, TC:TC + tw], st_q[:, 0:tw], 1.0 / D_), waits=[pe_q], inc=True)
        stat_free = (v1, v2)
        if not rms:
            v3 = P.op("vector", lambda v, tw=tw: v.tensor_tensor(
                tmp_row[:, 0:tw], mrow[:, 0:tw], mrow[:, 0:tw], ALU.mult),
                waits=[v1], inc=True)
            var_tok = P.op("vector", lambda v, tw=tw: v.tensor_sub(
                mrow[:, TC:TC + tw], mrow[:, TC:TC + tw], tmp_row[:, 0:tw]),
                waits=[v2, v3], inc=True)
        else:
            var_tok = v2
        eps_t, eps_tok = constf(ctx, eps)
        s1 = P.op("scalar", lambda s, tw=tw: s.activation(
            mrow[:, TC:TC + tw], mrow[:, TC:TC + tw], AF.Ln, bias=eps_t[0:1, 0:1]),
            waits=[var_tok, eps_tok], inc=True)
        s2 = P.op("scalar", lambda s, tw=tw: s.activation(
            mrow[:, TC:TC + tw], mrow[:, TC:TC + tw], AF.Exp, scale=-0.5),
            waits=[s1], inc=True)
        b_m = P.op("tensor", lambda t, tw=tw: t.matmul(
            mb[:, 0:tw], bc1[0:1, :], mrow[:, 0:tw], start=True, stop=True),
            waits=[v1] + list(mb_free), inc=True) if not rms else None
        b_r = P.op("tensor", lambda t, tw=tw: t.matmul(
            rb[:, 0:tw], bc1[0:1, :], mrow[:, TC:TC + tw], start=True, stop=True),
            waits=[s2] + list(mb_free), inc=True)
        ytoks = []
        for j in range(kt):
            if not rms:
                w1 = P.op("vector", lambda v, j=j, xb=xb, sqb=sqb, tw=tw: v.tensor_sub(
                    sqb[:, j * TC:j * TC + tw], xb[:, j * TC:j * TC + tw], mb[:, 0:tw]),
                    waits=[b_m, pe_q], inc=True)
                src = sqb
            else:
                w1 = None
                src = xb
            if gamma is None:
                w2 = P.op("vector", lambda v, j=j, src=src, yb=yb, tw=tw: v.tensor_tensor(
                    yb[:, j * TC:j * TC + tw], src[:, j * TC:j * TC + tw],
                    rb[:, 0:tw], ALU.mult),
                    waits=[w1, b_r] + list(free_y[pp]), inc=True)
            else:
                w2 = P.op("vector", lambda v, j=j, src=src, sqb=sqb, tw=tw: v.tensor_tensor(
                    sqb[:, j * TC:j * TC + tw], src[:, j * TC:j * TC + tw],
                    rb[:, 0:tw], ALU.mult),
                    waits=[w1, b_r, pe_q], inc=True)
                if beta is not None:
                    w2 = P.op("vector", lambda v, j=j, sqb=sqb, yb=yb, tw=tw: v.tensor_scalar(
                        yb[:, j * TC:j * TC + tw], sqb[:, j * TC:j * TC + tw],
                        gamma[:, j:j + 1], beta[:, j:j + 1], ALU.mult, ALU.add),
                        waits=[w2] + list(free_y[pp]), inc=True)
                else:
                    w2 = P.op("vector", lambda v, j=j, sqb=sqb, yb=yb, tw=tw: v.tensor_scalar_mul(
                        yb[:, j * TC:j * TC + tw], sqb[:, j * TC:j * TC + tw],
                        gamma[:, j:j + 1]),
                        waits=[w2] + list(free_y[pp]), inc=True)
            ytoks.append(w2)
        mb_free = tuple(ytoks)
        free_x[pp] = tuple(ytoks)
        free_sq[pp] = tuple(ytoks)
        sts = [P.dma("scalar", out_ap[j * 128:(j + 1) * 128, t0:t0 + tw],
                     yb[:, j * TC:j * TC + tw], waits=[ytoks[j]], sem=sy[pp]) for j in range(kt)]
        free_y[pp] = tuple(sts)
        last = sts[-1]
    return last



def emit_linear_fm(P, ctx, x_ap, Ttot, K, N, w, name, *, w_dtype=F32,
                   bias=None, act=None, act_scale=1.0, out_ap=None,
                   out_dtype=BF16, resid=None, wait_x=(), wait_w=(),
                   x_dtype=BF16, evac_extra=None):
    kt = K // 128
    ot = N // 128
    assert K % 128 == 0 and N % 128 == 0
    if isinstance(w, tuple):
        wsb, wtok = w
    elif w_dtype == BF16:
        wsb = P.sbuf(f"w_{name}", [128, kt * N], BF16)
        wtok = None
        wls = P.dsems(1)[0]
        for j in range(kt):
            wtok = P.dma("sync", wsb[:, j * N:(j + 1) * N],
                         w[j * 128:(j + 1) * 128, 0:N], waits=wait_w, sem=wls)
    else:
        wsb, wtok = load_weight_bf16(P, w, K, N, name, waits=wait_w)
    bias_sb, bias_tok = bias if bias is not None else (None, None)

    xb = [P.sbuf(f"{name}_x", [128, kt * TC], x_dtype) for _ in range(2)]
    ob = [P.sbuf(f"{name}_o", [128, ot * TC], out_dtype) for _ in range(2)]
    ps = [P.psum(f"{name}_ps", [128, TC], F32) for _ in range(2)]
    sx = P.dsems(2)
    so = P.dsems(2)
    if resid is not None:
        xold_ap, xnew_ap, alpha = resid
        rb = [P.sbuf(f"{name}_r", [128, ot * TC], F32) for _ in range(2)]
        sr = P.dsems(2)
        rfree = [(), ()]
    free_x = [(), ()]
    free_o = [(), ()]
    ps_free = [(), ()]
    last = None
    for c in range(cdiv(Ttot, TC)):
        t0, tw = c * TC, min(TC, Ttot - c * TC)
        pp = c % 2
        lds = [P.dma("sync", xb[pp][:, j * TC:j * TC + tw],
                     x_ap[j * 128:(j + 1) * 128, t0:t0 + tw],
                     waits=free_x[pp] + tuple(wait_x), sem=sx[pp]) for j in range(kt)]
        if resid is not None:
            rlds = [P.dma("gpsimd", rb[pp][:, o * TC:o * TC + tw],
                          xold_ap[o * 128:(o + 1) * 128, t0:t0 + tw],
                          waits=rfree[pp], sem=sr[pp]) for o in range(ot)]
        otoks = []
        for o in range(ot):
            pb = ps[o % 2]
            mmtok = None
            for j in range(kt):
                mmtok = P.op("tensor", lambda t, o=o, j=j, pb=pb, pp=pp, tw=tw: t.matmul(
                    pb[:, 0:tw],
                    wsb[:, j * N + o * 128:j * N + o * 128 + 128],
                    xb[pp][:, j * TC:j * TC + tw],
                    start=(j == 0), stop=(j == kt - 1)),
                    waits=([wtok] if j == 0 else []) + (list(lds) if j == 0 else [])
                          + (list(ps_free[o % 2]) if j == 0 else []),
                    inc=(j == kt - 1))
            afn = act if act is not None else AF.Identity
            bap = bias_sb[:, o:o + 1] if bias_sb is not None else 0.0
            ev = P.op("scalar", lambda s, o=o, pb=pb, pp=pp, tw=tw, afn=afn, bap=bap: s.activation(
                ob[pp][:, o * TC:o * TC + tw], pb[:, 0:tw], afn,
                bias=bap, scale=act_scale),
                waits=[mmtok, bias_tok] + list(free_o[pp]), inc=True)
            ps_free[o % 2] = (ev,)
            if evac_extra is not None:
                ev = evac_extra(P, o, c, tw, ob[pp], ev)
            if resid is not None:
                ev = P.op("vector", lambda v, o=o, pp=pp, tw=tw: v.scalar_tensor_tensor(
                    rb[pp][:, o * TC:o * TC + tw], ob[pp][:, o * TC:o * TC + tw],
                    alpha, rb[pp][:, o * TC:o * TC + tw], ALU.mult, ALU.add),
                    waits=[ev] + rlds, inc=True)
                st = P.dma("scalar", xnew_ap[o * 128:(o + 1) * 128, t0:t0 + tw],
                           rb[pp][:, o * TC:o * TC + tw], waits=[ev], sem=so[pp])
            else:
                st = P.dma("scalar", out_ap[o * 128:(o + 1) * 128, t0:t0 + tw],
                           ob[pp][:, o * TC:o * TC + tw], waits=[ev], sem=so[pp])
            otoks.append(ev)
            last = st
        free_x[pp] = tuple(otoks)
        free_o[pp] = (last,)
        if resid is not None:
            rfree[pp] = tuple(otoks) + (last,)
    return last



def emit_compose_M(P, ctx, cst, wr_ap, wi_ap, m_ap, name="mcomp"):
    n = 1024
    kt = 8
    wr, twr = load_weight_bf16(P, wr_ap, n, n, f"{name}_wr")
    wi, twi = load_weight_bf16(P, wi_ap, n, n, f"{name}_wi")
    sfl = P.dsems(1)[0]
    fr = P.sbuf(f"{name}_fr", [128, kt * n], BF16)
    fi = P.sbuf(f"{name}_fi", [128, kt * n], BF16)
    fin = P.sbuf(f"{name}_fin", [128, kt * n], BF16)
    ftoks = []
    for j in range(kt):
        ftoks.append(P.dma("sync", fr[:, j * n:(j + 1) * n],
                           cst["F_r"][j * 128:(j + 1) * 128, :], sem=sfl))
        ftoks.append(P.dma("sync", fi[:, j * n:(j + 1) * n],
                           cst["F_i"][j * 128:(j + 1) * 128, :], sem=sfl))
        ftoks.append(P.dma("sync", fin[:, j * n:(j + 1) * n],
                           cst["F_i_neg"][j * 128:(j + 1) * 128, :], sem=sfl))
    ar = P.sbuf(f"{name}_ar", [128, kt * n], BF16)
    ai = P.sbuf(f"{name}_ai", [128, kt * n], BF16)
    ps = [P.psum(f"{name}_ps", [128, TC], F32) for _ in range(2)]
    ps_free = [(), ()]
    atoks = []
    pi = 0
    for (dst, w1, f1, w2, f2) in ((ar, wr, fr, wi, fin), (ai, wi, fr, wr, fi)):
        for o in range(8):
            for dchunk in range(2):
                pb = ps[pi % 2]
                mm = None
                for j in range(kt):
                    mm = P.op("tensor", lambda t, pb=pb, w1=w1, f1=f1, o=o, j=j, dchunk=dchunk: t.matmul(
                        pb[:, 0:TC], w1[:, j * n + o * 128:j * n + o * 128 + 128],
                        f1[:, j * n + dchunk * TC:j * n + (dchunk) * TC + TC],
                        start=(j == 0), stop=False),
                        waits=([twr, twi] + ftoks + list(ps_free[pi % 2])) if j == 0 else [])
                for j in range(kt):
                    mm = P.op("tensor", lambda t, pb=pb, w2=w2, f2=f2, o=o, j=j, dchunk=dchunk: t.matmul(
                        pb[:, 0:TC], w2[:, j * n + o * 128:j * 1024 + o * 128 + 128],
                        f2[:, j * n + dchunk * TC:j * n + dchunk * TC + TC],
                        start=False, stop=(j == kt - 1)),
                        waits=[], inc=(j == kt - 1))
                ev = P.op("vector", lambda v, dst=dst, pb=pb, o=o, dchunk=dchunk: v.tensor_copy(
                    dst[:, o * n + dchunk * TC:o * n + dchunk * TC + TC], pb[:, 0:TC]),
                    waits=[mm], inc=True)
                ps_free[pi % 2] = (ev,)
                atoks.append(ev)
                pi += 1
    gr = P.sbuf(f"{name}_gr", [128, kt * n], BF16)
    gin = P.sbuf(f"{name}_gin", [128, kt * n], BF16)
    sgl = P.dsems(1)[0]
    gtoks = []
    for j in range(kt):
        gtoks.append(P.dma("sync", gr[:, j * n:(j + 1) * n],
                           cst["G_r"][j * 128:(j + 1) * 128, :], sem=sgl))
        gtoks.append(P.dma("sync", gin[:, j * n:(j + 1) * n],
                           cst["G_i_neg"][j * 128:(j + 1) * 128, :], sem=sgl))
    mb = P.sbuf(f"{name}_m", [128, TC], BF16)
    smo = P.dsems(1)[0]
    mfree = ()
    last = None
    for dt_ in range(8):
        for oc in range(2):
            pb = ps[pi % 2]
            mm = None
            for j in range(kt):
                mm = P.op("tensor", lambda t, pb=pb, j=j, dt_=dt_, oc=oc: t.matmul(
                    pb[:, 0:TC], ar[:, j * n + dt_ * 128:j * n + dt_ * 128 + 128],
                    gr[:, j * n + oc * TC:j * n + oc * TC + TC],
                    start=(j == 0), stop=False),
                    waits=(atoks + gtoks + list(ps_free[pi % 2])) if j == 0 else [])
            for j in range(kt):
                mm = P.op("tensor", lambda t, pb=pb, j=j, dt_=dt_, oc=oc: t.matmul(
                    pb[:, 0:TC], ai[:, j * n + dt_ * 128:j * n + dt_ * 128 + 128],
                    gin[:, j * n + oc * TC:j * n + oc * TC + TC],
                    start=False, stop=(j == kt - 1)),
                    waits=[], inc=(j == kt - 1))
            ev = P.op("vector", lambda v, pb=pb: v.tensor_copy(mb[:, 0:TC], pb[:, 0:TC]),
                      waits=[mm] + list(mfree), inc=True)
            ps_free[pi % 2] = (ev,)
            st = P.dma("scalar", m_ap[dt_ * 128:dt_ * 128 + 128, oc * TC:oc * TC + TC],
                       mb[:, 0:TC], waits=[ev], sem=smo)
            mfree = (st,)
            last = st
            pi += 1
    return last



def make_phi_elu_extra(P, name):
    scr = P.sbuf(f"{name}_phi", [128, TC], F32)

    def cb(P_, o, c, tw, buf, tok):
        t1 = P_.op("vector", lambda v, o=o, tw=tw, buf=buf: v.tensor_scalar_min(
            scr[:, 0:tw], buf[:, o * TC:o * TC + tw], 0.0), waits=[tok], inc=True)
        t2 = P_.op("scalar", lambda s, tw=tw: s.activation(
            scr[:, 0:tw], scr[:, 0:tw], AF.Exp), waits=[t1], inc=True)
        t3 = P_.op("vector", lambda v, o=o, tw=tw, buf=buf: v.scalar_tensor_tensor(
            buf[:, o * TC:o * TC + tw], buf[:, o * TC:o * TC + tw], 0.0,
            scr[:, 0:tw], ALU.max, ALU.add), waits=[t2], inc=True)
        return t3

    return cb


def emit_kv_colsum(P, ctx, kf_ap, vf_ap, Ttot, name="kv"):
    kvh = P.sbuf(f"{name}_kv", [128, KT], F32)
    ksh = P.sbuf(f"{name}_ks", [128, KT], F32)
    z0 = P.op("vector", lambda v: v.memset(kvh[:, :], 0.0), inc=True)
    z1 = P.op("vector", lambda v: v.memset(ksh[:, :], 0.0), inc=True)
    kb = [P.sbuf(f"{name}_k", [128, KT * TC], F32) for _ in range(2)]
    vb = [P.sbuf(f"{name}_v", [128, KT * TC], F32) for _ in range(2)]
    prod = P.sbuf(f"{name}_p", [128, TC], F32)
    col = P.sbuf(f"{name}_c", [128, 2], F32)
    sk = P.dsems(2)
    sv = P.dsems(2)
    free = [(z0, z1), (z0, z1)]
    tok = (None,)
    for c in range(cdiv(Ttot, TC)):
        t0, tw = c * TC, min(TC, Ttot - c * TC)
        pp = c % 2
        lk = [P.dma("sync", kb[pp][:, j * TC:j * TC + tw],
                    kf_ap[j * 128:(j + 1) * 128, t0:t0 + tw],
                    waits=free[pp], sem=sk[pp]) for j in range(KT)]
        lv = [P.dma("gpsimd", vb[pp][:, j * TC:j * TC + tw],
                    vf_ap[j * 128:(j + 1) * 128, t0:t0 + tw],
                    waits=free[pp], sem=sv[pp]) for j in range(KT)]
        jtoks = []
        for j in range(KT):
            m1 = P.op("vector", lambda v, j=j, pp=pp, tw=tw: v.tensor_mul(
                prod[:, 0:tw], kb[pp][:, j * TC:j * TC + tw], vb[pp][:, j * TC:j * TC + tw]),
                waits=lk + lv + list(tok), inc=True)
            r1 = P.op("vector", lambda v, tw=tw: v.reduce_sum(
                col[:, 0:1], prod[:, 0:tw], AX.X), waits=[m1], inc=True)
            r2 = P.op("vector", lambda v, j=j, pp=pp, tw=tw: v.reduce_sum(
                col[:, 1:2], kb[pp][:, j * TC:j * TC + tw], AX.X), waits=[r1], inc=True)
            a1 = P.op("vector", lambda v, j=j: v.tensor_add(
                kvh[:, j:j + 1], kvh[:, j:j + 1], col[:, 0:1]), waits=[r1], inc=True)
            a2 = P.op("vector", lambda v, j=j: v.tensor_add(
                ksh[:, j:j + 1], ksh[:, j:j + 1], col[:, 1:2]), waits=[r2], inc=True)
            tok = (a2,)
            jtoks.append(a2)
        free[pp] = tuple(jtoks)
    return kvh, ksh, tok[0]


def emit_l3_z(P, ctx, qf_ap, kvh, ksh, out_ap, Ttot, wait, name="l3z"):
    qb = [P.sbuf(f"{name}_q", [128, KT * TC], F32) for _ in range(2)]
    ob = [P.sbuf(f"{name}_o", [128, KT * TC], BF16) for _ in range(2)]
    den = P.sbuf(f"{name}_d", [128, TC], F32)
    num = P.sbuf(f"{name}_n", [128, TC], F32)
    sq = P.dsems(2)
    so = P.dsems(2)
    free_q = [wait, wait]
    free_o = [(), ()]
    last = None
    prev = None
    for c in range(cdiv(Ttot, TC)):
        t0, tw = c * TC, min(TC, Ttot - c * TC)
        pp = c % 2
        lq = [P.dma("sync", qb[pp][:, j * TC:j * TC + tw],
                    qf_ap[j * 128:(j + 1) * 128, t0:t0 + tw],
                    waits=free_q[pp], sem=sq[pp]) for j in range(KT)]
        jt = []
        for j in range(KT):
            d1 = P.op("vector", lambda v, j=j, pp=pp, tw=tw: v.tensor_scalar(
                den[:, 0:tw], qb[pp][:, j * TC:j * TC + tw],
                ksh[:, j:j + 1], 1e-6, ALU.mult, ALU.add),
                waits=lq + ([prev] if prev else []), inc=True)
            d2 = P.op("vector", lambda v, tw=tw: v.reciprocal(
                den[:, 0:tw], den[:, 0:tw]), waits=[d1], inc=True)
            n1 = P.op("vector", lambda v, j=j, pp=pp, tw=tw: v.tensor_scalar_mul(
                num[:, 0:tw], qb[pp][:, j * TC:j * TC + tw],
                kvh[:, j:j + 1]), waits=lq + [d2], inc=True)
            o1 = P.op("vector", lambda v, j=j, pp=pp, tw=tw: v.tensor_mul(
                ob[pp][:, j * TC:j * TC + tw], num[:, 0:tw], den[:, 0:tw]),
                waits=[n1] + list(free_o[pp]), inc=True)
            prev = o1
            jt.append(o1)
        free_q[pp] = tuple(jt)
        sts = [P.dma("scalar", out_ap[j * 128:(j + 1) * 128, t0:t0 + tw],
                     ob[pp][:, j * TC:j * TC + tw], waits=[jt[j]], sem=so[pp])
               for j in range(KT)]
        free_o[pp] = tuple(sts)
        last = sts[-1]
    return last


def emit_l3_attn(P, ctx, ln_ap, xold_ap, xnew_ap, Ttot, w, name="l3a"):
    qf = P.dram(f"{name}_qf", [D, Ttot], F32)
    kf = P.dram(f"{name}_kf", [D, Ttot], F32)
    vf = P.dram(f"{name}_vf", [D, Ttot], F32)
    va = P.dram(f"{name}_va", [D, Ttot], BF16)
    with P.scope():
        bq, tbq = load_vec_cols(P, w["bq"], D, f"{name}bq")
        phi = make_phi_elu_extra(P, f"{name}q")
        t1 = emit_linear_fm(P, ctx, ln_ap, Ttot, D, D, w["wq"], f"{name}q",
                            bias=(bq, tbq), out_ap=qf, out_dtype=F32,
                            evac_extra=phi)
    P.barrier()
    with P.scope():
        bk, tbk = load_vec_cols(P, w["bk"], D, f"{name}bk")
        phi = make_phi_elu_extra(P, f"{name}k")
        t2 = emit_linear_fm(P, ctx, ln_ap, Ttot, D, D, w["wk"], f"{name}k",
                            bias=(bk, tbk), out_ap=kf, out_dtype=F32,
                            evac_extra=phi)
    P.barrier()
    with P.scope():
        bv, tbv = load_vec_cols(P, w["bv"], D, f"{name}bv")
        t3 = emit_linear_fm(P, ctx, ln_ap, Ttot, D, D, w["wv"], f"{name}v",
                            bias=(bv, tbv), out_ap=vf, out_dtype=F32)
    P.barrier()
    with P.scope():
        kvh, ksh, t4 = emit_kv_colsum(P, ctx, kf, vf, Ttot, name=f"{name}kv")
        t5 = emit_l3_z(P, ctx, qf, kvh, ksh, va, Ttot, (t4,), name=f"{name}z")
    P.barrier()
    with P.scope():
        bo, tbo = load_vec_cols(P, w["bo"], D, f"{name}bo")
        t6 = emit_linear_fm(P, ctx, va, Ttot, D, D, w["wo"], f"{name}o",
                            bias=(bo, tbo), act=AF.Gelu_apprx_tanh,
                            resid=(xold_ap, xnew_ap, 1.0))
    P.barrier()
    return t6



def emit_l2_attn(P, ctx, ln_ap, xold_ap, xnew_ap, Ttot, w, name="l2a"):
    H = 16
    scale = 64.0 ** -0.25
    phiq_ap = P.dram(f"{name}_pq", [D, Ttot], BF16)
    nch = cdiv(Ttot, TC)
    assert Ttot % 128 == 0

    with P.scope():
        wq, twq = load_weight_bf16(P, w["wqT"], D, D, f"{name}wq")
        wk, twk = load_weight_bf16(P, w["wkT"], D, D, f"{name}wk")
        wv, twv = load_weight_bf16(P, w["wvT"], D, D, f"{name}wv")
        bq, tbq = load_vec_cols(P, w["bq"], D, f"{name}bq")
        bqs = P.sbuf(f"{name}_bqs", [128, KT], F32)
        tbqs = P.op("vector", lambda v: v.tensor_scalar_mul(
            bqs[:, 0:KT], bq[:, 0:KT], scale), waits=[tbq], inc=True)
        bvr = P.sbuf(f"{name}_bvr", [1, D], F32)
        tbv = P.dma("sync", bvr[0:1, :], w["bv"].unsqueeze(0), sem=P.dsems(1)[0])
        bvb = P.sbuf(f"{name}_bvb", [128, D], F32)
        tb = None
        with P.scope():
            bvps = P.psum(f"{name}_bvp", [128, TC], F32)
            for half in range(2):
                t1 = P.op("tensor", lambda t, half=half: t.matmul(
                    bvps[:, 0:TC], ctx["bcast1"][0:1, :], bvr[0:1, half * TC:half * TC + TC],
                    start=True, stop=True), waits=[tbv, ctx["const_tok"], tb], inc=True)
                tb = P.op("vector", lambda v, half=half: v.tensor_copy(
                    bvb[:, half * TC:half * TC + TC], bvps[:, 0:TC]), waits=[t1], inc=True)
        P.barrier()
        wtoks = [twq, twk, twv, tbqs, tb]

        lnb = [P.sbuf(f"{name}_ln", [128, KT * TC], BF16) for _ in range(2)]
        pqb = [P.sbuf(f"{name}_pqo", [128, KT * TC], BF16) for _ in range(2)]
        ktb = P.sbuf(f"{name}_kt", [128, D], BF16)
        vtb = P.sbuf(f"{name}_vt", [128, D], BF16)
        scr = P.sbuf(f"{name}_scr", [128, TC], F32)
        scr2 = P.sbuf(f"{name}_scr2", [128, TC], F32)
        psf = [P.psum(f"{name}_psf", [128, TC], F32) for _ in range(2)]
        pst = [P.psum(f"{name}_pst", [128, 2 * TC], F32) for _ in range(2)]
        kvp = P.psum(f"{name}_kvp", [128, 2 * TC], F32)
        sln = P.dsems(2)
        spq = P.dsems(2)
        free_ln = [(), ()]
        free_pq = [(), ()]
        psf_free = [(), ()]
        pst_free = [(), ()]
        kt_free = ()
        vt_free = ()
        scr_free = ()
        scr2_free = ()
        bank_first = {0: True, 1: True}
        kv_last_mm = None
        for c in range(nch):
            t0, tw = c * TC, min(TC, Ttot - c * TC)
            pp = c % 2
            lds = [P.dma("sync", lnb[pp][:, j * TC:j * TC + tw],
                         ln_ap[j * 128:(j + 1) * 128, t0:t0 + tw],
                         waits=free_ln[pp], sem=sln[pp]) for j in range(KT)]
            pq_toks = []
            pq_last = None
            for o in range(KT):
                pb = psf[o % 2]
                mm = None
                for j in range(KT):
                    mm = P.op("tensor", lambda t, o=o, j=j, pb=pb, pp=pp, tw=tw: t.matmul(
                        pb[:, 0:tw], wq[:, j * D + o * 128:j * D + o * 128 + 128],
                        lnb[pp][:, j * TC:j * TC + tw],
                        start=(j == 0), stop=(j == KT - 1)),
                        waits=(wtoks + lds + list(psf_free[o % 2])) if j == 0 else [],
                        inc=(j == KT - 1))
                e1 = P.op("scalar", lambda s, o=o, pb=pb, tw=tw: s.activation(
                    scr[:, 0:tw], pb[:, 0:tw], AF.Silu,
                    bias=bqs[:, o:o + 1], scale=scale),
                    waits=[mm] + list(scr_free), inc=True)
                psf_free[o % 2] = (e1,)
                e2 = P.op("vector", lambda v, o=o, pp=pp, tw=tw: v.tensor_scalar_add(
                    pqb[pp][:, o * TC:o * TC + tw], scr[:, 0:tw], 1.0),
                    waits=[e1] + list(free_pq[pp]), inc=True)
                scr_free = (e2,)
                st = P.dma("scalar", phiq_ap[o * 128:(o + 1) * 128, t0:t0 + tw],
                           pqb[pp][:, o * TC:o * TC + tw], waits=[e2], sem=spq[pp])
                pq_toks.append(e2)
                pq_last = st
            free_pq[pp] = (pq_last,)
            kv_toks = []
            last_iter_c = (c == nch - 1)
            for ts_ in range(tw // 128):
                last_iter = last_iter_c and (ts_ == tw // 128 - 1)
                ktoks = []
                for which in ("k", "v"):
                    pb = pst[0 if which == "k" else 1]
                    wmat = wk if which == "k" else wv
                    for nh in range(2):
                        mm = None
                        for j in range(KT):
                            mm = P.op("tensor", lambda t, pb=pb, wmat=wmat, j=j, nh=nh,
                                      pp=pp, ts_=ts_: t.matmul(
                                pb[:, nh * TC:nh * TC + TC],
                                lnb[pp][:, j * TC + ts_ * 128:j * TC + ts_ * 128 + 128],
                                wmat[:, j * D + nh * TC:j * D + nh * TC + TC],
                                start=(j == 0), stop=(j == KT - 1)),
                                waits=(list(pst_free[0 if which == "k" else 1]) + lds)
                                      if (j == 0 and nh == 0) else [],
                                inc=(j == KT - 1))
                        if which == "k":
                            ev = P.op("scalar", lambda s, nh=nh, pb=pb: s.activation(
                                scr2[:, 0:TC], pb[:, nh * TC:nh * TC + TC], AF.Silu,
                                scale=scale),
                                waits=[mm] + list(scr2_free), inc=True)
                            ev = P.op("vector", lambda v, nh=nh: v.tensor_scalar_add(
                                ktb[:, nh * TC:nh * TC + TC], scr2[:, 0:TC], 1.0),
                                waits=[ev] + list(kt_free), inc=True)
                            scr2_free = (ev,)
                        else:
                            ev = P.op("vector", lambda v, nh=nh, pb=pb: v.tensor_tensor(
                                vtb[:, nh * TC:nh * TC + TC], pb[:, nh * TC:nh * TC + TC],
                                bvb[:, nh * TC:nh * TC + TC], ALU.add),
                                waits=[mm] + list(vt_free), inc=True)
                        ktoks.append(ev)
                for m in range(8):
                    bank = 0 if m < 4 else 1
                    for hh in range(2):
                        h = 2 * m + hh
                        st_flag = bank_first[bank]
                        bank_first[bank] = False
                        sp_flag = last_iter and hh == 1 and m in (3, 7)
                        kv_last_mm = P.op("tensor", lambda t, m=m, h=h, hh=hh, st_flag=st_flag, sp_flag=sp_flag: t.matmul(
                            kvp[:, m * 128 + hh * 64:m * 128 + hh * 64 + 64],
                            ktb[:, m * 128:m * 128 + 128],
                            vtb[:, h * 64:h * 64 + 64],
                            start=st_flag, stop=sp_flag),
                            waits=ktoks, inc=(m == 7 and hh == 1))
                kv_toks.append(kv_last_mm)
                kt_free = (kv_last_mm,)
                vt_free = (kv_last_mm,)
            free_ln[pp] = tuple(pq_toks) + tuple(kv_toks)
        kvsb = P.sbuf(f"{name}_kvsb", [128, 2 * TC], BF16)
        kv_ev = P.op("vector", lambda v: v.tensor_copy(kvsb[:, :], kvp[:, :]),
                     waits=[kv_last_mm], inc=True)
        z1 = P.op("vector", lambda v: v.memset(
            kvsb[64:128, :].rearrange("p (m c) -> p m c", c=128)[:, :, 0:64], 0.0),
            waits=[kv_ev], inc=True)
        kv_ev = P.op("vector", lambda v: v.memset(
            kvsb[0:64, :].rearrange("p (m c) -> p m c", c=128)[:, :, 64:128], 0.0),
            waits=[z1], inc=True)
    P.barrier()
    with P.scope():
        wo, two = load_weight_bf16(P, w["woT"], D, D, f"{name}wo")
        bo, tbo = load_vec_cols(P, w["bo"], D, f"{name}bo")
        pqc = [P.sbuf(f"{name}_pqc", [128, KT * TC], BF16) for _ in range(2)]
        wvs = [P.sbuf(f"{name}_wvs", [128, KT * TC], BF16) for _ in range(2)]
        rb = [P.sbuf(f"{name}_rb", [128, KT * TC], F32) for _ in range(2)]
        ob = [P.sbuf(f"{name}_ob", [128, KT * TC], F32) for _ in range(2)]
        psh = [P.psum(f"{name}_psh", [128, TC], F32) for _ in range(2)]
        pso = [P.psum(f"{name}_pso", [128, TC], F32) for _ in range(2)]
        spc = P.dsems(2)
        srb = P.dsems(2)
        sst = P.dsems(2)
        free_pqc = [(), ()]
        free_rb = [(), ()]
        free_ob = [(), ()]
        psh_free = [(), ()]
        pso_free = [(), ()]
        last = None
        for c in range(nch):
            t0, tw = c * TC, min(TC, Ttot - c * TC)
            pp = c % 2
            lds = [P.dma("sync", pqc[pp][:, j * TC:j * TC + tw],
                         phiq_ap[j * 128:(j + 1) * 128, t0:t0 + tw],
                         waits=free_pqc[pp], sem=spc[pp]) for j in range(KT)]
            rlds = [P.dma("gpsimd", rb[pp][:, j * TC:j * TC + tw],
                          xold_ap[j * 128:(j + 1) * 128, t0:t0 + tw],
                          waits=free_rb[pp], sem=srb[pp]) for j in range(KT)]
            wv_toks = []
            for o in range(KT):
                pb = psh[o % 2]
                mm = P.op("tensor", lambda t, o=o, pb=pb, pp=pp, tw=tw: t.matmul(
                    pb[:, 0:tw], kvsb[:, o * 128:o * 128 + 128],
                    pqc[pp][:, o * TC:o * TC + tw],
                    start=True, stop=True),
                    waits=[kv_ev] + lds + list(psh_free[o % 2]), inc=True)
                ev = P.op("scalar", lambda s, o=o, pp=pp, pb=pb, tw=tw: s.activation(
                    wvs[pp][:, o * TC:o * TC + tw], pb[:, 0:tw], AF.Identity),
                    waits=[mm], inc=True)
                psh_free[o % 2] = (ev,)
                wv_toks.append(ev)
            otoks = []
            for o in range(KT):
                pb = pso[o % 2]
                mm = None
                for j in range(KT):
                    mm = P.op("tensor", lambda t, o=o, j=j, pb=pb, pp=pp, tw=tw: t.matmul(
                        pb[:, 0:tw], wo[:, j * D + o * 128:j * D + o * 128 + 128],
                        wvs[pp][:, j * TC:j * TC + tw],
                        start=(j == 0), stop=(j == KT - 1)),
                        waits=([two] + wv_toks + list(pso_free[o % 2])) if j == 0 else [],
                        inc=(j == KT - 1))
                e1 = P.op("scalar", lambda s, o=o, pb=pb, pp=pp, tw=tw: s.activation(
                    ob[pp][:, o * TC:o * TC + tw], pb[:, 0:tw], AF.Identity,
                    bias=bo[:, o:o + 1]),
                    waits=[mm, tbo] + list(free_ob[pp]), inc=True)
                pso_free[o % 2] = (e1,)
                e2 = P.op("vector", lambda v, o=o, pp=pp, tw=tw: v.tensor_add(
                    rb[pp][:, o * TC:o * TC + tw], rb[pp][:, o * TC:o * TC + tw],
                    ob[pp][:, o * TC:o * TC + tw]),
                    waits=[e1] + rlds, inc=True)
                st = P.dma("scalar", xnew_ap[o * 128:(o + 1) * 128, t0:t0 + tw],
                           rb[pp][:, o * TC:o * TC + tw], waits=[e2], sem=sst[pp])
                otoks.append(e2)
                last = st
            free_pqc[pp] = tuple(wv_toks)
            free_rb[pp] = tuple(otoks) + (last,)
            free_ob[pp] = tuple(otoks)
    P.barrier()
    return last



QMID = (0.25, 0.75, 1.25, 1.75, 2.5, 3.5, 5.0)


def make_quant(P, name, C):
    nb = C // 16
    am = P.sbuf(f"{name}_am", [128, nb], F32)
    thr = P.sbuf(f"{name}_th", [128, nb], F32)
    absx = P.sbuf(f"{name}_ab", [128, C], BF16)
    accA = P.sbuf(f"{name}_aa", [128, C], BF16)
    accB = P.sbuf(f"{name}_ab2", [128, C], BF16)
    ind = P.sbuf(f"{name}_in", [128, C], BF16)
    i7 = P.sbuf(f"{name}_i7", [128, C], BF16)
    sgn = P.sbuf(f"{name}_sg", [128, C], BF16)
    state = {"free": ()}

    def qfn(x_ap, out_ap, cols, waits=(), out_waits=()):
        b = cols // 16
        x3 = x_ap.rearrange("p (g k) -> p g k", k=16)
        waits = tuple(waits) + state["free"]
        t0 = P.op("vector", lambda v: v.tensor_reduce(
            am[:, 0:b].unsqueeze(2), x3, AX.X, ALU.max, apply_absolute_value=True),
            waits=waits, inc=True)
        t1 = P.op("vector", lambda v: v.tensor_scalar(
            am[:, 0:b], am[:, 0:b], 1.0 / 6.0, 1e-12, ALU.mult, ALU.max),
            waits=[t0], inc=True)
        ts_ = P.op("scalar", lambda s: s.sign(sgn[:, 0:cols], x_ap), waits=waits, inc=True)
        ta = P.op("scalar", lambda s: s.activation(
            absx[:, 0:cols], x_ap, AF.Abs), waits=waits, inc=True)
        toks = []
        prev_thr = t1
        for j in range(7):
            dst = accA if j == 0 else (accB if j == 4 else (i7 if j == 6 else ind))
            tt_ = P.op("vector", lambda v, j=j: v.tensor_scalar_mul(
                thr[:, 0:b], am[:, 0:b], QMID[j]), waits=[prev_thr], inc=True)
            thr3 = thr[:, 0:b].unsqueeze(2).broadcast_to([128, b, 16])
            g = P.op("vector", lambda v, dst=dst, thr3=thr3: v.tensor_tensor(
                dst[:, 0:cols].rearrange("p (g k) -> p g k", k=16),
                absx[:, 0:cols].rearrange("p (g k) -> p g k", k=16),
                thr3, ALU.is_gt), waits=[tt_, ta], inc=True)
            prev_thr = g
            if j in (1, 2, 3):
                g = P.op("vector", lambda v: v.tensor_add(
                    accA[:, 0:cols], accA[:, 0:cols], ind[:, 0:cols]),
                    waits=[g], inc=True)
            elif j == 5:
                g = P.op("vector", lambda v: v.tensor_add(
                    accB[:, 0:cols], accB[:, 0:cols], ind[:, 0:cols]),
                    waits=[g], inc=True)
            prev_thr = g
            toks.append(g)
        q1 = P.op("vector", lambda v: v.scalar_tensor_tensor(
            accA[:, 0:cols], accA[:, 0:cols], 0.5, accB[:, 0:cols],
            ALU.mult, ALU.add), waits=toks, inc=True)
        q2 = P.op("vector", lambda v: v.scalar_tensor_tensor(
            accA[:, 0:cols], i7[:, 0:cols], 2.0, accA[:, 0:cols],
            ALU.mult, ALU.add), waits=[q1], inc=True)
        q3 = P.op("vector", lambda v: v.tensor_mul(
            accA[:, 0:cols], accA[:, 0:cols], sgn[:, 0:cols]), waits=[q2, ts_], inc=True)
        am3 = am[:, 0:b].unsqueeze(2).broadcast_to([128, b, 16])
        q4 = P.op("vector", lambda v, am3=am3: v.tensor_tensor(
            out_ap.rearrange("p (g k) -> p g k", k=16),
            accA[:, 0:cols].rearrange("p (g k) -> p g k", k=16),
            am3, ALU.mult), waits=[q3] + list(out_waits), inc=True)
        state["free"] = (q4,)
        return q4

    return qfn




def emit_l1_wprep(P, ctx, w, wq, ident_ap, name="l1w"):
    idb = P.sbuf(f"{name}_id", [128, 128], BF16)
    tid = P.dma("sync", idb[:, :], ident_ap, sem=P.dsems(1)[0])
    last = None
    with P.scope():
        qfn = make_quant(P, f"{name}q", 2816)
        stg = P.sbuf(f"{name}_st", [128, 2816], F32)
        qin = P.sbuf(f"{name}_qi", [128, 2816], BF16)
        qout = P.sbuf(f"{name}_qo", [128, 2816], BF16)
        trp = [P.psum(f"{name}_tp", [128, 128], BF16) for _ in range(2)]
        trev = [P.sbuf(f"{name}_te", [128, 128], BF16) for _ in range(2)]
        sld = P.dsems(1)[0]
        sst = P.dsems(2)
        free_st = ()
        free_ev = [(), ()]
        trp_free = [(), ()]
        for (src, dst, R, C) in ((w["w_gate"], wq["wqT_gate"], 2816, 1024),
                                 (w["w_up"], wq["wqT_up"], 2816, 1024),
                                 (w["w_down"], wq["wqT_down"], 1024, 2816)):
            for r in range(R // 128):
                ld = P.dma("sync", stg[:, 0:C], src[r * 128:(r + 1) * 128, 0:C],
                           waits=free_st, sem=sld)
                ct = P.op("vector", lambda v, C=C: v.tensor_copy(qin[:, 0:C], stg[:, 0:C]),
                          waits=[ld], inc=True)
                qt = qfn(qin[:, 0:C], qout[:, 0:C], C, waits=[ct])
                free_st = (qt,)
                for kb in range(C // 128):
                    pi = kb % 2
                    tt_ = P.op("tensor", lambda t, kb=kb, pi=pi: t.transpose(
                        trp[pi][:, :], qout[:, kb * 128:kb * 128 + 128], idb[:, :]),
                        waits=[qt, tid] + list(trp_free[pi]), inc=True)
                    ev = P.op("scalar", lambda s, pi=pi: s.copy(
                        trev[pi][:, :], trp[pi][:, :]),
                        waits=[tt_] + list(free_ev[pi]), inc=True)
                    trp_free[pi] = (ev,)
                    st = P.dma("scalar", dst[kb * 128:(kb + 1) * 128, r * 128:(r + 1) * 128],
                               trev[pi][:, :], waits=[ev], sem=sst[pi])
                    free_ev[pi] = (st,)
                    last = st
    P.barrier()
    return last


def emit_l1_ffn(P, ctx, ln_ap, xold_ap, xnew_ap, Ttot, wq, ident_ap, name="l1f"):
    I = 2816
    NC5 = [(i * TC, min(TC, I - i * TC)) for i in range(cdiv(I, TC))]
    idb = P.sbuf(f"{name}_id", [128, 128], BF16)
    tid = P.dma("sync", idb[:, :], ident_ap, sem=P.dsems(1)[0])
    qfn = make_quant(P, f"{name}q", I)
    lnb = P.sbuf(f"{name}_ln", [128, KT * TC], BF16)
    lntok = P.sbuf(f"{name}_lt", [128, D], BF16)
    xqtok = P.sbuf(f"{name}_xq", [128, D], BF16)
    xqfm = P.sbuf(f"{name}_xf", [128, KT * TC], BF16)
    gsb = P.sbuf(f"{name}_gs", [128, I], F32)
    htok = P.sbuf(f"{name}_ht", [128, I], BF16)
    hqtok = P.sbuf(f"{name}_hq", [128, I], BF16)
    hqfm = P.sbuf(f"{name}_hf", [128, 22 * TC], BF16)
    rb = P.sbuf(f"{name}_rb", [128, KT * TC], F32)
    ob = P.sbuf(f"{name}_ob", [128, KT * TC], F32)
    wbuf = [P.sbuf(f"{name}_wb", [128, KT * TC], BF16) for _ in range(2)]
    wdbuf = [P.sbuf(f"{name}_wd", [128, 22 * 128], BF16) for _ in range(2)]
    trp = [P.psum(f"{name}_tp", [128, 128], BF16) for _ in range(2)]
    trev = [P.sbuf(f"{name}_te", [128, 128], BF16) for _ in range(2)]
    psg = P.psum(f"{name}_pg", [128, TC], F32)
    psu = P.psum(f"{name}_pu", [128, TC], F32)
    psd = [P.psum(f"{name}_pd", [128, TC], F32) for _ in range(2)]
    sln = P.dsems(2)
    sw = P.dsems(2)
    swd = P.dsems(2)
    srb = P.dsems(1)[0]
    sst = P.dsems(1)[0]
    free_ln = ()
    free_rb = ()
    free_w = [(), ()]
    free_wd = [(), ()]
    trp_free = [(), ()]
    trev_free = [(), ()]
    psg_free = ()
    psu_free = ()
    psd_free = [(), ()]
    xqfm_free = ()
    hqfm_free = ()
    xqtok_free = ()
    hqtok_free = ()
    gs_free = ()
    ht_free = ()
    hq_free = ()
    ob_free = ()
    last = None
    nch = cdiv(Ttot, TC)
    for c in range(nch):
        t0, tw = c * TC, min(TC, Ttot - c * TC)
        nts = tw // 128
        lds = [P.dma("sync", lnb[:, j * TC:j * TC + tw],
                     ln_ap[j * 128:(j + 1) * 128, t0:t0 + tw],
                     waits=free_ln, sem=sln[c % 2]) for j in range(KT)]
        rlds = [P.dma("gpsimd", rb[:, j * TC:j * TC + tw],
                      xold_ap[j * 128:(j + 1) * 128, t0:t0 + tw],
                      waits=free_rb, sem=srb) for j in range(KT)]
        xq_toks = []
        xqtok_free = xqtok_free[-8:]
        hqtok_free = hqtok_free[-22:]
        for ts_ in range(nts):
            evs = []
            for j in range(KT):
                pi = j % 2
                tt_ = P.op("tensor", lambda t, j=j, ts_=ts_, pi=pi: t.transpose(
                    trp[pi][:, :], lnb[:, j * TC + ts_ * 128:j * TC + ts_ * 128 + 128],
                    idb[:, :]),
                    waits=lds + [tid] + list(trp_free[pi]), inc=True)
                ev = P.op("scalar", lambda s, j=j, pi=pi: s.copy(
                    lntok[:, j * 128:j * 128 + 128], trp[pi][:, :]),
                    waits=[tt_] + list(trev_free[pi]), inc=True)
                trp_free[pi] = (ev,)
                evs.append(ev)
            qt = qfn(lntok[:, 0:D], xqtok[:, 0:D], D, waits=evs,
                     out_waits=xqtok_free)
            trev_free = [(qt,), (qt,)]
            for j in range(KT):
                pi = j % 2
                tt_ = P.op("tensor", lambda t, j=j, pi=pi: t.transpose(
                    trp[pi][:, :], xqtok[:, j * 128:j * 128 + 128], idb[:, :]),
                    waits=[qt] + list(trp_free[pi]), inc=True)
                ev = P.op("scalar", lambda s, j=j, ts_=ts_, pi=pi: s.copy(
                    xqfm[:, j * TC + ts_ * 128:j * TC + ts_ * 128 + 128], trp[pi][:, :]),
                    waits=[tt_] + list(xqfm_free), inc=True)
                trp_free[pi] = (ev,)
                xqtok_free = xqtok_free + (tt_,)
                xq_toks.append(ev)
        h_toks = []
        for ts_ in range(nts):
            ts_toks = []
            for ni, (n0, nw) in enumerate(NC5):
                wl = []
                for which in range(2):
                    wsrc = wq["wqT_gate"] if which == 0 else wq["wqT_up"]
                    wl.append([P.dma("sync" if which == 0 else "gpsimd",
                                     wbuf[which][:, j * TC:j * TC + nw],
                                     wsrc[j * 128:(j + 1) * 128, n0:n0 + nw],
                                     waits=free_w[which], sem=sw[which])
                               for j in range(KT)])
                mmg = mmu = None
                for j in range(KT):
                    mmg = P.op("tensor", lambda t, j=j, ts_=ts_, nw=nw: t.matmul(
                        psg[:, 0:nw], xqfm[:, j * TC + ts_ * 128:j * TC + ts_ * 128 + 128],
                        wbuf[0][:, j * TC:j * TC + nw],
                        start=(j == 0), stop=(j == KT - 1)),
                        waits=(xq_toks + wl[0] + list(psg_free)) if j == 0 else [],
                        inc=(j == KT - 1))
                for j in range(KT):
                    mmu = P.op("tensor", lambda t, j=j, ts_=ts_, nw=nw: t.matmul(
                        psu[:, 0:nw], xqfm[:, j * TC + ts_ * 128:j * TC + ts_ * 128 + 128],
                        wbuf[1][:, j * TC:j * TC + nw],
                        start=(j == 0), stop=(j == KT - 1)),
                        waits=(wl[1] + list(psu_free)) if j == 0 else [],
                        inc=(j == KT - 1))
                free_w[0] = (mmg,)
                free_w[1] = (mmu,)
                e1 = P.op("scalar", lambda s, n0=n0, nw=nw: s.activation(
                    gsb[:, n0:n0 + nw], psg[:, 0:nw], AF.Silu),
                    waits=[mmg] + list(gs_free), inc=True)
                psg_free = (e1,)
                e2 = P.op("vector", lambda v, n0=n0, nw=nw: v.tensor_tensor(
                    htok[:, n0:n0 + nw], psu[:, 0:nw], gsb[:, n0:n0 + nw], ALU.mult),
                    waits=[e1, mmu] + list(ht_free), inc=True)
                psu_free = (e2,)
                ts_toks.append(e2)
            qt2 = qfn(htok[:, 0:I], hqtok[:, 0:I], I, waits=ts_toks,
                      out_waits=hqtok_free)
            gs_free = (qt2,)
            ht_free = (qt2,)
            for g in range(22):
                pi = g % 2
                tt_ = P.op("tensor", lambda t, g=g, pi=pi: t.transpose(
                    trp[pi][:, :], hqtok[:, g * 128:g * 128 + 128], idb[:, :]),
                    waits=[qt2] + list(trp_free[pi]), inc=True)
                ev = P.op("scalar", lambda s, g=g, ts_=ts_, pi=pi: s.copy(
                    hqfm[:, g * TC + ts_ * 128:g * TC + ts_ * 128 + 128], trp[pi][:, :]),
                    waits=[tt_] + list(hqfm_free), inc=True)
                trp_free[pi] = (ev,)
                hqtok_free = hqtok_free + (tt_,)
                h_toks.append(ev)
        hq_free = tuple(h_toks)
        otoks = []
        for o in range(KT):
            pi = o % 2
            wld = [P.dma("sync", wdbuf[pi][:, g * 128:g * 128 + 128],
                         wq["wqT_down"][g * 128:(g + 1) * 128, o * 128:(o + 1) * 128],
                         waits=free_wd[pi], sem=swd[pi]) for g in range(22)]
            pb = psd[pi]
            mm = None
            for g in range(22):
                mm = P.op("tensor", lambda t, g=g, o=o, pb=pb, pi=pi, tw=tw: t.matmul(
                    pb[:, 0:tw], wdbuf[pi][:, g * 128:g * 128 + 128],
                    hqfm[:, g * TC:g * TC + tw],
                    start=(g == 0), stop=(g == 21)),
                    waits=(h_toks + wld + list(psd_free[pi])) if g == 0 else [],
                    inc=(g == 21))
            free_wd[pi] = (mm,)
            e1 = P.op("scalar", lambda s, o=o, pb=pb, tw=tw: s.activation(
                ob[:, o * TC:o * TC + tw], pb[:, 0:tw], AF.Identity),
                waits=[mm] + list(ob_free), inc=True)
            psd_free[pi] = (e1,)
            e2 = P.op("vector", lambda v, o=o, tw=tw: v.scalar_tensor_tensor(
                rb[:, o * TC:o * TC + tw], ob[:, o * TC:o * TC + tw], 0.5,
                rb[:, o * TC:o * TC + tw], ALU.mult, ALU.add),
                waits=[e1] + rlds, inc=True)
            st = P.dma("scalar", xnew_ap[o * 128:(o + 1) * 128, t0:t0 + tw],
                       rb[:, o * TC:o * TC + tw], waits=[e2], sem=sst)
            otoks.append(e2)
            last = st
        ob_free = tuple(otoks)
        free_ln = tuple(xq_toks)
        free_rb = tuple(otoks) + (last,)
        xqfm_free = (mmg, mmu)
        hqfm_free = (mm,)
        gs_free = gs_free
        ht_free = ht_free
    return last


def emit_l1_conv(P, ctx, h_ap, x0_ap, x1_ap, Ttot, wconv_ap, convb, name="l1c"):
    cb_sb, cb_tok = convb
    wsb = P.sbuf(f"{name}_w", [128, 4 * 2 * D], BF16)
    wtok = None
    with P.scope():
        stg = P.sbuf(f"{name}_ws", [128, D], F32)
        sld = P.dsems(1)[0]
        free = ()
        for k in range(4):
            for ct in range(2):
                ld = P.dma("sync", stg[:, 0:D],
                           wconv_ap[k, ct * 128:(ct + 1) * 128, 0:D],
                           waits=free, sem=sld)
                wtok = P.op("vector", lambda v, k=k, ct=ct: v.tensor_copy(
                    wsb[:, (k * 2 + ct) * D:(k * 2 + ct) * D + D], stg[:, 0:D]),
                    waits=[ld], inc=True)
                free = (wtok,)
    hb = [P.sbuf(f"{name}_h", [128, KT * (TC + 3)], BF16) for _ in range(2)]
    rb = [P.sbuf(f"{name}_r", [128, KT * TC], F32) for _ in range(2)]
    ps = [P.psum(f"{name}_ps", [128, TC], F32) for _ in range(2)]
    sh = P.dsems(2)
    sr = P.dsems(2)
    sst = P.dsems(2)
    free_h = [(wtok,), (wtok,)]
    free_r = [(wtok,), (wtok,)]
    ps_free = [(), ()]
    CW = TC + 3
    last = None
    for c in range(cdiv(Ttot, TC)):
        t0, tw = c * TC, min(TC, Ttot - c * TC)
        pp = c % 2
        hl = []
        for j in range(KT):
            if c == 0:
                hl.append(P.dma("sync", hb[pp][:, j * CW + 3:j * CW + 3 + tw],
                                h_ap[j * 128:(j + 1) * 128, 0:tw],
                                waits=free_h[pp], sem=sh[pp]))
            else:
                hl.append(P.dma("sync", hb[pp][:, j * CW:j * CW + 3 + tw],
                                h_ap[j * 128:(j + 1) * 128, t0 - 3:t0 + tw],
                                waits=free_h[pp], sem=sh[pp]))
        if c == 0:
            zt = P.op("vector", lambda v, pp=pp: v.memset(
                hb[pp][:, 0:KT * CW].rearrange("p (j w) -> p j w", w=CW)[:, :, 0:3], 0.0),
                waits=list(free_h[pp]), inc=True)
            hl.append(zt)
        rl = [P.dma("gpsimd", rb[pp][:, j * TC:j * TC + tw],
                    x0_ap[j * 128:(j + 1) * 128, t0:t0 + tw],
                    waits=free_r[pp], sem=sr[pp]) for j in range(KT)]
        otoks = []
        for o in range(KT):
            g = o // 2
            pb = ps[o % 2]
            mm = None
            first = True
            for k in range(4):
                for ct in range(2):
                    cj = g * 2 + ct
                    mm = P.op("tensor", lambda t, o=o, k=k, ct=ct, cj=cj, pb=pb, pp=pp, tw=tw: t.matmul(
                        pb[:, 0:tw],
                        wsb[:, (k * 2 + ct) * D + o * 128:(k * 2 + ct) * D + o * 128 + 128],
                        hb[pp][:, cj * CW + k:cj * CW + k + tw],
                        start=(k == 0 and ct == 0), stop=(k == 3 and ct == 1)),
                        waits=(hl + [wtok] + list(ps_free[o % 2])) if first else [],
                        inc=(k == 3 and ct == 1))
                    first = False
            e1 = P.op("vector", lambda v, o=o, pb=pb, pp=pp, tw=tw: v.tensor_scalar(
                pb[:, 0:tw], pb[:, 0:tw], cb_sb[:, o:o + 1], None, ALU.add),
                waits=[mm, cb_tok], inc=True)
            e2 = P.op("vector", lambda v, o=o, pb=pb, pp=pp, tw=tw: v.tensor_add(
                rb[pp][:, o * TC:o * TC + tw], rb[pp][:, o * TC:o * TC + tw], pb[:, 0:tw]),
                waits=[e1] + rl, inc=True)
            ps_free[o % 2] = (e2,)
            st = P.dma("scalar", x1_ap[o * 128:(o + 1) * 128, t0:t0 + tw],
                       rb[pp][:, o * TC:o * TC + tw], waits=[e2], sem=sst[pp])
            otoks.append(e2)
            last = st
        free_h[pp] = tuple(otoks)
        free_r[pp] = tuple(otoks) + (last,)
    return last


def emit_l1_pool(P, ctx, h_ap, x1_ap, x2_ap, Ttot, w, name="l1p"):
    wqkv, twq = load_weight_bf16(P, w["Wqkv3T"], D, 3 * D, f"{name}w")
    bqkv, tb = load_vec_cols(P, w["bqkv3"], 3 * D, f"{name}b")
    isc_sb = P.sbuf(f"{name}_is", [128, 4096], F32)
    tis = P.dma("sync", isc_sb[:, 0:Ttot], w["inv_scale_b"][:, 0:Ttot], sem=P.dsems(1)[0])
    lnb = [P.sbuf(f"{name}_ln", [128, KT * TC], BF16) for _ in range(2)]
    qkv = P.sbuf(f"{name}_qkv", [128, 3 * KT * TC], F32)
    ones = P.sbuf(f"{name}_1", [128, TC], F32)
    t1s = P.op("vector", lambda v: v.memset(ones[:, :], 1.0), inc=True)
    carry = P.sbuf(f"{name}_cy", [128, 2 * KT], F32)
    tcy = P.op("vector", lambda v: v.memset(carry[:, :], 0.0), inc=True)
    scr = P.sbuf(f"{name}_sc", [128, TC], F32)
    rb = [P.sbuf(f"{name}_r", [128, KT * TC], F32) for _ in range(2)]
    ps = [P.psum(f"{name}_ps", [128, TC], F32) for _ in range(2)]
    sln = P.dsems(2)
    sr = P.dsems(2)
    sst = P.dsems(2)
    free_ln = [(), ()]
    free_r = [(), ()]
    ps_free = [(), ()]
    qkv_free = ()
    scr_free = ()
    carry_tok = tcy
    last = None
    for c in range(cdiv(Ttot, TC)):
        t0, tw = c * TC, min(TC, Ttot - c * TC)
        pp = c % 2
        lds = [P.dma("sync", lnb[pp][:, j * TC:j * TC + tw],
                     h_ap[j * 128:(j + 1) * 128, t0:t0 + tw],
                     waits=free_ln[pp], sem=sln[pp]) for j in range(KT)]
        rl = [P.dma("gpsimd", rb[pp][:, j * TC:j * TC + tw],
                    x1_ap[j * 128:(j + 1) * 128, t0:t0 + tw],
                    waits=free_r[pp], sem=sr[pp]) for j in range(KT)]
        evs = []
        for o in range(3 * KT):
            pb = ps[o % 2]
            mm = None
            for j in range(KT):
                mm = P.op("tensor", lambda t, o=o, j=j, pb=pb, pp=pp, tw=tw: t.matmul(
                    pb[:, 0:tw], wqkv[:, j * 3 * D + o * 128:j * 3 * D + o * 128 + 128],
                    lnb[pp][:, j * TC:j * TC + tw],
                    start=(j == 0), stop=(j == KT - 1)),
                    waits=([twq] + lds + list(ps_free[o % 2])) if j == 0 else [],
                    inc=(j == KT - 1))
            ev = P.op("scalar", lambda s, o=o, pb=pb, tw=tw: s.activation(
                qkv[:, o * TC:o * TC + tw], pb[:, 0:tw], AF.Identity,
                bias=bqkv[:, o:o + 1]),
                waits=[mm, tb] + list(qkv_free), inc=True)
            ps_free[o % 2] = (ev,)
            evs.append(ev)
        otoks = []
        for j in range(KT):
            qs = qkv[:, j * TC:j * TC + tw]
            ks = qkv[:, (KT + j) * TC:(KT + j) * TC + tw]
            vs = qkv[:, (2 * KT + j) * TC:(2 * KT + j) * TC + tw]
            s1 = P.op("vector", lambda v, qs=qs, tw=tw, j=j: v.tensor_tensor_scan(
                qs, ones[:, 0:tw], qs, carry[:, j:j + 1], ALU.mult, ALU.add),
                waits=evs + [t1s, carry_tok], inc=True)
            c1 = P.op("vector", lambda v, qs=qs, tw=tw, j=j: v.tensor_copy(
                carry[:, j:j + 1], qs[:, tw - 1:tw]), waits=[s1], inc=True)
            s2 = P.op("vector", lambda v, qs=qs, tw=tw, t0=t0: v.tensor_tensor(
                scr[:, 0:tw], qs, isc_sb[:, t0:t0 + tw], ALU.mult),
                waits=[s1, tis] + list(scr_free), inc=True)
            s3 = P.op("vector", lambda v, ks=ks, tw=tw: v.tensor_mul(
                scr[:, 0:tw], scr[:, 0:tw], ks), waits=[s2], inc=True)
            s4 = P.op("vector", lambda v, tw=tw, j=j: v.tensor_tensor_scan(
                scr[:, 0:tw], ones[:, 0:tw], scr[:, 0:tw], carry[:, KT + j:KT + j + 1],
                ALU.mult, ALU.add), waits=[s3, c1], inc=True)
            c2 = P.op("vector", lambda v, tw=tw, j=j: v.tensor_copy(
                carry[:, KT + j:KT + j + 1], scr[:, tw - 1:tw]), waits=[s4], inc=True)
            carry_tok = c2
            s5 = P.op("vector", lambda v, tw=tw, t0=t0: v.tensor_tensor(
                scr[:, 0:tw], scr[:, 0:tw], isc_sb[:, t0:t0 + tw], ALU.mult),
                waits=[s4, c2], inc=True)
            s6 = P.op("vector", lambda v, vs=vs, tw=tw: v.tensor_mul(
                scr[:, 0:tw], scr[:, 0:tw], vs), waits=[s5], inc=True)
            s7 = P.op("vector", lambda v, j=j, pp=pp, tw=tw: v.tensor_add(
                rb[pp][:, j * TC:j * TC + tw], rb[pp][:, j * TC:j * TC + tw],
                scr[:, 0:tw]), waits=[s6] + rl, inc=True)
            scr_free = (s7,)
            st = P.dma("scalar", x2_ap[j * 128:(j + 1) * 128, t0:t0 + tw],
                       rb[pp][:, j * TC:j * TC + tw], waits=[s7], sem=sst[pp])
            otoks.append(s7)
            last = st
        qkv_free = tuple(otoks)
        free_ln[pp] = tuple(evs)
        free_r[pp] = tuple(otoks) + (last,)
    return last


def emit_add3(P, ctx, a_ap, b_ap, c_ap, out_ap, Ttot, name="a3", alpha_c=1.0):
    ab = [P.sbuf(f"{name}_a", [128, KT * TC], F32) for _ in range(2)]
    bb = [P.sbuf(f"{name}_b", [128, KT * TC], F32) for _ in range(2)]
    cb = [P.sbuf(f"{name}_c", [128, KT * TC], F32) for _ in range(2)]
    sa = P.dsems(2)
    sb_ = P.dsems(2)
    sc_ = P.dsems(2)
    so = P.dsems(1)[0]
    free = [(), ()]
    last = None
    for c in range(cdiv(Ttot, TC)):
        t0, tw = c * TC, min(TC, Ttot - c * TC)
        pp = c % 2
        la = [P.dma("sync", ab[pp][:, j * TC:j * TC + tw],
                    a_ap[j * 128:(j + 1) * 128, t0:t0 + tw], waits=free[pp], sem=sa[pp])
              for j in range(KT)]
        lb = [P.dma("gpsimd", bb[pp][:, j * TC:j * TC + tw],
                    b_ap[j * 128:(j + 1) * 128, t0:t0 + tw], waits=free[pp], sem=sb_[pp])
              for j in range(KT)]
        lc = [P.dma("scalar", cb[pp][:, j * TC:j * TC + tw],
                    c_ap[j * 128:(j + 1) * 128, t0:t0 + tw], waits=free[pp], sem=sc_[pp])
              for j in range(KT)]
        toks = []
        for j in range(KT):
            t1 = P.op("vector", lambda v, j=j, pp=pp, tw=tw: v.scalar_tensor_tensor(
                cb[pp][:, j * TC:j * TC + tw], cb[pp][:, j * TC:j * TC + tw], alpha_c,
                bb[pp][:, j * TC:j * TC + tw], ALU.mult, ALU.add),
                waits=la + lb + lc, inc=True)
            t2 = P.op("vector", lambda v, j=j, pp=pp, tw=tw: v.tensor_add(
                cb[pp][:, j * TC:j * TC + tw], cb[pp][:, j * TC:j * TC + tw],
                ab[pp][:, j * TC:j * TC + tw]), waits=[t1], inc=True)
            st = P.dma("gpsimd", out_ap[j * 128:(j + 1) * 128, t0:t0 + tw],
                       cb[pp][:, j * TC:j * TC + tw], waits=[t2], sem=so)
            toks.append(t2)
            last = st
        free[pp] = tuple(toks) + (last,)
    return last


def emit_l1_attn(P, ctx, ln_ap, xold_ap, xnew_ap, Ttot, w, name="l1a"):
    h1 = P.dram(f"{name}_h1", [D, Ttot], BF16)
    x1 = P.dram(f"{name}_x1", [D, Ttot], F32)
    h2 = P.dram(f"{name}_h2", [D, Ttot], BF16)
    x2 = P.dram(f"{name}_x2", [D, Ttot], F32)
    h3 = P.dram(f"{name}_h3", [D, Ttot], BF16)
    hb = P.dram(f"{name}_hb", [4 * D, Ttot], BF16)
    hu = P.dram(f"{name}_hu", [D, Ttot], F32)
    with P.scope():
        g1, tg1 = load_vec_cols(P, w["ln_conv_g"], D, f"{name}g1")
        b1, tb1 = load_vec_cols(P, w["ln_conv_b"], D, f"{name}b1")
        t = emit_ln(P, ctx, ln_ap, h1, Ttot, out_dtype=BF16, gamma=g1, beta=b1,
                    eps=1e-5, wait_x=(tg1, tb1), name=f"{name}ln1")
    P.barrier()
    with P.scope():
        cb, tcb = load_vec_cols(P, w["conv_b"], D, f"{name}cb")
        t = emit_l1_conv(P, ctx, h1, ln_ap, x1, Ttot, w["wconv4"], (cb, tcb),
                         name=f"{name}cv")
    P.barrier()
    with P.scope():
        g2, tg2 = load_vec_cols(P, w["ln_attn_g"], D, f"{name}g2")
        b2, tb2 = load_vec_cols(P, w["ln_attn_b"], D, f"{name}b2")
        t = emit_ln(P, ctx, x1, h2, Ttot, out_dtype=BF16, gamma=g2, beta=b2,
                    eps=1e-5, wait_x=(tg2, tb2), name=f"{name}ln2")
    P.barrier()
    with P.scope():
        t = emit_l1_pool(P, ctx, h2, x1, x2, Ttot, w, name=f"{name}pl")
    P.barrier()
    with P.scope():
        g3, tg3 = load_vec_cols(P, w["ln_boom_g"], D, f"{name}g3")
        b3, tb3 = load_vec_cols(P, w["ln_boom_b"], D, f"{name}b3")
        t = emit_ln(P, ctx, x2, h3, Ttot, out_dtype=BF16, gamma=g3, beta=b3,
                    eps=1e-5, wait_x=(tg3, tb3), name=f"{name}ln3")
    P.barrier()
    with P.scope():
        bbv, tbb = load_vec_cols(P, w["bb"], 4 * D, f"{name}bb")
        t = emit_linear_fm(P, ctx, h3, Ttot, D, 4 * D, w["WbT"], f"{name}mb",
                           bias=(bbv, tbb), act=AF.Gelu, out_ap=hb, out_dtype=BF16)
    P.barrier()
    with P.scope():
        buv, tbu = load_vec_cols(P, w["bu"], D, f"{name}bu")
        t = emit_linear_fm(P, ctx, hb, Ttot, 4 * D, D, w["WuT"], f"{name}mu",
                           bias=(buv, tbu), out_ap=hu, out_dtype=F32)
    P.barrier()
    with P.scope():
        t = emit_add3(P, ctx, xold_ap, x2, hu, xnew_ap, Ttot, name=f"{name}a3")
    P.barrier()
    return t


def emit_l2_ffn(P, ctx, lnss_ap, hss_ap, w, name="l2f", TL=512, NB=8):
    xw_dram = P.dram(f"{name}_xw", [NB, 3 * D, TL], BF16)
    bfic, tbf = load_vec_cols(P, w["bfic"], 3 * D, f"{name}bf")
    inc_, tinc = load_vec_cols(P, w["in_c"], D, f"{name}ic")
    with P.scope():
        wx, twx = load_weight_bf16(P, w["WxT"], D, 3 * D, f"{name}wx")
        t1 = None
        for t in range(NB):
            with P.scope():
                t1 = emit_linear_fm(P, ctx, lnss_ap[t], TL, D, 3 * D, (wx, twx),
                                    f"{name}s1_{t}", out_ap=xw_dram[t], out_dtype=BF16)
            P.barrier()
    P.barrier()
    cst = P.sbuf(f"{name}_c", [128, KT * TL], F32)
    cbf = P.sbuf(f"{name}_cb", [128, KT * TL], BF16)
    stage2 = P.scope()
    stage2.__enter__()
    ut, tut = load_weight_bf16(P, w["UT"], D, 3 * D, f"{name}ut")
    fb = P.sbuf(f"{name}_f", [128, KT * TL], F32)
    ib = P.sbuf(f"{name}_i", [128, KT * TL], F32)
    gb = P.sbuf(f"{name}_g", [128, KT * TL], F32)
    scr = P.sbuf(f"{name}_s", [128, TL], F32)
    scr2 = P.sbuf(f"{name}_s2", [128, KT * TL], F32)
    xwb = [P.sbuf(f"{name}_xwb", [128, TL], F32) for _ in range(2)]
    xwl = [P.sbuf(f"{name}_xwl", [128, TL], BF16) for _ in range(2)]
    ps = [P.psum(f"{name}_ps", [128, TL], F32) for _ in range(2)]
    sxw = P.dsems(2)
    ctoks = [P.op("scalar", lambda s, j=j: s.activation(
        cst[:, j * TL:(j + 1) * TL], cst[:, j * TL:(j + 1) * TL], AF.Identity,
        bias=inc_[:, j:j + 1], scale=0.0), waits=[tinc], inc=True) for j in range(KT)]
    ps_free = [(), ()]
    xw_free = [(), ()]
    xwo_free = [(), ()]
    scr_free = ()
    gate_free = ()
    for t in range(NB):
        casts = [P.op("vector", lambda v, j=j: v.tensor_copy(
            cbf[:, j * TL:(j + 1) * TL], cst[:, j * TL:(j + 1) * TL]),
            waits=ctoks, inc=True) for j in range(KT)]
        gtoks = []
        for o in range(24):
            pi = o % 2
            lx = P.dma("sync", xwl[pi][:, 0:TL],
                       xw_dram[t, o * 128:(o + 1) * 128, 0:TL],
                       waits=xw_free[pi], sem=sxw[pi])
            pb = ps[pi]
            mm = None
            for j in range(KT):
                mm = P.op("tensor", lambda tt, o=o, j=j, pb=pb: tt.matmul(
                    pb[:, 0:TL], ut[:, j * 3 * D + o * 128:j * 3 * D + o * 128 + 128],
                    cbf[:, j * TL:(j + 1) * TL],
                    start=(j == 0), stop=(j == KT - 1)),
                    waits=([tut] + casts + list(ps_free[pi])) if j == 0 else [],
                    inc=(j == KT - 1))
            a1 = P.op("vector", lambda v, pb=pb, pi=pi: v.tensor_add(
                xwb[pi][:, 0:TL], xwl[pi][:, 0:TL], pb[:, 0:TL]),
                waits=[mm, lx] + list(xwo_free[pi]), inc=True)
            ps_free[pi] = (a1,)
            dst, af = ((fb, AF.Sigmoid) if o < 8 else
                       ((ib, AF.Sigmoid) if o < 16 else (gb, AF.Tanh)))
            oo = o % 8
            a2 = P.op("scalar", lambda s, dst=dst, af=af, oo=oo, o=o, pi=pi: s.activation(
                dst[:, oo * TL:oo * TL + TL], xwb[pi][:, 0:TL], af,
                bias=bfic[:, o:o + 1]),
                waits=[a1, tbf] + list(gate_free), inc=True)
            xw_free[pi] = (a1,)
            xwo_free[pi] = (a2,)
            gtoks.append(a2)
        ctoks = []
        for j in range(KT):
            m1 = P.op("vector", lambda v, j=j: v.tensor_tensor(
                scr2[:, j * TL:(j + 1) * TL], ib[:, j * TL:(j + 1) * TL],
                gb[:, j * TL:(j + 1) * TL], ALU.mult),
                waits=gtoks + list(scr_free), inc=True)
            m2 = P.op("vector", lambda v, j=j: v.tensor_mul(
                cst[:, j * TL:(j + 1) * TL], fb[:, j * TL:(j + 1) * TL],
                cst[:, j * TL:(j + 1) * TL]), waits=[m1] + casts, inc=True)
            m3 = P.op("vector", lambda v, j=j: v.tensor_add(
                cst[:, j * TL:(j + 1) * TL], cst[:, j * TL:(j + 1) * TL],
                scr2[:, j * TL:(j + 1) * TL]), waits=[m2], inc=True)
            ctoks.append(m3)
        scr_free = tuple(ctoks)
        gate_free = tuple(ctoks)
    stage2.__exit__(None, None, None)
    P.barrier()
    with P.scope():
        wo, two = load_weight_bf16(P, w["woT"], D, D, f"{name}wo")
        uo, tuo = load_weight_bf16(P, w["uoT"], D, D, f"{name}uo")
        bo, tbo = load_vec_cols(P, w["bo"], D, f"{name}bo")
        cfb = cbf
        casts = [P.op("vector", lambda v, j=j: v.tensor_copy(
            cfb[:, j * TL:(j + 1) * TL], cst[:, j * TL:(j + 1) * TL]),
            waits=ctoks, inc=True) for j in range(KT)]
        cuo = P.sbuf(f"{name}_cuo", [128, KT * TL], F32)
        pso = [P.psum(f"{name}_pso", [128, TL], F32) for _ in range(2)]
        pso_free = [(), ()]
        for o in range(KT):
            pi = o % 2
            pb = pso[pi]
            mm = None
            for j in range(KT):
                mm = P.op("tensor", lambda tt, o=o, j=j, pb=pb: tt.matmul(
                    pb[:, 0:TL], uo[:, j * D + o * 128:j * D + o * 128 + 128],
                    cfb[:, j * TL:(j + 1) * TL],
                    start=(j == 0), stop=(j == KT - 1)),
                    waits=([tuo] + casts + list(pso_free[pi])) if j == 0 else [],
                    inc=(j == KT - 1))
            ev = P.op("vector", lambda v, o=o, pb=pb: v.tensor_copy(
                cuo[:, o * TL:(o + 1) * TL], pb[:, 0:TL]), waits=[mm], inc=True)
            pso_free[pi] = (ev,)
        cuo_tok = ev
        lnb = P.sbuf(f"{name}_lnb", [128, KT * TL], BF16)
        hb = P.sbuf(f"{name}_hb", [128, KT * TL], F32)
        sln = P.dsems(1)[0]
        sho = P.dsems(1)[0]
        ln_free = ()
        hb_free = ()
        last = None
        for t in range(NB):
            lds = [P.dma("sync", lnb[:, j * TL:(j + 1) * TL],
                         lnss_ap[t, j * 128:(j + 1) * 128, 0:TL],
                         waits=ln_free, sem=sln) for j in range(KT)]
            otoks = []
            for o in range(KT):
                pi = o % 2
                pb = pso[pi]
                mm = None
                for j in range(KT):
                    mm = P.op("tensor", lambda tt, o=o, j=j, pb=pb: tt.matmul(
                        pb[:, 0:TL], wo[:, j * D + o * 128:j * D + o * 128 + 128],
                        lnb[:, j * TL:(j + 1) * TL],
                        start=(j == 0), stop=(j == KT - 1)),
                        waits=([two, cuo_tok] + lds + list(pso_free[pi])) if j == 0 else [],
                        inc=(j == KT - 1))
                a1 = P.op("vector", lambda v, o=o, pb=pb: v.tensor_add(
                    pb[:, 0:TL], pb[:, 0:TL], cuo[:, o * TL:(o + 1) * TL]),
                    waits=[mm], inc=True)
                a2 = P.op("scalar", lambda s, o=o, pb=pb: s.activation(
                    hb[:, o * TL:(o + 1) * TL], pb[:, 0:TL], AF.Tanh,
                    bias=bo[:, o:o + 1]),
                    waits=[a1, tbo] + list(hb_free), inc=True)
                pso_free[pi] = (a2,)
                st = P.dma("scalar", hss_ap[t, o * 128:(o + 1) * 128, 0:TL],
                           hb[:, o * TL:(o + 1) * TL], waits=[a2], sem=sho)
                otoks.append(a2)
                last = st
            ln_free = tuple(otoks)
            hb_free = tuple(otoks) + (last,)
    P.barrier()
    return last


DI = 2048
DIT = DI // 128


def emit_l0_conv(P, ctx, xz_ap, u_ap, Ttot, w, name="l0c"):
    cw, tcw = load_vec_cols(P, w["convw4"], 4 * DI, f"{name}w")
    cb, tcb = load_vec_cols(P, w["conv_b"], DI, f"{name}b")
    CW = TC + 3
    ub = [P.sbuf(f"{name}_u", [128, DIT * CW], BF16) for _ in range(2)]
    acc = P.sbuf(f"{name}_a", [128, TC], F32)
    ob = [P.sbuf(f"{name}_o", [128, DIT * TC], BF16) for _ in range(2)]
    su = P.dsems(2)
    so = P.dsems(2)
    free_u = [(), ()]
    free_o = [(), ()]
    acc_free = ()
    last = None
    for c in range(cdiv(Ttot, TC)):
        t0, tw = c * TC, min(TC, Ttot - c * TC)
        pp = c % 2
        hl = []
        for j in range(DIT):
            if c == 0:
                hl.append(P.dma("sync", ub[pp][:, j * CW + 3:j * CW + 3 + tw],
                                xz_ap[j * 128:(j + 1) * 128, 0:tw],
                                waits=free_u[pp], sem=su[pp]))
            else:
                hl.append(P.dma("sync", ub[pp][:, j * CW:j * CW + 3 + tw],
                                xz_ap[j * 128:(j + 1) * 128, t0 - 3:t0 + tw],
                                waits=free_u[pp], sem=su[pp]))
        if c == 0:
            zt = P.op("vector", lambda v, pp=pp: v.memset(
                ub[pp][:, 0:DIT * CW].rearrange("p (j w) -> p j w", w=CW)[:, :, 0:3], 0.0),
                waits=list(free_u[pp]), inc=True)
            hl.append(zt)
        otoks = []
        for j in range(DIT):
            t1 = P.op("vector", lambda v, j=j, pp=pp, tw=tw: v.tensor_scalar_mul(
                acc[:, 0:tw], ub[pp][:, j * CW + 0:j * CW + 0 + tw], cw[:, 0 * 16 + j:0 * 16 + j + 1]),
                waits=hl + [tcw] + list(acc_free), inc=True)
            for k in (1, 2, 3):
                t1 = P.op("vector", lambda v, j=j, k=k, pp=pp, tw=tw: v.scalar_tensor_tensor(
                    acc[:, 0:tw], ub[pp][:, j * CW + k:j * CW + k + tw],
                    cw[:, k * 16 + j:k * 16 + j + 1], acc[:, 0:tw],
                    ALU.mult, ALU.add), waits=[t1], inc=True)
            t2 = P.op("scalar", lambda s, j=j, pp=pp, tw=tw: s.activation(
                ob[pp][:, j * TC:j * TC + tw], acc[:, 0:tw], AF.Silu,
                bias=cb[:, j:j + 1]),
                waits=[t1, tcb] + list(free_o[pp]), inc=True)
            acc_free = (t2,)
            st = P.dma("scalar", u_ap[j * 128:(j + 1) * 128, t0:t0 + tw],
                       ob[pp][:, j * TC:j * TC + tw], waits=[t2], sem=so[pp])
            otoks.append(t2)
            last = st
        free_u[pp] = tuple(otoks)
        free_o[pp] = (last,)
    return last


def emit_l0_scan(P, ctx, u_ap, xz_ap, ln_ap, xin2_ap, Ttot, w, name="l0s"):
    xpT, txp = load_weight_bf16(P, w["xpT"], DI, 128, f"{name}xp")
    dtwT, tdtw = load_weight_bf16(P, w["dtwT"], 64 * 2 // 128 * 64 if False else 64, DI, f"{name}dtw") if False else (None, None)
    dtw = P.sbuf(f"{name}_dtw", [128, DI], BF16)
    with P.scope():
        stg = P.sbuf(f"{name}_dst", [128, DI], F32)
        l1 = P.dma("sync", stg[64:128, :], w["dtwT"], sem=P.dsems(1)[0])
        tdtw = P.op("vector", lambda v: v.tensor_copy(dtw[64:128, :], stg[64:128, :]),
                    waits=[l1], inc=True)
    dtb, tdtb = load_vec_cols(P, w["dtb"], DI, f"{name}dtb", waits=(tdtw,))
    dsk, tdsk = load_vec_cols(P, w["Dskip"], DI, f"{name}dsk", waits=(tdtw,))
    opT, topw = load_weight_bf16(P, w["outprojT"], DI, D, f"{name}op", waits=(tdtw,))
    sc_l = P.dsems(1)[0]
    rm = P.sbuf(f"{name}_rm", [128, 16 * 128], BF16)
    ym = P.sbuf(f"{name}_ym", [128, 16 * 128], BF16)
    eb = P.sbuf(f"{name}_eb", [48, 128], BF16)
    als = P.sbuf(f"{name}_als", [128, 256], F32)
    ct1 = P.dma("sync", rm[:, :], w["Rm"], sem=sc_l, waits=(tdtw,))
    ct2 = P.dma("sync", ym[:, :], w["Ym"], sem=sc_l)
    ct3 = P.dma("sync", eb[0:16, :], w["EB"], sem=sc_l)
    ct3b = P.dma("sync", eb[32:48, :], w["EB"], sem=sc_l)
    ct4 = P.dma("sync", als[:, :], w["A_scan"], sem=sc_l)
    a1 = P.op("scalar", lambda s: s.activation(als[:, :], als[:, :], AF.Exp),
              waits=[ct1, ct2, ct3, ct3b, ct4], inc=True)
    atok = P.op("vector", lambda v: v.tensor_scalar_mul(als[:, :], als[:, :], -1.0),
                waits=[a1], inc=True)
    carry = P.sbuf(f"{name}_cy", [128, 256], F32)
    tcy = P.op("vector", lambda v: v.memset(carry[:, :], 0.0), inc=True)
    ub = P.sbuf(f"{name}_u", [128, DIT * TC], BF16)
    zb = P.sbuf(f"{name}_z", [128, DIT * TC], BF16)
    lnb = P.sbuf(f"{name}_ln", [128, KT * TC], F32)
    dbc = P.sbuf(f"{name}_dbc", [128, TC], BF16)
    dtbf = P.sbuf(f"{name}_dtbf", [128, DIT * TC], BF16)
    dts = P.sbuf(f"{name}_dt", [128, DIT * TC], F32)
    dut = P.sbuf(f"{name}_du", [128, TC], BF16)
    bbc = P.sbuf(f"{name}_bb", [128, TC], F32)
    cbc = P.sbuf(f"{name}_cb", [128, TC], F32)
    asb = P.sbuf(f"{name}_as", [128, TC], F32)
    bsb = P.sbuf(f"{name}_bs", [128, TC], F32)
    ssb = P.sbuf(f"{name}_ss", [128, TC], F32)
    scb = P.sbuf(f"{name}_scb", [128, TC], BF16)
    yvb = P.sbuf(f"{name}_yvb", [128, TC], F32)
    ygs = P.sbuf(f"{name}_yg", [128, DIT * TC], BF16)
    zss = P.sbuf(f"{name}_zs", [128, TC], F32)
    scr = P.sbuf(f"{name}_scr", [128, TC], F32)
    xo = P.sbuf(f"{name}_xo", [128, KT * TC], F32)
    ps_g = [P.psum(f"{name}_pg", [128, TC], F32) for _ in range(2)]
    ps_dt = P.psum(f"{name}_pdt", [128, TC], F32)
    ps_du = P.psum(f"{name}_pdu", [128, TC], F32)
    ps_y = P.psum(f"{name}_py", [128, TC], F32)
    ps_bc = P.psum(f"{name}_pbc", [128, TC], F32)
    sus = P.dsems(2)
    szs = P.dsems(2)
    slns = P.dsems(2)
    sxo = P.dsems(1)[0]
    free_u = ()
    free_z = ()
    free_ln = ()
    psg_free = [(), ()]
    dbc_free = ()
    dt_free = ()
    dtbf_free = ()
    bc_free = ()
    scan_free = ()
    y_free = ()
    yg_free = ()
    xo_free = ()
    carry_tok = tcy
    last = None
    for c in range(cdiv(Ttot, TC)):
        t0, tw = c * TC, min(TC, Ttot - c * TC)
        lu = [P.dma("sync", ub[:, j * TC:j * TC + tw],
                    u_ap[j * 128:(j + 1) * 128, t0:t0 + tw],
                    waits=free_u, sem=sus[c % 2]) for j in range(DIT)]
        lz = [P.dma("gpsimd", zb[:, j * TC:j * TC + tw],
                    xz_ap[(DIT + j) * 128:(DIT + j + 1) * 128, t0:t0 + tw],
                    waits=free_z, sem=szs[c % 2]) for j in range(DIT)]
        lln = [P.dma("scalar", lnb[:, j * TC:j * TC + tw],
                     ln_ap[j * 128:(j + 1) * 128, t0:t0 + tw],
                     waits=free_ln, sem=slns[c % 2]) for j in range(KT)]
        mm = None
        for j in range(DIT):
            mm = P.op("tensor", lambda t, j=j, tw=tw: t.matmul(
                ps_bc[:, 0:tw], xpT[:, j * 128:(j + 1) * 128],
                ub[:, j * TC:j * TC + tw],
                start=(j == 0), stop=(j == DIT - 1)),
                waits=([txp] + lu + list(bc_free)) if j == 0 else [],
                inc=(j == DIT - 1))
        dv = P.op("vector", lambda v, tw=tw: v.tensor_copy(dbc[:, 0:tw], ps_bc[:, 0:tw]),
                  waits=[mm] + list(dbc_free), inc=True)
        bm = P.op("tensor", lambda t, tw=tw: t.matmul(
            ps_bc[:, 0:tw], eb[0:16, :], dbc[0:16, 0:tw], start=True, stop=True),
            waits=[dv], inc=True)
        bv = P.op("vector", lambda v, tw=tw: v.tensor_copy(bbc[:, 0:tw], ps_bc[:, 0:tw]),
                  waits=[bm] + list(bc_free), inc=True)
        cm = P.op("tensor", lambda t, tw=tw: t.matmul(
            ps_bc[:, 0:tw], eb[32:48, :], dbc[32:48, 0:tw], start=True, stop=True),
            waits=[bv], inc=True)
        cv = P.op("vector", lambda v, tw=tw: v.tensor_copy(cbc[:, 0:tw], ps_bc[:, 0:tw]),
                  waits=[cm], inc=True)
        bc_free = (cv,)
        dt_toks = []
        for jt in range(DIT):
            pi = jt % 2
            pb = ps_g[pi]
            m1 = P.op("tensor", lambda t, jt=jt, pb=pb, tw=tw: t.matmul(
                pb[:, 0:tw], dtw[64:128, jt * 128:(jt + 1) * 128], dbc[64:128, 0:tw],
                start=True, stop=True),
                waits=[tdtw, dv] + list(psg_free[pi]), inc=True)
            r1 = P.op("vector", lambda v, jt=jt, pb=pb, tw=tw: v.tensor_scalar(
                dts[:, jt * TC:jt * TC + tw], pb[:, 0:tw], dtb[:, jt:jt + 1], None,
                ALU.add), waits=[m1, tdtb] + list(dt_free), inc=True)
            a2 = P.op("scalar", lambda s, tw=tw: s.activation(
                scr[:, 0:tw], dts[0:128, 0:tw] if False else dts[:, 0:tw], AF.Abs),
                waits=[r1], inc=True) if False else None
            a2 = P.op("scalar", lambda s, jt=jt, tw=tw: s.activation(
                scr[:, 0:tw], dts[:, jt * TC:jt * TC + tw], AF.Abs),
                waits=[r1] + list(scan_free), inc=True)
            a3 = P.op("scalar", lambda s, tw=tw: s.activation(
                scr[:, 0:tw], scr[:, 0:tw], AF.Exp, scale=-1.0), waits=[a2], inc=True)
            one_t, one_tok = constf(ctx, 1.0)
            a4 = P.op("scalar", lambda s, tw=tw, one_t=one_t: s.activation(
                scr[:, 0:tw], scr[:, 0:tw], AF.Ln, bias=one_t[:, 0:1]),
                waits=[a3, one_tok], inc=True)
            r2 = P.op("vector", lambda v, jt=jt, tw=tw: v.tensor_scalar_max(
                dts[:, jt * TC:jt * TC + tw], dts[:, jt * TC:jt * TC + tw], 0.0),
                waits=[a2], inc=True)
            r3 = P.op("vector", lambda v, jt=jt, tw=tw: v.tensor_add(
                dts[:, jt * TC:jt * TC + tw], dts[:, jt * TC:jt * TC + tw],
                scr[:, 0:tw]), waits=[r2, a4], inc=True)
            r4 = P.op("vector", lambda v, jt=jt, tw=tw: v.tensor_copy(
                dtbf[:, jt * TC:jt * TC + tw], dts[:, jt * TC:jt * TC + tw]),
                waits=[r3] + list(dtbf_free), inc=True)
            psg_free[pi] = (r1,)
            scan_free = (r4,)
            dt_toks.append(r4)
        yg_toks = []
        prev_yg = None
        for jt in range(DIT):
            du = P.op("vector", lambda v, jt=jt, tw=tw: v.tensor_tensor(
                dut[:, 0:tw], dts[:, jt * TC:jt * TC + tw], ub[:, jt * TC:jt * TC + tw],
                ALU.mult), waits=[dt_toks[jt]] + list(dt_free), inc=True)
            ymm_last = None
            prev_ae = None
            for m in range(16):
                j = jt * 16 + m
                b1 = P.op("tensor", lambda t, jt=jt, m=m, tw=tw: t.matmul(
                    ps_dt[:, 0:tw], rm[:, m * 128:(m + 1) * 128],
                    dtbf[:, jt * TC:jt * TC + tw], start=True, stop=True),
                    waits=[dt_toks[jt], atok, prev_ae] + list(dt_free), inc=True)
                b2 = P.op("tensor", lambda t, m=m, tw=tw: t.matmul(
                    ps_du[:, 0:tw], rm[:, m * 128:(m + 1) * 128], dut[:, 0:tw],
                    start=True, stop=True), waits=[du] + list(y_free), inc=True)
                ae = P.op("scalar", lambda s, j=j, tw=tw: s.activation(
                    asb[:, 0:tw], ps_dt[:, 0:tw], AF.Exp, scale=als[:, j:j + 1]),
                    waits=[b1] + list(scan_free), inc=True)
                prev_ae = ae
                be = P.op("vector", lambda v, tw=tw: v.tensor_tensor(
                    bsb[:, 0:tw], ps_du[:, 0:tw], bbc[:, 0:tw], ALU.mult),
                    waits=[b2, bv] + list(scan_free), inc=True)
                sc_ = P.op("vector", lambda v, j=j, tw=tw: v.tensor_tensor_scan(
                    ssb[:, 0:tw], asb[:, 0:tw], bsb[:, 0:tw], carry[:, j:j + 1],
                    ALU.mult, ALU.add), waits=[ae, be, carry_tok], inc=True)
                cy = P.op("vector", lambda v, j=j, tw=tw: v.tensor_copy(
                    carry[:, j:j + 1], ssb[:, tw - 1:tw]), waits=[sc_], inc=True)
                carry_tok = cy
                sm = P.op("gpsimd", lambda g, tw=tw: g.tensor_tensor(
                    scb[:, 0:tw], ssb[:, 0:tw], cbc[:, 0:tw], ALU.mult),
                    waits=[sc_, cv], inc=True)
                ymm_last = P.op("tensor", lambda t, m=m, tw=tw: t.matmul(
                    ps_y[:, 0:tw], ym[:, m * 128:(m + 1) * 128], scb[:, 0:tw],
                    start=(m == 0), stop=(m == 15)),
                    waits=[sm] + (list(yg_free) if m == 0 else []), inc=(m == 15))
                scan_free = (ymm_last,)
                y_free = (ymm_last,)
            zs = P.op("scalar", lambda s, jt=jt, tw=tw: s.activation(
                zss[:, 0:tw], zb[:, jt * TC:jt * TC + tw], AF.Silu),
                waits=lz + list(yg_free), inc=True)
            yv = P.op("vector", lambda v, jt=jt, tw=tw: v.scalar_tensor_tensor(
                yvb[:, 0:tw], ub[:, jt * TC:jt * TC + tw], dsk[:, jt:jt + 1],
                ps_y[:, 0:tw], ALU.mult, ALU.add),
                waits=[ymm_last, tdsk, sm, prev_yg], inc=True)
            yg = P.op("vector", lambda v, jt=jt, tw=tw: v.tensor_tensor(
                ygs[:, jt * TC:jt * TC + tw], yvb[:, 0:tw], zss[:, 0:tw], ALU.mult),
                waits=[yv, zs] + list(yg_free), inc=True)
            prev_yg = yg
            yg_free = (yg,)
            yg_toks.append(yg)
        free_u = tuple(yg_toks)
        free_z = tuple(yg_toks)
        dt_free = tuple(yg_toks)
        dtbf_free = tuple(yg_toks)
        otoks = []
        for o in range(KT):
            pi = o % 2
            pb = ps_g[pi]
            mm = None
            for j in range(DIT):
                mm = P.op("tensor", lambda t, o=o, j=j, pb=pb, tw=tw: t.matmul(
                    pb[:, 0:tw], opT[:, j * D + o * 128:j * D + o * 128 + 128],
                    ygs[:, j * TC:j * TC + tw],
                    start=(j == 0), stop=(j == DIT - 1)),
                    waits=([topw] + yg_toks + list(psg_free[pi])) if j == 0 else [],
                    inc=(j == DIT - 1))
            e1 = P.op("vector", lambda v, o=o, pb=pb, tw=tw: v.tensor_add(
                xo[:, o * TC:o * TC + tw], lnb[:, o * TC:o * TC + tw], pb[:, 0:tw]),
                waits=[mm] + lln + list(xo_free), inc=True)
            psg_free[pi] = (e1,)
            st = P.dma("scalar", xin2_ap[o * 128:(o + 1) * 128, t0:t0 + tw],
                       xo[:, o * TC:o * TC + tw], waits=[e1], sem=sxo)
            otoks.append(e1)
            last = st
        free_ln = tuple(otoks)
        xo_free = tuple(otoks) + (last,)
        dbc_free = (dv,) if False else tuple(dt_toks) + (bm, cm)
    return last


def emit_l0_attn(P, ctx, ln_ap, xold_ap, xnew_ap, Ttot, w, name="l0a"):
    x0 = P.dram(f"{name}_x0", [D, Ttot], BF16)
    xz = P.dram(f"{name}_xz", [2 * DI, Ttot], BF16)
    u = P.dram(f"{name}_u", [DI, Ttot], BF16)
    xin2 = P.dram(f"{name}_xi", [D, Ttot], F32)
    z2 = P.dram(f"{name}_z2", [D, Ttot], BF16)
    h1 = P.dram(f"{name}_h1", [DI, Ttot], BF16)
    h2 = P.dram(f"{name}_h2", [DI, Ttot], BF16)
    h3 = P.dram(f"{name}_h3", [D, Ttot], F32)
    with P.scope():
        g1, tg1 = load_vec_cols(P, w["rms1_w"], D, f"{name}r1")
        emit_ln(P, ctx, ln_ap, x0, Ttot, out_dtype=BF16, gamma=g1, rms=True,
                eps=1e-6, wait_x=(tg1,), name=f"{name}rm1")
    P.barrier()
    with P.scope():
        emit_linear_fm(P, ctx, x0, Ttot, D, 2 * DI, w["inprojT"], f"{name}ip",
                       out_ap=xz, out_dtype=BF16)
    P.barrier()
    with P.scope():
        emit_l0_conv(P, ctx, xz, u, Ttot, w, name=f"{name}cv")
    P.barrier()
    with P.scope():
        emit_l0_scan(P, ctx, u, xz, ln_ap, xin2, Ttot, w, name=f"{name}sc")
    P.barrier()
    with P.scope():
        g2, tg2 = load_vec_cols(P, w["rms2_w"], D, f"{name}r2")
        emit_ln(P, ctx, xin2, z2, Ttot, out_dtype=BF16, gamma=g2, rms=True,
                eps=1e-6, wait_x=(tg2,), name=f"{name}rm2")
    P.barrier()
    with P.scope():
        b1, tb1 = load_vec_cols(P, w["fc1_b"], DI, f"{name}b1")
        emit_linear_fm(P, ctx, z2, Ttot, D, DI, w["fc1T"], f"{name}f1",
                       bias=(b1, tb1), act=AF.Relu, out_ap=h1, out_dtype=BF16)
    P.barrier()
    with P.scope():
        b2, tb2 = load_vec_cols(P, w["fc2_b"], DI, f"{name}b2")
        emit_linear_fm(P, ctx, h1, Ttot, DI, DI, w["fc2T"], f"{name}f2",
                       bias=(b2, tb2), act=AF.Relu, out_ap=h2, out_dtype=BF16)
    P.barrier()
    with P.scope():
        b3, tb3 = load_vec_cols(P, w["fc3_b"], D, f"{name}b3")
        emit_linear_fm(P, ctx, h2, Ttot, DI, D, w["fc3T"], f"{name}f3",
                       bias=(b3, tb3), out_ap=h3, out_dtype=F32)
    P.barrier()
    with P.scope():
        t = emit_add3(P, ctx, xold_ap, xin2, h3, xnew_ap, Ttot, name=f"{name}a3")
    P.barrier()
    return t


def emit_shift(P, ctx, x_ap, out_ap, Ttot, name="shf"):
    zc = P.sbuf(f"{name}_z", [128, 1], F32)
    zt = P.op("vector", lambda v: v.memset(zc[:, :], 0.0), inc=True)
    sd, zsem = P.dsems(2)
    toks = []
    for j in range(KT):
        if j < 4:
            toks.append(P.dma("sync", out_ap[j * 128:(j + 1) * 128, 0:Ttot],
                              x_ap[j * 128:(j + 1) * 128, 0:Ttot], sem=sd))
        else:
            toks.append(P.dma("sync", out_ap[j * 128:(j + 1) * 128, 1:Ttot],
                              x_ap[j * 128:(j + 1) * 128, 0:Ttot - 1], sem=sd))
            toks.append(P.dma("scalar", out_ap[j * 128:(j + 1) * 128, 0:1],
                              zc[:, 0:1], waits=[zt], sem=zsem,
                              allow_slow_non_contiguous=True))
    return toks


def emit_l3f_attn(P, ctx, xs_ap, x_ap, xout_ap, Ttot, w, li, name="l3fa"):
    nch = cdiv(Ttot, TC)
    wqkv, twq = load_weight_bf16(P, w["Wqkv"][li], D, 3 * D, f"{name}w")
    qs_sb, tqs = load_vec_cols(P, w["q_scale_x"][li], D, f"{name}qs", waits=(twq,))
    ks_sb, tks = load_vec_cols(P, w["k_scale_x"][li], D, f"{name}ks", waits=(twq,))
    idb = P.sbuf(f"{name}_id", [128, 128], BF16)
    tid = P.dma("sync", idb[:, :], w["ident_bf"], sem=P.dsems(1)[0], waits=(twq,))
    selo = P.sbuf(f"{name}_sel", [128, 8 * 16], BF16)
    selt = P.sbuf(f"{name}_selt", [16, 8 * 128], BF16)
    rot = P.sbuf(f"{name}_rot", [128, 128], BF16)
    msk = P.sbuf(f"{name}_msk", [128, 3 * 384], F32)
    sc_l = P.dsems(1)[0]
    tc1 = P.dma("sync", selo[:, :], w["sel_o"], sem=sc_l, waits=(twq,))
    tc2 = P.dma("sync", selt[:, :], w["sel_t"], sem=sc_l)
    tc3 = P.dma("sync", rot[:, :], w["rot_m"], sem=sc_l)
    tc4 = P.dma("sync", msk[:, :], w["band_masks"], sem=sc_l)
    ctoks = [twq, tqs, tks, tid, tc1, tc2, tc3, tc4]

    xbf = [P.sbuf(f"{name}_xbf", [128, KT * TC], BF16) for _ in range(2)]
    rb = [P.sbuf(f"{name}_r", [128, KT * TC], F32)] * 2
    qf = [P.sbuf(f"{name}_qf", [128, KT * TC], BF16) for _ in range(2)]
    kcur = [P.sbuf(f"{name}_kc", [128, KT * TC], BF16) for _ in range(2)]
    khal = [P.sbuf(f"{name}_kh", [128, KT * 256], BF16) for _ in range(2)]
    vcur = [P.sbuf(f"{name}_vc", [128, 4 * D], BF16) for _ in range(2)]
    vhal = [P.sbuf(f"{name}_vh", [128, 2 * D], BF16) for _ in range(2)]
    wob = [P.sbuf(f"{name}_wob", [64, 16 * 128], BF16) for _ in range(2)]
    rope = P.sbuf(f"{name}_rp", [128, 4 * TC], BF16)
    scr = P.sbuf(f"{name}_s1", [128, TC], F32)
    scr2 = P.sbuf(f"{name}_s2", [128, TC], F32)
    qa = P.sbuf(f"{name}_qa", [128, TC], BF16)
    ssn = P.sbuf(f"{name}_ssn", [16, TC], F32)
    ssnb = P.sbuf(f"{name}_ssnb", [16, TC], BF16)
    scrb = P.sbuf(f"{name}_scrb", [128, TC], BF16)
    smm = [P.sbuf(f"{name}_smm", [128, 768], F32) for _ in range(2)]
    pex = [P.sbuf(f"{name}_pex", [128, 768], BF16) for _ in range(2)]
    nmx = [P.sbuf(f"{name}_nmx", [128, 4], F32) for _ in range(2)]
    ptr = [P.sbuf(f"{name}_ptr", [128, 3 * 128], BF16) for _ in range(2)]
    afh = P.sbuf(f"{name}_af", [64, 16 * TC], BF16)
    psP = [P.psum(f"{name}_pP", [128, TC], F32) for _ in range(2)]
    psQ0 = P.psum(f"{name}_pQ0", [128, TC], F32)
    psQ1 = P.psum(f"{name}_pQ1", [128, TC], F32)
    psQ2 = P.psum(f"{name}_pQ2", [128, TC], F32)
    psQ3 = P.psum(f"{name}_pQ3", [128, TC], F32)
    ps_tr = [P.psum(f"{name}_ptp", [128, 128], BF16) for _ in range(2)]
    sx = P.dsems(2)
    sr = P.dsems(2)
    srp = P.dsems(2)
    sst = P.dsems(2)
    swo = P.dsems(2)
    zt1 = P.op("vector", lambda v: v.memset(khal[0][:, :], 0.0), inc=True)
    zt2 = P.op("vector", lambda v: v.memset(vhal[0][:, :], 0.0), inc=True)
    bfree = {}

    def bf(b):
        return list(bfree.get(id(b), ()))

    def setbf(b, *toks):
        bfree[id(b)] = tuple(t for t in toks if t)

    sbuf_free = {}

    def sf(key):
        return list(sbuf_free.get(key, ()))

    def setsf(key, *toks):
        sbuf_free[key] = tuple(t for t in toks if t)

    setsf(("kh", 0), zt1)
    setsf(("vh", 0), zt2)
    free_r = [(), ()]
    free_rp = [(), ()]
    last = None
    pair_idx = 0
    for c in range(nch):
        t0, tw = c * TC, min(TC, Ttot - c * TC)
        pp = c % 2
        nts = tw // 128
        cst = []
        for j in range(KT):
            wx = tuple(sf(("xbf", pp)))
            if j >= 4:
                if c == 0:
                    cst.append(P.dma("gpsimd", xbf[pp][:, j * TC + 1:j * TC + tw],
                                     xs_ap[j * 128:(j + 1) * 128, 0:tw - 1],
                                     waits=wx, sem=sx[pp]))
                else:
                    cst.append(P.dma("gpsimd", xbf[pp][:, j * TC:j * TC + tw],
                                     xs_ap[j * 128:(j + 1) * 128, t0 - 1:t0 + tw - 1],
                                     waits=wx, sem=sx[pp]))
            else:
                cst.append(P.dma("gpsimd", xbf[pp][:, j * TC:j * TC + tw],
                                 xs_ap[j * 128:(j + 1) * 128, t0:t0 + tw],
                                 waits=wx, sem=sx[pp]))
        if c == 0:
            cst.append(P.op("vector", lambda v, pp=pp: v.memset(
                xbf[pp][:, 0:KT * TC].rearrange("p (j t) -> p j t", t=TC)[:, 4:, 0:1],
                0.0), waits=sf(("xbf", pp)), inc=True))
        rl = [P.dma("gpsimd", rb[pp][:, j * TC:j * TC + tw],
                    x_ap[j * 128:(j + 1) * 128, t0:t0 + tw],
                    waits=free_r[pp], sem=sr[pp]) for j in range(KT)]
        lrp = [P.dma("scalar", rope[:, i * TC:i * TC + tw],
                     w["rope4"][i, 0:128, t0:t0 + tw],
                     waits=free_rp[pp], sem=srp[pp]) for i in range(4)]
        qk_toks = []
        for which in (0, 1):
            ss_last = None
            raw_toks = []
            for o in range(KT):
                pb = psP[o % 2]
                mm = None
                for j in range(KT):
                    mm = P.op("tensor", lambda t, o=o, j=j, pb=pb, pp=pp, tw=tw, which=which: t.matmul(
                        pb[:, 0:tw],
                        wqkv[:, j * 3 * D + which * D + o * 128:j * 3 * D + which * D + o * 128 + 128],
                        xbf[pp][:, j * TC:j * TC + tw],
                        start=(j == 0), stop=(j == KT - 1)),
                        waits=(ctoks + cst + bf(pb)) if j == 0 else [],
                        inc=(j == KT - 1))
                tgt = (qf[pp][:, o * TC:o * TC + tw] if which == 0
                       else kcur[pp][:, o * TC:o * TC + tw])
                st_ = P.op("vector", lambda v, pb=pb, tgt=tgt: v.tensor_copy(
                    tgt, pb[:, 0:tgt.shape[-1]]),
                    waits=[mm] + sf(("qk", pp, which, o)) + sf("qa"), inc=True)
                e1 = P.op("scalar", lambda s, tgt=tgt, tw=tw: s.activation(
                    scrb[:, 0:tw], tgt, AF.Square),
                    waits=[st_] + sf("scrb"), inc=True)
                raw_toks.append(st_)
                setbf(pb, e1, st_)
                ss_last = P.op("tensor", lambda t, o=o, tw=tw: t.matmul(
                    psQ2[0:16, 0:tw], selo[:, o * 16:(o + 1) * 16], scrb[:, 0:tw],
                    start=(o == 0), stop=(o == KT - 1)),
                    waits=([e1] + (bf(psQ2) if o == 0 else [])),
                    inc=(o == KT - 1))
                setsf("scrb", ss_last)
            n1 = P.op("vector", lambda v, tw=tw: v.tensor_copy(
                ssn[:, 0:tw], psQ2[0:16, 0:tw]), waits=[ss_last], inc=True)
            setbf(psQ2, n1)
            n2 = P.op("scalar", lambda s, tw=tw: s.activation(
                ssn[:, 0:tw], ssn[:, 0:tw], AF.Sqrt), waits=[n1], inc=True)
            n3 = P.op("vector", lambda v, tw=tw: v.tensor_scalar_max(
                ssn[:, 0:tw], ssn[:, 0:tw], 1e-12), waits=[n2], inc=True)
            n4 = P.op("vector", lambda v, tw=tw: v.reciprocal(
                ssn[:, 0:tw], ssn[:, 0:tw]), waits=[n3], inc=True)
            n5 = P.op("vector", lambda v, tw=tw: v.tensor_copy(
                ssnb[:, 0:tw], ssn[:, 0:tw]), waits=[n4] + sf("ssnb"), inc=True)
            for o in range(KT):
                tgt = (qf[pp][:, o * TC:o * TC + tw] if which == 0
                       else kcur[pp][:, o * TC:o * TC + tw])
                bmx = P.op("tensor", lambda t, o=o, tw=tw: t.matmul(
                    psQ0[:, 0:tw], selt[:, o * 128:(o + 1) * 128], ssnb[:, 0:tw],
                    start=True, stop=True), waits=[n5] + bf(psQ0), inc=True)
                setsf("ssnb", bmx)
                u1 = P.op("vector", lambda v, tw=tw, tgt=tgt: v.tensor_tensor(
                    scr[:, 0:tw], tgt, psQ0[:, 0:tw], ALU.mult),
                    waits=[bmx, raw_toks[o]] + sf("scr"), inc=True)
                setbf(psQ0, u1)
                sccol = qs_sb if which == 0 else ks_sb
                u2 = P.op("vector", lambda v, o=o, tw=tw, sccol=sccol: v.tensor_scalar_mul(
                    qa[:, 0:tw], scr[:, 0:tw], sccol[:, o:o + 1]),
                    waits=[u1, tqs, tks] + sf("qa"), inc=True)
                rtm = P.op("tensor", lambda t, tw=tw: t.matmul(
                    psQ1[:, 0:tw], rot[:, :], qa[:, 0:tw], start=True, stop=True),
                    waits=[u2] + bf(psQ1), inc=True)
                ci = 0 if which == 0 else 2
                u3 = P.op("vector", lambda v, tw=tw, ci=ci: v.tensor_tensor(
                    scr[:, 0:tw], qa[:, 0:tw], rope[:, ci * TC:ci * TC + tw], ALU.mult),
                    waits=[rtm] + lrp + [u1], inc=True)
                u4 = P.op("vector", lambda v, tw=tw, ci=ci: v.tensor_tensor(
                    scr2[:, 0:tw], psQ1[:, 0:tw], rope[:, (ci + 1) * TC:(ci + 1) * TC + tw],
                    ALU.mult), waits=[rtm] + lrp + sf("scr2"), inc=True)
                setbf(psQ1, u4)
                u5 = P.op("vector", lambda v, tw=tw, tgt=tgt: v.tensor_add(
                    tgt, scr[:, 0:tw], scr2[:, 0:tw]), waits=[u3, u4], inc=True)
                setsf("scr", u5)
                setsf("scr2", u5)
                setsf("qa", u3, rtm)
                qk_toks.append(u5)
        rk = None
        if c + 1 < nch:
            np_ = (c + 1) % 2
            rk = P.op("vector", lambda v, pp=pp, np_=np_: v.tensor_copy(
                khal[np_][:, :].rearrange("p (j w) -> p j w", w=256),
                kcur[pp][:, :].rearrange("p (j t) -> p j t", t=TC)[:, :, 256:512]),
                waits=qk_toks + sf(("kh", np_)), inc=True)
            setsf(("khro", np_), rk)
        v_toks = []
        for ts_ in range(nts):
            for nh in range(2):
                mm = None
                for j in range(KT):
                    mm = P.op("tensor", lambda t, ts_=ts_, nh=nh, j=j, pp=pp: t.matmul(
                        psQ0[:, 0:512],
                        xbf[pp][:, j * TC + ts_ * 128:j * TC + ts_ * 128 + 128],
                        wqkv[:, j * 3 * D + 2 * D + nh * 512:j * 3 * D + 2 * D + nh * 512 + 512],
                        start=(j == 0), stop=(j == KT - 1)),
                        waits=(cst + bf(psQ0)) if j == 0 else [],
                        inc=(j == KT - 1))
                ev = P.op("scalar", lambda s, ts_=ts_, nh=nh, pp=pp: s.activation(
                    vcur[pp][:, ts_ * D + nh * 512:ts_ * D + nh * 512 + 512],
                    psQ0[:, 0:512], AF.Identity),
                    waits=[mm] + sf(("vc", pp)), inc=True)
                setbf(psQ0, ev)
                v_toks.append(ev)
        rv = None
        if c + 1 < nch:
            np_ = (c + 1) % 2
            rv = P.op("vector", lambda v, pp=pp, np_=np_: v.tensor_copy(
                vhal[np_][:, 0:2 * D], vcur[pp][:, 2 * D:4 * D]),
                waits=v_toks + sf(("vh", np_)), inc=True)
            setsf(("vhro", np_), rv)
        setsf(("xbf", pp), *(tuple(v_toks) + tuple(qk_toks)))
        free_rp[pp] = tuple(qk_toks)
        halo_wait = sf(("khro", pp)) + sf(("vhro", pp)) if c > 0 else [zt1, zt2]
        af_toks = []
        kread = []
        vread = []
        for ts_ in range(nts):
            gq = c * 4 + ts_
            mvar = 0 if gq == 0 else (1 if gq == 1 else 2)
            for hp in range(8):
                pi = pair_idx % 2
                pair_idx += 1
                simbanks = (psQ0, psQ1) if pi == 0 else (psQ3, psP[1])
                pvbank = psQ2 if pi == 0 else psP[0]
                trbank = ps_tr[pi]
                sm_, px_, nm_, pt_ = smm[pi], pex[pi], nmx[pi], ptr[pi]
                sims = []
                for hh in range(2):
                    h = 2 * hp + hh
                    kt_, bs = h // 2, 64 * (h % 2)
                    pq = simbanks[hh]
                    mmq = None
                    segs = []
                    if ts_ < 2:
                        hw_ = (2 - ts_) * 128
                        segs.append(("h", ts_ * 128, hw_, 0))
                        segs.append(("c", 0, 384 - hw_, hw_))
                    else:
                        segs.append(("c", (ts_ - 2) * 128, 384, 0))
                    first = True
                    for (kind, koff, w_, ooff) in segs:
                        src = khal[pp] if kind == "h" else kcur[pp]
                        soff = (kt_ * 256 + koff) if kind == "h" else (kt_ * TC + koff)
                        mmq = P.op("tensor", lambda t, bs=bs, pq=pq, src=src, soff=soff,
                                   w_=w_, ooff=ooff, kt_=kt_, ts_=ts_, pp=pp, first=first: t.matmul(
                            pq[:, ooff:ooff + w_],
                            qf[pp][bs:bs + 64, kt_ * TC + ts_ * 128:kt_ * TC + ts_ * 128 + 128],
                            src[bs:bs + 64, soff:soff + w_],
                            start=True, stop=True),
                            waits=(qk_toks + halo_wait + bf(pq)) if first else [],
                            inc=not first or len(segs) == 1)
                        if len(segs) == 2 and first:
                            pass
                        first = False
                    if len(segs) == 2:
                        pass
                    sims.append(mmq)
                    kread.append(mmq)
                mks = []
                for hh in range(2):
                    pq = simbanks[hh]
                    mks.append(P.op("vector", lambda v, hh=hh, mvar=mvar, pq=pq, sm_=sm_: v.scalar_tensor_tensor(
                        sm_[:, hh * 384:hh * 384 + 384], pq[:, 0:384],
                        8.0, msk[:, mvar * 384:mvar * 384 + 384], ALU.mult, ALU.add),
                        waits=[sims[hh]] + sf(("smm", pi)), inc=True))
                    setbf(pq, mks[hh])
                r1 = P.op("vector", lambda v, sm_=sm_, nm_=nm_: v.tensor_reduce(
                    nm_[:, 0:2].unsqueeze(2), sm_[:, 0:768].rearrange("p (a b) -> p a b", b=384),
                    AX.X, ALU.max, negate=True), waits=mks + sf(("nmx", pi)), inc=True)
                exps = []
                for hh in range(2):
                    exps.append(P.op("scalar", lambda s, hh=hh, sm_=sm_, px_=px_, nm_=nm_: s.activation(
                        px_[:, hh * 384:hh * 384 + 384], sm_[:, hh * 384:hh * 384 + 384],
                        AF.Exp, bias=nm_[:, hh:hh + 1], accum_out=nm_[:, 2 + hh:3 + hh]),
                        waits=[r1] + sf(("pex", pi)), inc=True))
                setsf(("smm", pi), *exps)
                r2 = P.op("vector", lambda v, nm_=nm_: v.reciprocal(
                    nm_[:, 2:4], nm_[:, 2:4]), waits=exps, inc=True)
                setsf(("nmx", pi), r2)
                sc1 = []
                for hh in range(2):
                    sc1.append(P.op("vector", lambda v, hh=hh, px_=px_, nm_=nm_: v.tensor_scalar_mul(
                        px_[:, hh * 384:hh * 384 + 384], px_[:, hh * 384:hh * 384 + 384],
                        nm_[:, 2 + hh:3 + hh]), waits=[r2], inc=True))
                setsf(("pex", pi), *sc1)
                for hh in range(2):
                    h = 2 * hp + hh
                    pvmm = None
                    for b3 in range(3):
                        g = ts_ + b3
                        vsrc = vhal[pp] if g < 2 else vcur[pp]
                        voff = (g * D if g < 2 else (g - 2) * D) + h * 64
                        tr1 = P.op("tensor", lambda t, hh=hh, b3=b3, px_=px_, trbank=trbank: t.transpose(
                            trbank[:, :], px_[:, hh * 384 + b3 * 128:hh * 384 + b3 * 128 + 128],
                            idb[:, :]), waits=[sc1[hh]] + bf(trbank), inc=True)
                        ev_eng = "scalar" if (b3 % 2 == 0) else "vector"
                        trcp = (lambda s, b3=b3, pt_=pt_, trbank=trbank: s.copy(
                                    pt_[:, b3 * 128:b3 * 128 + 128], trbank[:, :])) \
                            if ev_eng == "scalar" else \
                               (lambda v, b3=b3, pt_=pt_, trbank=trbank: v.tensor_copy(
                                    pt_[:, b3 * 128:b3 * 128 + 128], trbank[:, :]))
                        tr2 = P.op(ev_eng, trcp,
                                   waits=[tr1] + sf(("ptr", pi, b3)), inc=True)
                        setbf(trbank, tr2)
                        pvmm = P.op("tensor", lambda t, b3=b3, vsrc=vsrc, voff=voff,
                                    pt_=pt_, pvbank=pvbank: t.matmul(
                            pvbank[0:64, 0:128],
                            vsrc[:, voff:voff + 64],
                            pt_[:, b3 * 128:b3 * 128 + 128],
                            start=(b3 == 0), stop=(b3 == 2)),
                            waits=([tr2] + halo_wait + v_toks + (bf(pvbank) if b3 == 0 else [])),
                            inc=(b3 == 2))
                        setsf(("ptr", pi, b3), pvmm)
                        vread.append(pvmm)
                    ev = P.op("scalar", lambda s, h=h, ts_=ts_, pvbank=pvbank: s.activation(
                        afh[0:64, h * TC + ts_ * 128:h * TC + ts_ * 128 + 128],
                        pvbank[0:64, 0:128], AF.Identity),
                        waits=[pvmm] + sf("af"), inc=True)
                    setbf(pvbank, ev)
                    af_toks.append(ev)
        setsf(("kh", pp), *kread)
        setsf(("vh", pp), *vread)
        extra = ([rk] if rk else []) + ([rv] if rv else [])
        for o in range(KT):
            setsf(("qk", pp, 0, o), *(kread + extra))
            setsf(("qk", pp, 1, o), *(kread + extra))
        setsf(("vc", pp), *(vread + extra))
        otoks = []
        for o in range(KT):
            pb = psP[o % 2]
            wl = P.dma("sync",
                       wob[o % 2][0:64, 0:16 * 128].rearrange("s (a n) -> s a n", n=128),
                       w["Wo"][li].rearrange("(a s) n -> s a n", s=64)[:, :, o * 128:(o + 1) * 128],
                       waits=sf(("wob", o % 2)), sem=swo[o % 2])
            mm = None
            for k16 in range(16):
                mm = P.op("tensor", lambda t, o=o, k16=k16, pb=pb, tw=tw: t.matmul(
                    pb[:, 0:tw], wob[o % 2][0:64, k16 * 128:(k16 + 1) * 128],
                    afh[0:64, k16 * TC:k16 * TC + tw],
                    start=(k16 == 0), stop=(k16 == 15)),
                    waits=(af_toks + [wl] + bf(pb)) if k16 == 0 else [],
                    inc=(k16 == 15))
            setsf(("wob", o % 2), mm)
            e1 = P.op("vector", lambda v, o=o, pp=pp, pb=pb, tw=tw: v.tensor_add(
                rb[pp][:, o * TC:o * TC + tw], rb[pp][:, o * TC:o * TC + tw],
                pb[:, 0:tw]), waits=[mm] + rl, inc=True)
            setbf(pb, e1)
            st = P.dma("scalar", xout_ap[o * 128:(o + 1) * 128, t0:t0 + tw],
                       rb[pp][:, o * TC:o * TC + tw], waits=[e1], sem=sst[pp])
            otoks.append(e1)
            last = st
        setsf("af", *otoks)
        free_r[0] = free_r[1] = tuple(otoks) + (last,)
    return last


def emit_l3f_ffn(P, ctx, xs_ap, x_ap, xout_ap, Ttot, w, li, name="l3ff"):
    lnf = P.dram(f"{name}_ln", [D, Ttot], BF16)
    hg = P.dram(f"{name}_hg", [2816, Ttot], BF16)
    with P.scope():
        g1, tg1 = load_vec_cols(P, w["ln_g"][li], D, f"{name}g")
        b1, tb1 = load_vec_cols(P, w["ln_b"][li], D, f"{name}b")
        emit_ln(P, ctx, xs_ap, lnf, Ttot, out_dtype=BF16, gamma=g1, beta=b1,
                eps=1e-5, wait_x=(tg1, tb1), name=f"{name}l", shift_half=True)
    P.barrier()
    with P.scope():
        w1, tw1 = load_weight_bf16(P, w["W1p"][li], D, 5632, f"{name}w1")
        b1p, tb1p = load_vec_cols(P, w["b1p"][li], 5632, f"{name}b1")
        lnb = [P.sbuf(f"{name}_x", [128, KT * TC], BF16) for _ in range(2)]
        asb = P.sbuf(f"{name}_a", [128, TC], F32)
        gsb = P.sbuf(f"{name}_gg", [128, TC], F32)
        hb = [P.sbuf(f"{name}_h", [128, 22 * TC], BF16) for _ in range(2)]
        ps = [P.psum(f"{name}_ps", [128, TC], F32) for _ in range(2)]
        sx = P.dsems(2)
        sh = P.dsems(2)
        free_x = [(), ()]
        free_h = [(), ()]
        ps_free = [(), ()]
        a_free = ()
        g_free = ()
        last1 = None
        for c in range(cdiv(Ttot, TC)):
            t0, tw = c * TC, min(TC, Ttot - c * TC)
            pp = c % 2
            lds = [P.dma("sync", lnb[pp][:, j * TC:j * TC + tw],
                         lnf[j * 128:(j + 1) * 128, t0:t0 + tw],
                         waits=free_x[pp], sem=sx[pp]) for j in range(KT)]
            htoks = []
            for o in range(22):
                pb = ps[0]
                mm = None
                for j in range(KT):
                    mm = P.op("tensor", lambda t, o=o, j=j, pb=pb, pp=pp, tw=tw: t.matmul(
                        pb[:, 0:tw], w1[:, j * 5632 + o * 128:j * 5632 + o * 128 + 128],
                        lnb[pp][:, j * TC:j * TC + tw],
                        start=(j == 0), stop=(j == KT - 1)),
                        waits=([tw1] + lds + list(ps_free[0])) if j == 0 else [],
                        inc=(j == KT - 1))
                ea = P.op("scalar", lambda s, o=o, pb=pb, tw=tw: s.activation(
                    asb[:, 0:tw], pb[:, 0:tw], AF.Identity,
                    bias=b1p[:, o:o + 1]),
                    waits=[mm, tb1p] + list(a_free), inc=True)
                ps_free[0] = (ea,)
                pb = ps[1]
                mm = None
                for j in range(KT):
                    mm = P.op("tensor", lambda t, o=o, j=j, pb=pb, pp=pp, tw=tw: t.matmul(
                        pb[:, 0:tw], w1[:, j * 5632 + (22 + o) * 128:j * 5632 + (22 + o) * 128 + 128],
                        lnb[pp][:, j * TC:j * TC + tw],
                        start=(j == 0), stop=(j == KT - 1)),
                        waits=list(ps_free[1]) if j == 0 else [],
                        inc=(j == KT - 1))
                eg = P.op("scalar", lambda s, o=o, pb=pb, tw=tw: s.activation(
                    gsb[:, 0:tw], pb[:, 0:tw], AF.Gelu,
                    bias=b1p[:, 22 + o:23 + o]),
                    waits=[mm] + list(g_free), inc=True)
                ps_free[1] = (eg,)
                hm = P.op("vector", lambda v, o=o, pp=pp, tw=tw: v.tensor_mul(
                    hb[pp][:, o * TC:o * TC + tw], asb[:, 0:tw], gsb[:, 0:tw]),
                    waits=[ea, eg] + list(free_h[pp]), inc=True)
                a_free = (hm,)
                g_free = (hm,)
                st = P.dma("scalar", hg[o * 128:(o + 1) * 128, t0:t0 + tw],
                           hb[pp][:, o * TC:o * TC + tw], waits=[hm], sem=sh[pp])
                htoks.append(hm)
                last1 = st
            free_x[pp] = tuple(htoks)
            free_h[pp] = (last1,)
    P.barrier()
    with P.scope():
        b2, tb2 = load_vec_cols(P, w["b2"][li], D, f"{name}b2")
        t = emit_linear_fm(P, ctx, hg, Ttot, 2816, D, w["W2p"][li], f"{name}w2",
                           bias=(b2, tb2), resid=(x_ap, xout_ap, 1.0))
    P.barrier()
    return t


def emit_l3_ffn_block(P, ctx, ln_ap, xold_ap, xnew_ap, Ttot, w, name="l3f"):
    xa = P.dram(f"{name}_xa", [D, Ttot], F32)
    xb_ = P.dram(f"{name}_xb", [D, Ttot], F32)
    xsh = P.dram(f"{name}_xs", [D, Ttot], F32)
    cur, nxt = ln_ap, xa
    t = None
    for li in range(6):
        with P.scope():
            t = emit_l3f_attn(P, ctx, cur, cur, nxt, Ttot, w, li, name=f"{name}a{li}")
        P.barrier()
        cur, nxt = nxt, (xb_ if nxt is xa else xa)
        with P.scope():
            t = emit_l3f_ffn(P, ctx, cur, cur, nxt, Ttot, w, li, name=f"{name}f{li}")
        P.barrier()
        cur, nxt = nxt, (xb_ if nxt is xa else xa)
    with P.scope():
        t = emit_add3(P, ctx, xold_ap, None, cur, xnew_ap, Ttot,
                      name=f"{name}fin", alpha_c=0.5) if False else None
        t = emit_axpy(P, ctx, xold_ap, cur, xnew_ap, Ttot, 0.5, name=f"{name}fin")
    P.barrier()
    return t


def emit_axpy(P, ctx, a_ap, c_ap, out_ap, Ttot, alpha, name="axpy"):
    ab = [P.sbuf(f"{name}_a", [128, KT * TC], F32) for _ in range(2)]
    cb = [P.sbuf(f"{name}_c", [128, KT * TC], F32) for _ in range(2)]
    sa = P.dsems(2)
    sc_ = P.dsems(2)
    so = P.dsems(1)[0]
    free = [(), ()]
    last = None
    for c in range(cdiv(Ttot, TC)):
        t0, tw = c * TC, min(TC, Ttot - c * TC)
        pp = c % 2
        la = [P.dma("sync", ab[pp][:, j * TC:j * TC + tw],
                    a_ap[j * 128:(j + 1) * 128, t0:t0 + tw], waits=free[pp], sem=sa[pp])
              for j in range(KT)]
        lc = [P.dma("gpsimd", cb[pp][:, j * TC:j * TC + tw],
                    c_ap[j * 128:(j + 1) * 128, t0:t0 + tw], waits=free[pp], sem=sc_[pp])
              for j in range(KT)]
        toks = []
        for j in range(KT):
            t1 = P.op("vector", lambda v, j=j, pp=pp, tw=tw: v.scalar_tensor_tensor(
                cb[pp][:, j * TC:j * TC + tw], cb[pp][:, j * TC:j * TC + tw], alpha,
                ab[pp][:, j * TC:j * TC + tw], ALU.mult, ALU.add),
                waits=la + lc, inc=True)
            st = P.dma("scalar", out_ap[j * 128:(j + 1) * 128, t0:t0 + tw],
                       cb[pp][:, j * TC:j * TC + tw], waits=[t1], sem=so)
            toks.append(t1)
            last = st
        free[pp] = tuple(toks) + (last,)
    return last


def emit_cast_w(P, ctx, pairs, name="cw"):
    stg = [P.sbuf(f"{name}_s", [128, 4096], F32) for _ in range(2)]
    ob = [P.sbuf(f"{name}_o", [128, 4096], BF16) for _ in range(2)]
    sl = P.dsems(2)
    so = P.dsems(2)
    free = [(), ()]
    ofree = [(), ()]
    last = None
    i = 0
    for (srcap, dstap, R, C) in pairs:
        assert R % 128 == 0
        for r in range(R // 128):
            for c0 in range(0, C, 4096):
                cw_ = min(4096, C - c0)
                pi = i % 2
                ld = P.dma("sync", stg[pi][:, 0:cw_],
                           srcap[r * 128:(r + 1) * 128, c0:c0 + cw_],
                           waits=free[pi], sem=sl[pi])
                ct = P.op("vector", lambda v, pi=pi, cw_=cw_: v.tensor_copy(
                    ob[pi][:, 0:cw_], stg[pi][:, 0:cw_]),
                    waits=[ld] + list(ofree[pi]), inc=True)
                free[pi] = (ct,)
                st = P.dma("scalar", dstap[r * 128:(r + 1) * 128, c0:c0 + cw_],
                           ob[pi][:, 0:cw_], waits=[ct], sem=so[pi])
                ofree[pi] = (st,)
                last = st
                i += 1
    return last


def emit_axpy_slab(P, ctx, a_ap, c3d_ap, out_ap, Ttot, alpha, name="axps"):
    ab = [P.sbuf(f"{name}_a", [128, KT * TC], F32) for _ in range(2)]
    cb = [P.sbuf(f"{name}_c", [128, KT * TC], F32) for _ in range(2)]
    sa = P.dsems(2)
    sc_ = P.dsems(2)
    so = P.dsems(1)[0]
    free = [(), ()]
    last = None
    for c in range(cdiv(Ttot, TC)):
        t0, tw = c * TC, min(TC, Ttot - c * TC)
        pp = c % 2
        la = [P.dma("sync", ab[pp][:, j * TC:j * TC + tw],
                    a_ap[j * 128:(j + 1) * 128, t0:t0 + tw], waits=free[pp], sem=sa[pp])
              for j in range(KT)]
        lc = [P.dma("gpsimd", cb[pp][:, j * TC:j * TC + tw],
                    c3d_ap[c, j * 128:(j + 1) * 128, 0:tw], waits=free[pp], sem=sc_[pp])
              for j in range(KT)]
        toks = []
        for j in range(KT):
            t1 = P.op("vector", lambda v, j=j, pp=pp, tw=tw: v.scalar_tensor_tensor(
                cb[pp][:, j * TC:j * TC + tw], cb[pp][:, j * TC:j * TC + tw], alpha,
                ab[pp][:, j * TC:j * TC + tw], ALU.mult, ALU.add),
                waits=la + lc, inc=True)
            st = P.dma("scalar", out_ap[j * 128:(j + 1) * 128, t0:t0 + tw],
                       cb[pp][:, j * TC:j * TC + tw], waits=[t1], sem=so)
            toks.append(t1)
            last = st
        free[pp] = tuple(toks) + (last,)
    return last



def host_prep(I):
    import numpy as _np
    H = {}
    ct = lambda a: _np.ascontiguousarray(a, dtype=_np.float32)
    T_ = lambda a: _np.ascontiguousarray(_np.asarray(a).T, dtype=_np.float32)
    t = _np.arange(1024)
    ang = -2 * _np.pi * _np.outer(t, t) / 1024.0
    F = _np.exp(1j * ang)
    G = _np.exp(-1j * ang) / 1024.0
    H["F_r"] = _np.real(F).astype(ml_dtypes.bfloat16)
    H["F_i"] = _np.imag(F).astype(ml_dtypes.bfloat16)
    H["F_i_neg"] = (-_np.imag(F)).astype(ml_dtypes.bfloat16)
    H["G_r"] = _np.real(G).astype(ml_dtypes.bfloat16)
    H["G_i_neg"] = (-_np.imag(G)).astype(ml_dtypes.bfloat16)
    H["l0f_wr"] = ct(I["L0_ffn_w_freq_real"])
    H["l0f_wi"] = ct(I["L0_ffn_w_freq_imag"])
    H["l0f_bias"] = ct(I["L0_ffn_bias"])
    H["rms1_w"] = ct(I["L0_attn_rms1_w"])
    H["rms2_w"] = ct(I["L0_attn_rms2_w"])
    H["inprojT"] = T_(I["L0_attn_in_proj_w"])
    cw = _np.asarray(I["L0_attn_conv_w"])
    cw4 = _np.zeros(4 * DI, _np.float32)
    for k in range(4):
        cw4[k * DI:(k + 1) * DI] = cw[:, k]
    H["convw4"] = cw4
    H["conv_b"] = ct(I["L0_attn_conv_b"])
    xp = _np.asarray(I["L0_attn_x_proj_w"])
    xpT = _np.zeros((DI, 128), _np.float32)
    xpT[:, 0:16] = xp[64:80].T
    xpT[:, 32:48] = xp[80:96].T
    xpT[:, 64:128] = xp[0:64].T
    H["xpT"] = xpT
    H["dtwT"] = T_(I["L0_attn_dt_proj_w"])
    H["dtb"] = ct(I["L0_attn_dt_proj_b"])
    al = _np.asarray(I["L0_attn_A_log"])
    p = _np.arange(128)
    jj = _np.arange(256)
    H["A_scan"] = al[8 * jj[None, :] + (p[:, None] // 16), p[:, None] % 16].astype(_np.float32)
    Rm = _np.zeros((128, 2048), ml_dtypes.bfloat16)
    Ym = _np.zeros((128, 2048), ml_dtypes.bfloat16)
    for m in range(16):
        for pp_ in range(128):
            k = 8 * m + pp_ // 16
            Rm[k, m * 128 + pp_] = 1
            Ym[pp_, m * 128 + k] = 1
    H["Rm"] = Rm
    H["Ym"] = Ym
    EB = _np.zeros((16, 128), ml_dtypes.bfloat16)
    for pp_ in range(128):
        EB[pp_ % 16, pp_] = 1
    H["EB"] = EB
    H["Dskip"] = ct(I["L0_attn_D_skip"])
    H["outprojT"] = T_(I["L0_attn_out_proj_w"])
    H["fc1T"] = T_(I["L0_attn_fc1_w"])
    H["fc1_b"] = ct(I["L0_attn_fc1_b"])
    H["fc2T"] = T_(I["L0_attn_fc2_w"])
    H["fc2_b"] = ct(I["L0_attn_fc2_b"])
    H["fc3T"] = T_(I["L0_attn_fc3_w"])
    H["fc3_b"] = ct(I["L0_attn_fc3_b"])
    H["w_gate"] = ct(I["L1_ffn_w_gate"])
    H["w_up"] = ct(I["L1_ffn_w_up"])
    H["w_down"] = ct(I["L1_ffn_w_down"])
    H["ident_bf"] = _np.eye(128, dtype=ml_dtypes.bfloat16)
    for nm in ("ln_conv_g", "ln_conv_b", "ln_attn_g", "ln_attn_b",
               "ln_boom_g", "ln_boom_b"):
        H[nm] = ct(I["L1_attn_" + nm])
    cw1 = _np.asarray(I["L1_attn_conv_w"])
    wc4 = _np.zeros((4, 256, 1024), _np.float32)
    for k in range(4):
        wc4[k] = cw1[:, :, k].T
    H["wconv4"] = wc4
    H["l1_conv_b"] = ct(I["L1_attn_conv_b"])
    H["Wqkv3T"] = _np.ascontiguousarray(_np.concatenate(
        [I["L1_attn_Wq"], I["L1_attn_Wk"], I["L1_attn_Wv"]], 0).T.astype(_np.float32))
    H["bqkv3"] = _np.concatenate(
        [I["L1_attn_bq"], I["L1_attn_bk"], I["L1_attn_bv"]]).astype(_np.float32)
    inv_scale = (1.0 / _np.arange(1, T + 1)).astype(_np.float32)
    H["inv_scale_b"] = _np.broadcast_to(inv_scale, (128, T)).copy()
    H["WbT"] = T_(I["L1_attn_Wb"])
    H["bb"] = ct(I["L1_attn_bb"])
    H["WuT"] = T_(I["L1_attn_Wu"])
    H["bu"] = ct(I["L1_attn_bu"])
    H["l2a_wqT"] = T_(I["L2_attn_Wq"])
    H["l2a_wkT"] = T_(I["L2_attn_Wk"])
    H["l2a_wvT"] = T_(I["L2_attn_Wv"])
    H["l2a_woT"] = T_(I["L2_attn_Wo"])
    H["l2a_bq"] = ct(I["L2_attn_bq"])
    H["l2a_bv"] = ct(I["L2_attn_bv"])
    H["l2a_bo"] = ct(I["L2_attn_bo"])
    H["WxT"] = _np.ascontiguousarray(_np.concatenate(
        [I["L2_ffn_wf"], I["L2_ffn_wi"], I["L2_ffn_wc"]], 0).T.astype(_np.float32))
    H["UT"] = _np.ascontiguousarray(_np.concatenate(
        [I["L2_ffn_uf"], I["L2_ffn_ui"], I["L2_ffn_uc"]], 0).T.astype(_np.float32))
    H["bfic"] = _np.concatenate(
        [I["L2_ffn_bf"], I["L2_ffn_bi"], I["L2_ffn_bc"]]).astype(_np.float32)
    H["l2f_woT"] = T_(I["L2_ffn_wo"])
    H["l2f_uoT"] = T_(I["L2_ffn_uo"])
    H["l2f_bo"] = ct(I["L2_ffn_bo"])
    H["in_c"] = ct(I["L2_ffn_in_c"])
    H["l3a_wq"] = ct(I["L3_attn_wq"])
    H["l3a_wk"] = ct(I["L3_attn_wk"])
    H["l3a_wv"] = ct(I["L3_attn_wv"])
    H["l3a_wo"] = ct(I["L3_attn_wo"])
    H["l3a_bq"] = ct(I["L3_attn_bq"])
    H["l3a_bk"] = ct(I["L3_attn_bk"])
    H["l3a_bv"] = ct(I["L3_attn_bv"])
    H["l3a_bo"] = ct(I["L3_attn_bo"])
    H["Wqkv6"] = ct(I["L3_ffn_Wqkv"])
    H["Wo6"] = ct(I["L3_ffn_Wo"])
    H["q_scale_x"] = _np.ascontiguousarray(_np.tile(I["L3_ffn_q_scale"], (1, 16)).astype(_np.float32))
    H["k_scale_x"] = _np.ascontiguousarray(_np.tile(I["L3_ffn_k_scale"], (1, 16)).astype(_np.float32))
    H["ln_g6"] = ct(I["L3_ffn_ln_g"])
    H["ln_b6"] = ct(I["L3_ffn_ln_b"])
    W1 = _np.asarray(I["L3_ffn_W1"])
    DIN = 2730
    W1p = _np.zeros((6, 1024, 5632), _np.float32)
    W1p[:, :, 0:DIN] = W1[:, :, 0:DIN]
    W1p[:, :, 2816:2816 + DIN] = W1[:, :, DIN:]
    H["W1p6"] = W1p
    b1 = _np.asarray(I["L3_ffn_b1"])
    b1p = _np.zeros((6, 5632), _np.float32)
    b1p[:, 0:DIN] = b1[:, 0:DIN]
    b1p[:, 2816:2816 + DIN] = b1[:, DIN:]
    H["b1p6"] = b1p
    W2p = _np.zeros((6, 2816, 1024), _np.float32)
    W2p[:, 0:DIN] = I["L3_ffn_W2"]
    H["W2p6"] = W2p
    H["b26"] = ct(I["L3_ffn_b2"])
    sel_o = _np.zeros((128, 8 * 16), ml_dtypes.bfloat16)
    sel_t = _np.zeros((16, 8 * 128), ml_dtypes.bfloat16)
    for o in range(8):
        for pp_ in range(128):
            c = pp_ // 64 + 2 * o
            sel_o[pp_, o * 16 + c] = 1
            sel_t[c, o * 128 + pp_] = 1
    H["sel_o"] = sel_o
    H["sel_t"] = sel_t
    rotm = _np.zeros((128, 128), ml_dtypes.bfloat16)
    for pp_ in range(128):
        if pp_ % 64 < 32:
            rotm[pp_ + 32, pp_] = -1
        else:
            rotm[pp_ - 32, pp_] = 1
    H["rot_m"] = rotm
    msk = _np.zeros((128, 3 * 384), _np.float32)
    NEG = -1e30
    pr = _np.arange(128)[:, None]
    fr = _np.arange(128)[None, :]
    tri_lo = _np.where(fr >= pr, 0.0, NEG)
    causal = _np.where(fr <= pr, 0.0, NEG)
    for g, m0 in enumerate([(NEG, NEG), (NEG, 0.0), (None, 0.0)]):
        base = g * 384
        msk[:, base:base + 128] = tri_lo if m0[0] is None else m0[0]
        msk[:, base + 128:base + 256] = m0[1]
        msk[:, base + 256:base + 384] = causal
    H["band_masks"] = msk
    DH, WIN = 64, 256
    half = _np.arange(0, DH, 2, dtype=_np.float64)
    inv_freq = 1.0 / 10000.0 ** (half / DH)
    tt = _np.arange(T, dtype=_np.float64)
    freqs = _np.concatenate([tt[:, None] * inv_freq[None, :]] * 2, -1)
    base_scale = (half + 0.4 * DH) / (1.4 * DH)
    power = (tt - T // 2) / (WIN // 2)
    scale = _np.concatenate([base_scale[None, :] ** power[:, None]] * 2, -1)
    cos, sin = _np.cos(freqs), _np.sin(freqs)
    tabs = [cos * scale, sin * scale, cos / scale, sin / scale]
    rope4 = _np.zeros((4, 128, T), ml_dtypes.bfloat16)
    for i, tb in enumerate(tabs):
        r = tb.T.astype(ml_dtypes.bfloat16)
        rope4[i, 0:64] = r
        rope4[i, 64:128] = r
    H["rope4"] = rope4
    return H


def _declare_inputs(P, H, names):
    out = {}
    for nm in names:
        arr = H[nm]
        dt = BF16 if arr.dtype == ml_dtypes.bfloat16 else F32
        out[nm] = P.dram(nm, list(arr.shape), dt, kind="ExternalInput")
    return out


A_NAMES = ["F_r", "F_i", "F_i_neg", "G_r", "G_i_neg", "l0f_wr", "l0f_wi",
           "l0f_bias", "rms1_w", "rms2_w", "inprojT", "convw4", "conv_b",
           "xpT", "dtwT", "dtb", "A_scan", "Rm", "Ym", "EB", "Dskip",
           "outprojT", "fc1T", "fc1_b", "fc2T", "fc2_b", "fc3T", "fc3_b",
           "w_gate", "w_up", "w_down", "ident_bf",
           "ln_conv_g", "ln_conv_b", "ln_attn_g", "ln_attn_b", "ln_boom_g",
           "ln_boom_b", "wconv4", "l1_conv_b", "Wqkv3T", "bqkv3",
           "inv_scale_b", "WbT", "bb", "WuT", "bu"]
B_NAMES = ["WxT", "UT", "bfic", "l2f_woT", "l2f_uoT", "l2f_bo", "in_c"]
C_NAMES = ["l2a_wqT", "l2a_wkT", "l2a_wvT", "l2a_woT", "l2a_bq", "l2a_bv", "l2a_bo"]
E_NAMES = ["l3a_wq", "l3a_wk", "l3a_wv", "l3a_wo", "l3a_bq", "l3a_bk", "l3a_bv",
           "l3a_bo", "Wqkv6", "Wo6", "q_scale_x", "k_scale_x", "ln_g6", "ln_b6",
           "W1p6", "b1p6", "W2p6", "b26", "ident_bf", "sel_o", "sel_t", "rot_m",
           "band_masks", "rope4"]


def build_launch_A():
    P = Prog("A")
    H = None
    w = _declare_inputs.__wrapped__ if False else None
    aps = {}
    x_in = P.dram("x", [D, T], F32, kind="ExternalInput")
    x2_out = P.dram("x2_out", [D, T], F32, kind="ExternalOutput")
    ln1_out = P.dram("ln1_out", [D, T], BF16, kind="ExternalOutput")
    import types
    Hshapes = globals().get("_HSHAPES")
    aps = {}
    for nm in A_NAMES:
        sh, isbf = Hshapes[nm]
        aps[nm] = P.dram(nm, list(sh), BF16 if isbf else F32, kind="ExternalInput")
    ctx = make_ctx(P)
    mmat = P.dram("Mmat", [D, D], BF16)
    xa = P.dram("xa", [D, T], F32)
    xb = P.dram("xb", [D, T], F32)
    xc = P.dram("xc", [D, T], F32)
    lnb = P.dram("lnb", [D, T], BF16)
    lnf = P.dram("lnf", [D, T], F32)
    wq = {"wqT_gate": P.dram("wqTg", [D, 2816], BF16),
          "wqT_up": P.dram("wqTu", [D, 2816], BF16),
          "wqT_down": P.dram("wqTd", [2816, D], BF16)}
    with P.scope():
        emit_compose_M(P, ctx, {k: aps[k] for k in
                                ("F_r", "F_i", "F_i_neg", "G_r", "G_i_neg")},
                       aps["l0f_wr"], aps["l0f_wi"], mmat)
    P.barrier()
    emit_l1_wprep(P, ctx, {"w_gate": aps["w_gate"], "w_up": aps["w_up"],
                           "w_down": aps["w_down"]}, wq, aps["ident_bf"])

    def l0f(xin, xout):
        with P.scope():
            emit_ln(P, ctx, xin, lnb, T, out_dtype=BF16, eps=1e-5, name="gl")
        P.barrier()
        with P.scope():
            bs, btok = load_vec_cols(P, aps["l0f_bias"], D, "b0f")
            b05 = P.sbuf("b05", [128, 8], F32)
            bt2 = P.op("vector", lambda v: v.tensor_scalar_mul(
                b05[:, 0:8], bs[:, 0:8], 0.5), waits=[btok], inc=True)
            emit_linear_fm(P, ctx, lnb, T, D, D, mmat, "l0f", w_dtype=BF16,
                           bias=(b05, bt2), act=AF.Identity, act_scale=0.5,
                           resid=(xin, xout, 1.0))
        P.barrier()

    w0 = {k: aps[k] for k in ("rms1_w", "rms2_w", "inprojT", "convw4", "conv_b",
                              "xpT", "dtwT", "dtb", "A_scan", "Rm", "Ym", "EB",
                              "Dskip", "outprojT", "fc1T", "fc1_b", "fc2T",
                              "fc2_b", "fc3T", "fc3_b")}
    w1a = {"ln_conv_g": aps["ln_conv_g"], "ln_conv_b": aps["ln_conv_b"],
           "ln_attn_g": aps["ln_attn_g"], "ln_attn_b": aps["ln_attn_b"],
           "ln_boom_g": aps["ln_boom_g"], "ln_boom_b": aps["ln_boom_b"],
           "wconv4": aps["wconv4"], "conv_b": aps["l1_conv_b"],
           "Wqkv3T": aps["Wqkv3T"], "bqkv3": aps["bqkv3"],
           "inv_scale_b": aps["inv_scale_b"], "WbT": aps["WbT"], "bb": aps["bb"],
           "WuT": aps["WuT"], "bu": aps["bu"]}
    l0f(x_in, xa)
    with P.scope():
        emit_ln(P, ctx, xa, lnf, T, out_dtype=F32, eps=1e-5, name="gl0a")
    P.barrier()
    emit_l0_attn(P, ctx, lnf, xa, xb, T, w0)
    l0f(xb, xc)
    with P.scope():
        emit_ln(P, ctx, xc, lnb, T, out_dtype=BF16, eps=1e-5, name="gl1f")
    P.barrier()
    with P.scope():
        emit_l1_ffn(P, ctx, lnb, xc, xa, T, wq, aps["ident_bf"], name="l1fa")
    P.barrier()
    with P.scope():
        emit_ln(P, ctx, xa, lnf, T, out_dtype=F32, eps=1e-5, name="gl1a")
    P.barrier()
    emit_l1_attn(P, ctx, lnf, xa, xb, T, w1a)
    with P.scope():
        emit_ln(P, ctx, xb, lnb, T, out_dtype=BF16, eps=1e-5, name="gl1f2")
    P.barrier()
    with P.scope():
        emit_l1_ffn(P, ctx, lnb, xb, x2_out, T, wq, aps["ident_bf"], name="l1fb")
    P.barrier()
    with P.scope():
        t = emit_ln(P, ctx, x2_out, ln1_out, T, out_dtype=BF16, eps=1e-5, name="gl2f")
    P.barrier()
    P.op("sync", lambda e: e.nop(), waits=[t])
    return P


def build_launch_B():
    P = Prog("B")
    Hshapes = globals().get("_HSHAPES")
    lnss = P.dram("lnss", [8, D, 512], BF16, kind="ExternalInput")
    hss = P.dram("hss", [8, D, 512], F32, kind="ExternalOutput")
    aps = {}
    for nm in B_NAMES:
        sh, isbf = Hshapes[nm]
        aps[nm] = P.dram(nm, list(sh), BF16 if isbf else F32, kind="ExternalInput")
    ctx = make_ctx(P)
    w = {"WxT": aps["WxT"], "UT": aps["UT"], "bfic": aps["bfic"],
         "woT": aps["l2f_woT"], "uoT": aps["l2f_uoT"], "bo": aps["l2f_bo"],
         "in_c": aps["in_c"]}
    t = emit_l2_ffn(P, ctx, lnss, hss, w)
    P.op("sync", lambda e: e.nop(), waits=[t])
    return P


def build_launch_C():
    P = Prog("C")
    Hshapes = globals().get("_HSHAPES")
    x_in = P.dram("x", [D, T], F32, kind="ExternalInput")
    h1 = P.dram("h1", [8, D, 512], F32, kind="ExternalInput")
    x3_out = P.dram("x3_out", [D, T], F32, kind="ExternalOutput")
    ln2_out = P.dram("ln2_out", [D, T], BF16, kind="ExternalOutput")
    aps = {}
    for nm in C_NAMES:
        sh, isbf = Hshapes[nm]
        aps[nm] = P.dram(nm, list(sh), BF16 if isbf else F32, kind="ExternalInput")
    ctx = make_ctx(P)
    xa = P.dram("xa", [D, T], F32)
    lnb = P.dram("lnb", [D, T], BF16)
    with P.scope():
        emit_axpy_slab(P, ctx, x_in, h1, xa, T, 0.5)
    P.barrier()
    with P.scope():
        emit_ln(P, ctx, xa, lnb, T, out_dtype=BF16, eps=1e-5, name="gl2a")
    P.barrier()
    w = {"wqT": aps["l2a_wqT"], "wkT": aps["l2a_wkT"], "wvT": aps["l2a_wvT"],
         "woT": aps["l2a_woT"], "bq": aps["l2a_bq"], "bv": aps["l2a_bv"],
         "bo": aps["l2a_bo"]}
    emit_l2_attn(P, ctx, lnb, xa, x3_out, T, w)
    with P.scope():
        t = emit_ln(P, ctx, x3_out, ln2_out, T, out_dtype=BF16, eps=1e-5, name="gl2f2")
    P.barrier()
    P.op("sync", lambda e: e.nop(), waits=[t])
    return P


def build_launch_E():
    P = Prog("E")
    Hshapes = globals().get("_HSHAPES")
    x_in = P.dram("x", [D, T], F32, kind="ExternalInput")
    h2 = P.dram("h2", [8, D, 512], F32, kind="ExternalInput")
    x_out = P.dram("x_out", [D, T], F32, kind="ExternalOutput")
    aps = {}
    for nm in E_NAMES:
        sh, isbf = Hshapes[nm]
        aps[nm] = P.dram(nm, list(sh), BF16 if isbf else F32, kind="ExternalInput")
    ctx = make_ctx(P)
    xa = P.dram("xa", [D, T], F32)
    xb = P.dram("xb", [D, T], F32)
    xc = P.dram("xc", [D, T], F32)
    lnf = P.dram("lnf", [D, T], F32)
    lnb = P.dram("lnb", [D, T], BF16)
    pairs = []
    wb16 = {}
    for nm, key, R, C in [("Wqkv", "Wqkv6", D, 3 * D), ("Wo", "Wo6", D, D),
                          ("W1p", "W1p6", D, 5632), ("W2p", "W2p6", 2816, D)]:
        dst = P.dram(nm + "_b", [6, R, C], BF16)
        for i in range(6):
            pairs.append((aps[key][i], dst[i], R, C))
        wb16[nm] = [dst[i] for i in range(6)]
    with P.scope():
        emit_cast_w(P, ctx, pairs)
    P.barrier()
    w3f = {"Wqkv": wb16["Wqkv"], "Wo": wb16["Wo"],
           "W1p": wb16["W1p"], "W2p": wb16["W2p"],
           "q_scale_x": [aps["q_scale_x"][i] for i in range(6)],
           "k_scale_x": [aps["k_scale_x"][i] for i in range(6)],
           "ln_g": [aps["ln_g6"][i] for i in range(6)],
           "ln_b": [aps["ln_b6"][i] for i in range(6)],
           "b1p": [aps["b1p6"][i] for i in range(6)],
           "b2": [aps["b26"][i] for i in range(6)],
           "ident_bf": aps["ident_bf"], "sel_o": aps["sel_o"],
           "sel_t": aps["sel_t"], "rot_m": aps["rot_m"],
           "band_masks": aps["band_masks"], "rope4": aps["rope4"]}
    w3a = {"wq": aps["l3a_wq"], "wk": aps["l3a_wk"], "wv": aps["l3a_wv"],
           "wo": aps["l3a_wo"], "bq": aps["l3a_bq"], "bk": aps["l3a_bk"],
           "bv": aps["l3a_bv"], "bo": aps["l3a_bo"]}
    with P.scope():
        emit_axpy_slab(P, ctx, x_in, h2, xa, T, 0.5)
    P.barrier()
    with P.scope():
        emit_ln(P, ctx, xa, lnf, T, out_dtype=F32, eps=1e-5, name="gl3f")
    P.barrier()
    emit_l3_ffn_block(P, ctx, lnf, xa, xb, T, w3f, name="l3fA")
    with P.scope():
        emit_ln(P, ctx, xb, lnb, T, out_dtype=BF16, eps=1e-5, name="gl3a")
    P.barrier()
    emit_l3_attn(P, ctx, lnb, xb, xc, T, w3a)
    with P.scope():
        emit_ln(P, ctx, xc, lnf, T, out_dtype=F32, eps=1e-5, name="gl3f2")
    P.barrier()
    t = emit_l3_ffn_block(P, ctx, lnf, xc, x_out, T, w3f, name="l3fB")
    P.op("sync", lambda e: e.nop(), waits=[t])
    return P


_HSHAPES = None


def kernel(**inputs):
    global _HSHAPES
    H = host_prep(inputs)
    _HSHAPES = {nm: (H[nm].shape, H[nm].dtype == ml_dtypes.bfloat16) for nm in H}
    x = np.asarray(inputs["L0_attn_x"], dtype=np.float32)
    xcores = [np.ascontiguousarray(x[i].T) for i in range(8)]

    PA = build_launch_A()
    shared = {nm: H[nm] for nm in A_NAMES}
    resA = PA.run([{"x": xcores[i], **shared} for i in range(8)])
    x2 = [resA.results[i]["x2_out"] for i in range(8)]
    ln1 = [resA.results[i]["ln1_out"] for i in range(8)]

    PB = build_launch_B()
    shared = {nm: H[nm] for nm in B_NAMES}
    inB = []
    for j in range(8):
        lnss = np.stack([ln1[i][:, 512 * j:512 * (j + 1)] for i in range(8)])
        inB.append({"lnss": np.ascontiguousarray(lnss), **shared})
    resB = PB.run(inB)
    h1ss = [resB.results[j]["hss"] for j in range(8)]

    PC = build_launch_C()
    shared = {nm: H[nm] for nm in C_NAMES}
    inC = []
    for i in range(8):
        h1 = np.stack([h1ss[j][i] for j in range(8)])
        inC.append({"x": x2[i], "h1": np.ascontiguousarray(h1), **shared})
    resC = PC.run(inC)
    x3 = [resC.results[i]["x3_out"] for i in range(8)]
    ln2 = [resC.results[i]["ln2_out"] for i in range(8)]

    PD = build_launch_B()
    shared = {nm: H[nm] for nm in B_NAMES}
    inD = []
    for j in range(8):
        lnss = np.stack([ln2[i][:, 512 * j:512 * (j + 1)] for i in range(8)])
        inD.append({"lnss": np.ascontiguousarray(lnss), **shared})
    resD = PD.run(inD)
    h2ss = [resD.results[j]["hss"] for j in range(8)]

    PE = build_launch_E()
    shared = {nm: H[nm] for nm in E_NAMES}
    inE = []
    for i in range(8):
        h2 = np.stack([h2ss[j][i] for j in range(8)])
        inE.append({"x": x3[i], "h2": np.ascontiguousarray(h2), **shared})
    resE = PE.run(inE)
    out = np.stack([resE.results[i]["x_out"].T for i in range(8)])
    return np.ascontiguousarray(out, dtype=np.float32)
